# version 09
# baseline (speedup 1.0000x reference)
import numpy as np
import concourse.bass as bass
import concourse.mybir as mybir
from concourse.bass_utils import run_bass_kernel_spmd

F32 = mybir.dt.float32
BF16 = mybir.dt.bfloat16
AF = mybir.ActivationFunctionType
ALU = mybir.AluOpType
AX = mybir.AxisListType

T = 2048
NT = T // 128
NCH = T // 512
H = 16
FD = 16
HP = 32
EPS = 1e-5

ENGS = ("sync", "tensor", "vector", "scalar", "gpsimd")


class Prog:

    def __init__(self):
        self.nc = bass.Bass("TRN2", target_bir_lowering=False)
        self.ops = {e: [] for e in ENGS}
        self.cnt = {}
        self._sem_names = []

    def sem_names(self, *names):
        self._sem_names = list(names)
        for n in names:
            self.cnt[n] = 0

    def inc(self, name, n=1):
        self.cnt[name] += n
        return self.cnt[name]

    def do(self, eng, fn, inc=None, n=1):
        if inc is None:
            self.ops[eng].append(lambda e, fn=fn: fn(e))
        else:
            self.inc(inc, n)
            self.ops[eng].append(
                lambda e, fn=fn, inc=inc, n=n: fn(e).then_inc(self.sems[inc], n))
        return self.cnt.get(inc)

    def w(self, eng, sem, val):
        self.ops[eng].append(lambda e, sem=sem, val=val: e.wait_ge(self.sems[sem], val))

    def wcur(self, eng, sem):
        self.w(eng, sem, self.cnt[sem])

    def barrier(self, engs=ENGS):
        base = self.cnt["bar"]
        k = len(engs)
        for e in engs:
            self.ops[e].append(lambda en: en.nop().then_inc(self.sems["bar"], 1))
        self.cnt["bar"] = base + k
        for e in engs:
            self.w(e, "bar", base + k)

    def build(self, ctxmgrs):
        import contextlib
        stack = contextlib.ExitStack()
        self.sems = {}
        for n in self._sem_names:
            self.sems[n] = stack.enter_context(self.nc.semaphore(n))
        block = stack.enter_context(self.nc.Block())
        for eng in ENGS:
            if not self.ops[eng]:
                continue
            def make(eng):
                def body(e):
                    for f in self.ops[eng]:
                        f(e)
                return body
            getattr(block, eng)(make(eng))
        stack.close()
        return self.nc


_CUR_STACK = None
_CUR_PREFIX = ""


def _sb(nc, name, shape, dtype):
    name = _CUR_PREFIX + name
    if _CUR_STACK is not None:
        return _CUR_STACK.enter_context(nc.sbuf_tensor(name, shape, dtype))
    return nc.alloc_sbuf_tensor(name, shape, dtype)


def _ps(nc, name, shape, dtype=F32):
    return nc.alloc_psum_tensor(name, shape, dtype)


def load_weight_cast(P, dst, src, rows, cols, inc="dg", dst_off=0, src_off=0,
                     dst_free=None):
    kk = rows // 128
    if dst_free is None:
        dst_free = dst.shape[1]
    src_ap = bass.AP(src.tensor if hasattr(src, "tensor") else src, src_off,
                     [[cols, 128], [128 * cols, kk], [1, cols]])
    dst_ap = bass.AP(dst, dst_off, [[dst_free, 128], [cols, kk], [1, cols]])
    return P.do("gpsimd", lambda e: e.dma_start(out=dst_ap, in_=src_ap), inc=inc, n=16)


def build_launch1():
    P = Prog()
    nc = P.nc
    P.sem_names("dl", "dg", "pe", "dv", "ac", "po", "st", "bar", "dx0", "dx1", "dc0", "dc1", "dw0", "dw1", "dw2", "dwd")

    xT = nc.dram_tensor("xT", [1024, T], F32, kind="ExternalInput")
    wq = nc.dram_tensor("wq", [1024, H * HP], F32, kind="ExternalInput")
    wk = nc.dram_tensor("wk", [1024, H * HP], F32, kind="ExternalInput")
    wv = nc.dram_tensor("wv", [1024, 1024], F32, kind="ExternalInput")
    gam = nc.dram_tensor("gam", [1, FD], F32, kind="ExternalInput")
    bet = nc.dram_tensor("bet", [1, FD], F32, kind="ExternalInput")

    qt_o = nc.dram_tensor("qt", [T, 512], BF16, kind="ExternalOutput")
    kt_o = nc.dram_tensor("kt", [T, 512], BF16, kind="ExternalOutput")
    va_o = nc.dram_tensor("va", [T, 1040], BF16, kind="ExternalOutput")
    ds0_o = nc.dram_tensor("ds0", [128, H * 2 * 65], F32, kind="ExternalOutput")

    xhat = _sb(nc, "xhat", [128, 8 * T], BF16)
    wq_s = _sb(nc, "wq_s", [128, 8 * 512], BF16)
    wk_s = _sb(nc, "wk_s", [128, 8 * 512], BF16)
    wv_s = _sb(nc, "wv_s", [128, 8 * 1024], BF16)
    qt_s = _sb(nc, "qt_s", [128, NT * 512], BF16)
    kt_s = _sb(nc, "kt_s", [128, NT * 512], BF16)
    va_s = _sb(nc, "va_s", [128, NT * 1040], BF16)
    ds0_s = _sb(nc, "ds0_s", [128, H * 2 * 65], F32)
    gb = {n: _sb(nc, "gb_" + n, [128, 512], F32) for n in ("qg", "qb", "kg", "kb")}
    grow = _sb(nc, "grow", [1, FD], F32)
    brow = _sb(nc, "brow", [1, FD], F32)
    ones_col = _sb(nc, "ones_col", [1, 128], F32)
    qraw = _sb(nc, "qraw", [128, 512], F32)
    kraw = _sb(nc, "kraw", [128, 512], F32)
    sq = _sb(nc, "sq", [128, 512], F32)
    stat = _sb(nc, "stat", [128, 16 * 10], F32)
    tmp = _sb(nc, "tmp", [128, 512], F32)
    tmpk = _sb(nc, "tmpk", [128, 512], F32)
    sqk = _sb(nc, "sqk", [128, 512], F32)
    phik = _sb(nc, "phik", [128, 2 * 256], BF16)

    pp = [_ps(nc, "pp%d" % i, [128, 512], F32) for i in range(8)]
    psS = [pp[0], pp[1], pp[2], pp[3]]

    load_weight_cast(P, xhat, xT.ap(), 1024, T)
    load_weight_cast(P, wq_s, wq.ap(), 1024, 512)
    load_weight_cast(P, wk_s, wk.ap(), 1024, 512)
    load_weight_cast(P, wv_s, wv.ap(), 1024, 1024)
    dg_all = P.cnt["dg"]
    P.do("sync", lambda e: e.dma_start(out=grow[:, :], in_=gam.ap()), inc="dl", n=16)
    P.do("sync", lambda e: e.dma_start(out=brow[:, :], in_=bet.ap()), inc="dl", n=16)
    dl_gb = P.cnt["dl"]

    for t in gb.values():
        P.do("gpsimd", lambda e, t=t: e.memset(t[:, :], 0.0))
    P.do("gpsimd", lambda e: e.memset(ones_col[:, :], 1.0))
    P.do("gpsimd", lambda e: e.memset(tmp[:, :], 0.0))
    P.do("gpsimd", lambda e: e.memset(tmpk[:, :], 0.0))
    P.do("gpsimd", lambda e: e.memset(va_s[:, :], 1.0), inc="po")
    po_init = P.cnt["po"]
    P.w("tensor", "dl", dl_gb)
    P.w("tensor", "po", po_init)
    P.do("tensor", lambda e: e.matmul(pp[0][:, 0:16], ones_col[0:1, :],
                                      grow[0:1, :], start=True, stop=True))
    P.do("tensor", lambda e: e.matmul(pp[1][:, 0:16], ones_col[0:1, :],
                                      brow[0:1, :], start=True, stop=True), inc="pe")
    pe_gb = P.cnt["pe"]
    P.w("vector", "pe", pe_gb)
    for h in range(H):
        P.do("vector", lambda e, h=h: e.tensor_copy(gb["kg"][:, HP * h:HP * h + FD],
                                                    pp[0][:, 0:16]))
        P.do("vector", lambda e, h=h: e.tensor_copy(gb["kb"][:, HP * h:HP * h + FD],
                                                    pp[1][:, 0:16]))
    P.do("vector", lambda e: e.tensor_scalar(gb["qg"][:, :], gb["kg"][:, :], 0.25,
                                             None, ALU.mult))
    P.do("vector", lambda e: e.tensor_scalar(gb["qb"][:, :], gb["kb"][:, :], 0.25,
                                             None, ALU.mult), inc="dv")
    P.barrier()

    SC = 16 * 8
    ac_tile = {}
    dvv_tile = {}
    for i in range(NT):
        psQ = pp[0] if i % 2 == 0 else pp[4]
        psK = pp[1] if i % 2 == 0 else pp[5]
        psV0 = pp[2] if i % 2 == 0 else pp[6]
        psV1 = pp[3] if i % 2 == 0 else pp[7]
        P.w("tensor", "dg", dg_all)
        if i - 2 in ac_tile:
            P.w("tensor", "ac", ac_tile[i - 2])
        elif i == 1:
            P.w("tensor", "dv", P.cnt["dv"])
        for k in range(8):
            P.do("tensor", lambda e, k=k, i=i, psQ=psQ: e.matmul(
                psQ[:, :], xhat[:, 2048 * k + 128 * i: 2048 * k + 128 * i + 128],
                wq_s[:, 512 * k: 512 * k + 512], start=(k == 0), stop=(k == 7)))
        for k in range(8):
            P.do("tensor", lambda e, k=k, i=i, psK=psK: e.matmul(
                psK[:, :], xhat[:, 2048 * k + 128 * i: 2048 * k + 128 * i + 128],
                wk_s[:, 512 * k: 512 * k + 512], start=(k == 0), stop=(k == 7)))
        for k in range(8):
            for nh in range(2):
                last = (k == 7 and nh == 1)
                pv = psV0 if nh == 0 else psV1
                P.do("tensor", lambda e, k=k, i=i, nh=nh, pv=pv: e.matmul(
                    pv[:, :],
                    xhat[:, 2048 * k + 128 * i: 2048 * k + 128 * i + 128],
                    wv_s[:, 1024 * k + 512 * nh: 1024 * k + 512 * nh + 512],
                    start=(k == 0), stop=(k == 7)),
                    inc="pe" if last else None)
        pe_proj = P.cnt["pe"]

        P.w("scalar", "pe", pe_proj)
        if i - 1 in dvv_tile:
            P.w("scalar", "dv", dvv_tile[i - 1])
        P.do("scalar", lambda e, psQ=psQ: e.copy(qraw[:, :], psQ[:, :]))
        P.do("scalar", lambda e, psK=psK: e.copy(kraw[:, :], psK[:, :]))
        for h in range(H):
            last = (h == H - 1)
            pv = psV0 if h < 8 else psV1
            off = 64 * h if h < 8 else 64 * h - 512
            P.do("scalar", lambda e, h=h, i=i, pv=pv, off=off: e.copy(
                va_s[:, 1040 * i + 65 * h: 1040 * i + 65 * h + 64],
                pv[:, off: off + 64]), inc="ac" if last else None)
        ac_tile[i] = P.cnt["ac"]
        ac_evac = P.cnt["ac"]
        P.w("vector", "ac", ac_evac)
        QK = ((qraw, 0, tmp, sq, "qg", "qb", qt_s),
              (kraw, 4, tmpk, sqk, "kg", "kb", kt_s))
        SCR2 = {0: 16 * 8, 4: 16 * 9}
        for h in range(H):
            for (raw, base, cen, sqb, gn, bn, dst) in QK:
                P.do("vector", lambda e, raw=raw, h=h, base=base: e.reduce_sum(
                    stat[:, 16 * base + h: 16 * base + h + 1],
                    raw[:, HP * h: HP * h + FD], axis=AX.X))
        for (raw, base, cen, sqb, gn, bn, dst) in QK:
            P.do("vector", lambda e, base=base: e.tensor_scalar(
                stat[:, 16 * base: 16 * base + 16],
                stat[:, 16 * base: 16 * base + 16], 1.0 / FD, None, ALU.mult))
        for h in range(H):
            for (raw, base, cen, sqb, gn, bn, dst) in QK:
                P.do("vector", lambda e, raw=raw, cen=cen, h=h, base=base:
                     e.tensor_scalar(
                         cen[:, HP * h: HP * h + FD], raw[:, HP * h: HP * h + FD],
                         stat[:, 16 * base + h: 16 * base + h + 1], None,
                         ALU.subtract))
        for (raw, base, cen, sqb, gn, bn, dst) in QK:
            P.do("vector", lambda e, cen=cen, sqb=sqb: e.tensor_tensor(
                sqb[:, :], cen[:, :], cen[:, :], ALU.mult))
        for h in range(H):
            for (raw, base, cen, sqb, gn, bn, dst) in QK:
                P.do("vector", lambda e, sqb=sqb, h=h, base=base: e.reduce_sum(
                    stat[:, 16 * (base + 2) + h: 16 * (base + 2) + h + 1],
                    sqb[:, HP * h: HP * h + FD], axis=AX.X))
        for (raw, base, cen, sqb, gn, bn, dst) in QK:
            P.do("vector", lambda e, base=base: e.tensor_scalar(
                stat[:, 16 * (base + 2): 16 * (base + 2) + 16],
                stat[:, 16 * (base + 2): 16 * (base + 2) + 16], 1.0 / FD, EPS,
                ALU.mult, ALU.add), inc="dv")
        dv_var = P.cnt["dv"]
        dvv_tile[i] = dv_var
        P.w("scalar", "dv", dv_var)
        for (raw, base, cen, sqb, gn, bn, dst) in QK:
            P.do("scalar", lambda e, base=base: e.activation(
                stat[:, SCR2[base]: SCR2[base] + 16],
                stat[:, 16 * (base + 2): 16 * (base + 2) + 16], AF.Sqrt),
                inc="ac")
        ac_sqrt = P.cnt["ac"]
        P.w("vector", "ac", ac_sqrt)
        for (raw, base, cen, sqb, gn, bn, dst) in QK:
            P.do("vector", lambda e, base=base: e.reciprocal(
                stat[:, 16 * (base + 3): 16 * (base + 3) + 16],
                stat[:, SCR2[base]: SCR2[base] + 16]))
        for h in range(H):
            for (raw, base, cen, sqb, gn, bn, dst) in QK:
                P.do("vector", lambda e, cen=cen, h=h, base=base: e.tensor_scalar(
                    cen[:, HP * h: HP * h + FD], cen[:, HP * h: HP * h + FD],
                    stat[:, 16 * (base + 3) + h: 16 * (base + 3) + h + 1], None,
                    ALU.mult))
        for (raw, base, cen, sqb, gn, bn, dst) in QK:
            P.do("vector", lambda e, cen=cen, sqb=sqb, gn=gn: e.tensor_tensor(
                sqb[:, :], cen[:, :], gb[gn][:, :], ALU.mult))
        for (raw, base, cen, sqb, gn, bn, dst) in QK:
            P.do("vector", lambda e, sqb=sqb, bn=bn, dst=dst, i=i: e.tensor_tensor(
                dst[:, 512 * i: 512 * i + 512], sqb[:, :], gb[bn][:, :],
                ALU.add), inc="dv")

    P.barrier()

    pe_phi = {}
    for h in range(H):
        pb = (h % 2) * 2
        if h >= 2:
            P.w("tensor", "dv", P.cnt["dv"])
        for i in range(NT):
            g = 16 * h + i
            pho = 256 * (g % 2)
            if g - 2 in pe_phi:
                P.w("vector", "pe", pe_phi[g - 2])
            P.do("vector", lambda e, h=h, i=i, pho=pho: e.tensor_tensor(
                phik[:, pho: pho + 256].rearrange("p (x y) -> p x y", x=16),
                bass.AP(kt_s, 512 * i + HP * h, [[NT * 512, 128], [0, 16], [1, 16]]),
                bass.AP(kt_s, 512 * i + HP * h, [[NT * 512, 128], [1, 16], [0, 16]]),
                ALU.mult), inc="dv")
            dv_phi = P.cnt["dv"]
            P.w("tensor", "dv", dv_phi)
            for half in range(2):
                P.do("tensor", lambda e, h=h, i=i, half=half, pb=pb, pho=pho: e.matmul(
                    psS[pb + half][:, 0:65],
                    phik[:, pho + 128 * half: pho + 128 * half + 128],
                    va_s[:, 1040 * i + 65 * h: 1040 * i + 65 * h + 65],
                    start=(i == 0), stop=(i == NT - 1)),
                    inc="pe" if half == 1 else None)
            pe_phi[g] = P.cnt["pe"]
        P.w("vector", "pe", pe_phi[16 * h + NT - 1])
        for half in range(2):
            P.do("vector", lambda e, h=h, half=half, pb=pb: e.tensor_copy(
                ds0_s[:, (2 * h + half) * 65: (2 * h + half) * 65 + 65],
                psS[pb + half][:, 0:65]), inc="dv")

    P.barrier()

    for i in range(NT):
        P.do("sync", lambda e, i=i: e.dma_start(
            out=qt_o.ap()[128 * i: 128 * i + 128, :],
            in_=qt_s[:, 512 * i: 512 * i + 512]), inc="st", n=16)
        P.do("sync", lambda e, i=i: e.dma_start(
            out=kt_o.ap()[128 * i: 128 * i + 128, :],
            in_=kt_s[:, 512 * i: 512 * i + 512]), inc="st", n=16)
        P.do("sync", lambda e, i=i: e.dma_start(
            out=va_o.ap()[128 * i: 128 * i + 128, :],
            in_=va_s[:, 1040 * i: 1040 * i + 1040]), inc="st", n=16)
    P.do("sync", lambda e: e.dma_start(out=ds0_o.ap(), in_=ds0_s[:, :]), inc="st", n=16)
    P.wcur("sync", "st")

    return P.build(None)


def phase_l0attn(P, pf, pb, E):
    nc = P.nc
    qt_i, kt_i, va_i, ds0p_i, x_i, wo_i, x1_o = (
        E["qt"], E["kt"], E["va"], E["ds0p"], E["x"], E["wo"], E["x1"])

    ident = _sb(nc, "ident", [128, 128], BF16)
    bigm = _sb(nc, "bigm", [128, 1024], F32)
    wo_s = _sb(nc, "wo_s", [128, 8 * 1024], BF16)
    S0 = _sb(nc, "S0", [128, H * 2 * 65], F32)
    S0b = _sb(nc, "S0b", [128, H * 2 * 65], BF16)
    stg = [_sb(nc, "stg%d" % i, [128, H * 2 * 65], F32) for i in range(3)]
    qtc = [_sb(nc, "qtc%d" % i, [128, 2048], BF16) for i in range(2)]
    ktc = [_sb(nc, "ktc%d" % i, [128, 2048], BF16) for i in range(2)]
    vac = [_sb(nc, "vac%d" % i, [128, 4 * 1040], BF16) for i in range(2)]
    qf = _sb(nc, "qf", [128, 2048], BF16)
    kf = _sb(nc, "kf", [128, 2048], BF16)
    sbf = [_sb(nc, "sbf%d" % i, [128, 512], BF16) for i in range(4)]
    sqs = _sb(nc, "sqs", [128, 512], F32)
    sqs2 = _sb(nc, "sqs2", [128, 512], F32)
    phiq4 = _sb(nc, "phiq4", [128, 4 * 256], BF16)
    phT4 = _sb(nc, "phT4", [128, 4 * 256], BF16)
    phik4 = _sb(nc, "phik4", [128, 4 * 256], BF16)
    och = _sb(nc, "och", [128, 4 * 1024], BF16)
    zc = _sb(nc, "zc", [128, 16], F32)
    zr = _sb(nc, "zr", [128, 16], F32)
    oT = _sb(nc, "oT", [128, 1024], BF16)
    xin = [_sb(nc, "xin%d" % i, [128, 1024], F32) for i in range(2)]
    x1_s = _sb(nc, "x1_s", [128, NT * 1024], F32)
    lst = _sb(nc, "lst", [128, 64], F32)
    junk = _sb(nc, "junk", [128, 1024], F32)

    load_weight_cast(P, wo_s, wo_i.ap(), 1024, 1024)
    dg_w = P.cnt["dg"]
    for r in range(3):
        P.do("sync", lambda e, r=r: e.dma_start(out=stg[r][:, :],
                                                in_=ds0p_i.ap()[r]), inc="dl", n=16)
    dl_st = P.cnt["dl"]
    P.do("gpsimd", lambda e: e.memset(ident[:], 0.0))
    P.do("gpsimd", lambda e: e.affine_select(
        out=ident[:], in_=ident[:], compare_op=ALU.not_equal, fill=1.0,
        base=0, pattern=[[-1, 128]], channel_multiplier=1))
    P.do("gpsimd", lambda e: e.memset(bigm[:, :], 1.0))
    P.do("gpsimd", lambda e: e.affine_select(
        out=bigm[:, :], in_=bigm[:, :], compare_op=ALU.is_ge, fill=0.0,
        base=-512, pattern=[[1, 1024]], channel_multiplier=-1), inc="po")
    P.w("vector", "dl", dl_st)
    P.do("vector", lambda e: e.tensor_add(S0[:, :], stg[0][:, :], stg[1][:, :]))
    P.do("vector", lambda e: e.tensor_copy(junk[:, 0:64], stg[0][:, 0:64]))
    P.do("vector", lambda e: e.tensor_add(S0[:, :], S0[:, :], stg[2][:, :]))
    P.do("vector", lambda e: e.tensor_copy(junk[:, 0:64], stg[0][:, 0:64]))
    P.do("vector", lambda e: e.tensor_copy(S0b[:, :], S0[:, :]), inc="dv")
    P.barrier()

    chunk_cnt = {}
    sq_ac_hist = []
    msk_dv_hist = []

    def load_chunk(c):
        buf = c % 2
        sem = "dc%d" % buf
        for jt in range(4):
            i = 4 * c + jt
            P.do("sync", lambda e, i=i, jt=jt, buf=buf: e.dma_start(
                out=qtc[buf][:, 512 * jt: 512 * jt + 512],
                in_=qt_i.ap()[128 * i: 128 * i + 128, :]), inc=sem, n=16)
            P.do("sync", lambda e, i=i, jt=jt, buf=buf: e.dma_start(
                out=ktc[buf][:, 512 * jt: 512 * jt + 512],
                in_=kt_i.ap()[128 * i: 128 * i + 128, :]), inc=sem, n=16)
            P.do("sync", lambda e, i=i, jt=jt, buf=buf: e.dma_start(
                out=vac[buf][:, 1040 * jt: 1040 * jt + 1040],
                in_=va_i.ap()[128 * i: 128 * i + 128, :]), inc=sem, n=16)
        chunk_cnt[c] = (sem, P.cnt[sem])

    xin_cnt = {}
    load_chunk(0)
    P.do("sync", lambda e: e.dma_start(out=xin[0][:, :],
                                       in_=x_i.ap()[0:128, :]), inc="dx0", n=16)
    xin_cnt[0] = ("dx0", P.cnt["dx0"])

    for c in range(NCH):
        buf = c % 2
        if c + 1 < NCH:
            if c >= 1:
                P.w("sync", "pe", P.cnt["pe"])
                P.w("sync", "dv", P.cnt["dv"])
            load_chunk(c + 1)

        P.w("tensor", *chunk_cnt[c])
        P.w("tensor", "dg", dg_w)
        if c > 0:
            P.w("tensor", "dv", P.cnt["dv"])
        gidx = 0
        for (src, dstf) in ((qtc[buf], qf), (ktc[buf], kf)):
            for grp in range(4):
                pbx = pb[gidx % 2]
                if gidx >= 2:
                    P.w("tensor", "dv", P.cnt["dv"])
                gidx += 1
                for u in range(4):
                    tr = 4 * grp + u
                    jt, ft = tr // 4, tr % 4
                    P.do("tensor", lambda e, src=src, jt=jt, ft=ft, u=u, pbx=pbx:
                         e.transpose(
                             pbx[:, 128 * u: 128 * u + 128],
                             src[:, 512 * jt + 128 * ft: 512 * jt + 128 * ft + 128],
                             ident[:, :]), inc="pe" if u == 3 else None)
                pe_tr = P.cnt["pe"]
                P.w("vector", "pe", pe_tr)
                for u in range(4):
                    tr = 4 * grp + u
                    jt, ft = tr // 4, tr % 4
                    P.do("vector", lambda e, dstf=dstf, jt=jt, ft=ft, u=u, pbx=pbx:
                         e.tensor_copy(
                             dstf[:, 512 * ft + 128 * jt: 512 * ft + 128 * jt + 128],
                             pbx[:, 128 * u: 128 * u + 128]),
                         inc="dv" if u == 3 else None)

        P.w("tensor", "dv", P.cnt["dv"])
        for h in range(H):
            ft, rb = h // 4, 32 * (h % 4)
            tp = (rb, 0) if rb == 96 else None
            for jt in range(4):
                psc = pf[jt % 2]
                if len(sq_ac_hist) >= 2:
                    P.w("tensor", "ac", sq_ac_hist[-2])
                def mm_sc(e, jt=jt, ft=ft, rb=rb, c=c, psc=psc, tp=tp):
                    lhsT = kf[rb: rb + 32, 512 * ft + 128 * jt: 512 * ft + 128 * jt + 128]
                    rhs = qf[rb: rb + 32, 512 * ft: 512 * ft + 512]
                    if tp is not None:
                        e.ldweights(lhsT, tile_position=tp)
                        return e.matmul(psc[:, :], lhsT, rhs, start=True, stop=True,
                                        tile_position=tp)
                    return e.matmul(psc[:, :], lhsT, rhs, start=True, stop=True)
                P.do("tensor", mm_sc, inc="pe")
                pe_sc = P.cnt["pe"]
                P.w("scalar", "pe", pe_sc)
                sq_buf = sqs if jt % 2 == 0 else sqs2
                if len(msk_dv_hist) >= 2:
                    P.w("scalar", "dv", msk_dv_hist[-2])
                P.do("scalar", lambda e, psc=psc, sq_buf=sq_buf: e.activation(
                    sq_buf[:, :], psc[:, :], AF.Square), inc="ac")
                sq_ac_hist.append(P.cnt["ac"])
                P.w("vector", "ac", P.cnt["ac"])
                P.do("vector", lambda e, jt=jt, sq_buf=sq_buf: e.tensor_tensor(
                    sbf[jt][:, :], sq_buf[:, :],
                    bigm[:, 512 - 128 * jt: 1024 - 128 * jt], ALU.mult),
                    inc="dv")
                msk_dv_hist.append(P.cnt["dv"])
            dv_sbf = P.cnt["dv"]
            P.w("tensor", "dv", dv_sbf)
            for it in range(4):
                P.do("vector", lambda e, it=it, h=h, buf=buf: e.tensor_tensor(
                    phiq4[:, 256 * it: 256 * it + 256]
                    .rearrange("p (x y) -> p x y", x=16),
                    bass.AP(qtc[buf], 512 * it + HP * h, [[2048, 128], [0, 16], [1, 16]]),
                    bass.AP(qtc[buf], 512 * it + HP * h, [[2048, 128], [1, 16], [0, 16]]),
                    ALU.mult), inc="dv" if it == 3 else None)
            P.w("tensor", "dv", P.cnt["dv"])
            for it in range(4):
                for half in range(2):
                    P.do("tensor", lambda e, it=it, half=half: e.transpose(
                        pb[half][:, 128 * it: 128 * it + 128],
                        phiq4[:, 256 * it + 128 * half: 256 * it + 128 * half + 128],
                        ident[:, :]), inc="pe" if (it, half) == (3, 1) else None)
            P.w("vector", "pe", P.cnt["pe"])
            for half in range(2):
                P.do("vector", lambda e, half=half: e.tensor_copy(
                    bass.AP(phT4, 128 * half, [[1024, 128], [256, 4], [1, 128]]),
                    pb[half][:, 0:512].rearrange("p (i x) -> p i x", i=4)),
                    inc="dv" if half == 1 else None)
            P.w("tensor", "dv", P.cnt["dv"])
            for it in range(4):
                pso = pf[2 + it]
                first = True
                for jt in range(it + 1):
                    P.do("tensor", lambda e, jt=jt, it=it, h=h, buf=buf, pso=pso,
                         first=first: e.matmul(
                             pso[:, 0:65], sbf[jt][:, 128 * it: 128 * it + 128],
                             vac[buf][:, 1040 * jt + 65 * h: 1040 * jt + 65 * h + 65],
                             start=first, stop=False))
                    first = False
                P.do("tensor", lambda e, h=h, it=it, pso=pso: e.matmul(
                    pso[:, 0:65], phT4[:, 256 * it: 256 * it + 128],
                    S0b[:, (2 * h) * 65: (2 * h) * 65 + 65], start=False, stop=False))
                P.do("tensor", lambda e, h=h, it=it, pso=pso: e.matmul(
                    pso[:, 0:65], phT4[:, 256 * it + 128: 256 * it + 256],
                    S0b[:, (2 * h + 1) * 65: (2 * h + 1) * 65 + 65],
                    start=False, stop=True), inc="pe")
            pe_o = P.cnt["pe"]
            P.w("vector", "pe", pe_o)
            for it in range(4):
                P.do("vector", lambda e, it=it: e.tensor_scalar(
                    zc[:, it: it + 1], pf[2 + it][:, 64:65], EPS, None, ALU.add))
            for it in range(4):
                P.do("vector", lambda e, it=it: e.reciprocal(
                    zr[:, it: it + 1], zc[:, it: it + 1]))
            for it in range(4):
                P.do("vector", lambda e, it=it, h=h: e.tensor_scalar(
                    och[:, 1024 * it + 64 * h: 1024 * it + 64 * h + 64],
                    pf[2 + it][:, 0:64], zr[:, it: it + 1], None, ALU.mult),
                    inc="dv" if it == 3 else None)
            P.w("tensor", "dv", P.cnt["dv"])
            for jt in range(4):
                P.do("vector", lambda e, jt=jt, h=h, buf=buf: e.tensor_tensor(
                    phik4[:, 256 * jt: 256 * jt + 256]
                    .rearrange("p (x y) -> p x y", x=16),
                    bass.AP(ktc[buf], 512 * jt + HP * h, [[2048, 128], [0, 16], [1, 16]]),
                    bass.AP(ktc[buf], 512 * jt + HP * h, [[2048, 128], [1, 16], [0, 16]]),
                    ALU.mult), inc="dv" if jt == 3 else None)
            P.w("tensor", "dv", P.cnt["dv"])
            for jt in range(4):
                for half in range(2):
                    P.do("tensor", lambda e, jt=jt, h=h, half=half, buf=buf: e.matmul(
                        pf[half][:, 0:65],
                        phik4[:, 256 * jt + 128 * half: 256 * jt + 128 * half + 128],
                        vac[buf][:, 1040 * jt + 65 * h: 1040 * jt + 65 * h + 65],
                        start=(jt == 0), stop=(jt == 3)),
                        inc="pe" if (jt, half) == (3, 1) else None)
            P.w("vector", "pe", P.cnt["pe"])
            for half in range(2):
                P.do("vector", lambda e, h=h, half=half: e.tensor_tensor(
                    S0[:, (2 * h + half) * 65: (2 * h + half) * 65 + 65],
                    S0[:, (2 * h + half) * 65: (2 * h + half) * 65 + 65],
                    pf[half][:, 0:65], ALU.add))
            for half in range(2):
                P.do("vector", lambda e, h=h, half=half: e.tensor_copy(
                    S0b[:, (2 * h + half) * 65: (2 * h + half) * 65 + 65],
                    S0[:, (2 * h + half) * 65: (2 * h + half) * 65 + 65]),
                    inc="dv" if half == 1 else None)
            P.w("tensor", "dv", P.cnt["dv"])

        for it in range(4):
            i = 4 * c + it
            if i + 1 < NT:
                P.w("sync", "dv", P.cnt["dv"])
                sem = "dx%d" % ((i + 1) % 2)
                P.do("sync", lambda e, i=i, sem=sem: e.dma_start(
                    out=xin[(i + 1) % 2][:, :],
                    in_=x_i.ap()[128 * (i + 1): 128 * (i + 1) + 128, :]),
                    inc=sem, n=16)
                xin_cnt[i + 1] = (sem, P.cnt[sem])
            P.w("tensor", "dv", P.cnt["dv"])
            for grp in range(2):
                pbx = pb[grp]
                for u in range(4):
                    kf8 = 4 * grp + u
                    P.do("tensor", lambda e, it=it, kf8=kf8, u=u, pbx=pbx: e.transpose(
                        pbx[:, 128 * u: 128 * u + 128],
                        och[:, 1024 * it + 128 * kf8: 1024 * it + 128 * kf8 + 128],
                        ident[:, :]), inc="pe" if u == 3 else None)
            pe_tr = P.cnt["pe"]
            P.w("vector", "pe", pe_tr)
            for grp in range(2):
                pbx = pb[grp]
                for u in range(4):
                    kf8 = 4 * grp + u
                    P.do("vector", lambda e, kf8=kf8, u=u, pbx=pbx: e.tensor_copy(
                        oT[:, 128 * kf8: 128 * kf8 + 128], pbx[:, 128 * u: 128 * u + 128]),
                        inc="dv" if (grp, u) == (1, 3) else None)
            P.w("tensor", "dv", P.cnt["dv"])
            for k8 in range(8):
                for nh in range(2):
                    P.do("tensor", lambda e, k8=k8, nh=nh: e.matmul(
                        pf[nh][:, :], oT[:, 128 * k8: 128 * k8 + 128],
                        wo_s[:, 1024 * k8 + 512 * nh: 1024 * k8 + 512 * nh + 512],
                        start=(k8 == 0), stop=(k8 == 7)),
                        inc="pe" if (k8, nh) == (7, 1) else None)
            pe_wo = P.cnt["pe"]
            P.w("vector", "pe", pe_wo)
            P.w("vector", *xin_cnt[i])
            for nh in range(2):
                P.do("vector", lambda e, i=i, nh=nh: e.tensor_tensor(
                    x1_s[:, 1024 * i + 512 * nh: 1024 * i + 512 * nh + 512],
                    xin[i % 2][:, 512 * nh: 512 * nh + 512],
                    pf[nh][:, :], ALU.add), inc="dv" if nh == 1 else None)
            P.w("tensor", "dv", P.cnt["dv"])

        lnbase = P.cnt["dv"]
        NLN = 4
        for it in range(4):
            i = 4 * c + it
            P.do("vector", lambda e, i=i, it=it: e.reduce_sum(
                lst[:, it: it + 1], x1_s[:, 1024 * i: 1024 * i + 1024], axis=AX.X))
        for it in range(4):
            P.do("vector", lambda e, it=it: e.tensor_scalar(
                lst[:, it: it + 1], lst[:, it: it + 1], 1.0 / 1024, None, ALU.mult),
                inc="dv" if it == 3 else None)
        P.w("scalar", "dv", P.cnt["dv"])
        for it in range(4):
            i = 4 * c + it
            P.do("scalar", lambda e, i=i, it=it: e.activation(
                junk[:, :], x1_s[:, 1024 * i: 1024 * i + 1024], AF.Square,
                accum_out=lst[:, 4 + it: 5 + it]), inc="ac" if it == 3 else None)
        P.w("vector", "ac", P.cnt["ac"])
        for it in range(4):
            P.do("vector", lambda e, it=it: e.tensor_tensor(
                lst[:, 8 + it: 9 + it], lst[:, it: it + 1], lst[:, it: it + 1],
                ALU.mult))
        for it in range(4):
            P.do("vector", lambda e, it=it: e.tensor_scalar(
                lst[:, 12 + it: 13 + it], lst[:, 4 + it: 5 + it], 1.0 / 1024, None,
                ALU.mult))
        for it in range(4):
            P.do("vector", lambda e, it=it: e.tensor_tensor(
                lst[:, 16 + it: 17 + it], lst[:, 12 + it: 13 + it],
                lst[:, 8 + it: 9 + it], ALU.subtract))
        for it in range(4):
            P.do("vector", lambda e, it=it: e.tensor_scalar(
                lst[:, 16 + it: 17 + it], lst[:, 16 + it: 17 + it], EPS, None,
                ALU.add), inc="dv" if it == 3 else None)
        P.w("scalar", "dv", P.cnt["dv"])
        for it in range(4):
            P.do("scalar", lambda e, it=it: e.activation(
                lst[:, 20 + it: 21 + it], lst[:, 16 + it: 17 + it], AF.Sqrt),
                inc="ac" if it == 3 else None)
        P.w("vector", "ac", P.cnt["ac"])
        for it in range(4):
            P.do("vector", lambda e, it=it: e.reciprocal(
                lst[:, 24 + it: 25 + it], lst[:, 20 + it: 21 + it]))
        for it in range(4):
            i = 4 * c + it
            P.do("vector", lambda e, i=i, it=it: e.tensor_scalar(
                x1_s[:, 1024 * i: 1024 * i + 1024],
                x1_s[:, 1024 * i: 1024 * i + 1024],
                lst[:, it: it + 1], None, ALU.subtract))
        for it in range(4):
            i = 4 * c + it
            P.do("vector", lambda e, i=i, it=it: e.tensor_scalar(
                x1_s[:, 1024 * i: 1024 * i + 1024],
                x1_s[:, 1024 * i: 1024 * i + 1024],
                lst[:, 24 + it: 25 + it], None, ALU.mult),
                inc="dv" if it == 3 else None)

    P.barrier()
    for i in range(NT):
        P.do("sync", lambda e, i=i: e.dma_start(
            out=x1_o.ap()[128 * i: 128 * i + 128, :],
            in_=x1_s[:, 1024 * i: 1024 * i + 1024]), inc="st", n=16)
    P.wcur("sync", "st")


def build_launch2a():
    P = Prog()
    nc = P.nc
    P.sem_names("dl", "dg", "pe", "dv", "ac", "po", "st", "bar",
                "dx0", "dx1", "dc0", "dc1", "dw0", "dw1", "dw2", "dwd")
    E = dict(
        qt=nc.dram_tensor("qt", [T, 512], BF16, kind="ExternalInput"),
        kt=nc.dram_tensor("kt", [T, 512], BF16, kind="ExternalInput"),
        va=nc.dram_tensor("va", [T, 1040], BF16, kind="ExternalInput"),
        ds0p=nc.dram_tensor("ds0p", [3, 128, H * 2 * 65], F32, kind="ExternalInput"),
        x=nc.dram_tensor("x", [T, 1024], F32, kind="ExternalInput"),
        wo=nc.dram_tensor("wo", [1024, 1024], F32, kind="ExternalInput"),
        x1=nc.dram_tensor("x1", [T, 1024], F32, kind="ExternalOutput"))
    pf = [_ps(nc, "pf%d" % i, [128, 512], F32) for i in range(6)]
    pb = [_ps(nc, "pb%d" % i, [128, 1024], BF16) for i in range(2)]
    phase_l0attn(P, pf, pb, E)
    return P.build(None)



def emit_top2(P, EW_, lgtA, rbb, lgt, cmb, tks, mskd):
    AFl, ALUl, AXl = AF, ALU, AX
    P.w("vector", "ac", P.cnt["ac"])
    for i in range(NT):
        P.do("vector", lambda e, i=i: e.tensor_tensor(
            lgt[:, EW_ * i: EW_ * i + EW_], lgtA[:, EW_ * i: EW_ * i + EW_],
            rbb[:, :], ALUl.add))
    for i in range(NT):
        P.do("vector", lambda e, i=i: e.reduce_max(
            tks[:, i: i + 1], lgt[:, EW_ * i: EW_ * i + EW_], axis=AXl.X))
    for i in range(NT):
        P.do("vector", lambda e, i=i: e.tensor_scalar(
            cmb[:, EW_ * i: EW_ * i + EW_], lgt[:, EW_ * i: EW_ * i + EW_],
            tks[:, i: i + 1], None, ALUl.is_equal))
    for i in range(NT):
        P.do("vector", lambda e, i=i: e.scalar_tensor_tensor(
            mskd[:, EW_ * i: EW_ * i + EW_], cmb[:, EW_ * i: EW_ * i + EW_],
            -1e30, lgt[:, EW_ * i: EW_ * i + EW_], ALUl.mult, ALUl.add))
    for i in range(NT):
        P.do("vector", lambda e, i=i: e.reduce_max(
            tks[:, NT + i: NT + i + 1], mskd[:, EW_ * i: EW_ * i + EW_],
            axis=AXl.X))
    for i in range(NT):
        P.do("vector", lambda e, i=i: e.tensor_tensor(
            tks[:, 2 * NT + i: 2 * NT + i + 1], tks[:, i: i + 1],
            tks[:, NT + i: NT + i + 1], ALUl.subtract),
            inc="dv" if i == NT - 1 else None)
    P.w("scalar", "dv", P.cnt["dv"])
    for i in range(NT):
        P.do("scalar", lambda e, i=i: e.activation(
            tks[:, 3 * NT + i: 3 * NT + i + 1],
            tks[:, 2 * NT + i: 2 * NT + i + 1], AFl.Sigmoid),
            inc="ac" if i == NT - 1 else None)
    P.w("vector", "ac", P.cnt["ac"])
    for i in range(NT):
        P.do("vector", lambda e, i=i: e.tensor_scalar(
            tks[:, 4 * NT + i: 4 * NT + i + 1],
            tks[:, 3 * NT + i: 3 * NT + i + 1], -1.0, 1.0, ALUl.mult, ALUl.add))
    for i in range(NT):
        P.do("vector", lambda e, i=i: e.tensor_scalar(
            mskd[:, EW_ * i: EW_ * i + EW_], mskd[:, EW_ * i: EW_ * i + EW_],
            tks[:, NT + i: NT + i + 1], None, ALUl.is_equal))
    for i in range(NT):
        P.do("vector", lambda e, i=i: e.tensor_scalar(
            cmb[:, EW_ * i: EW_ * i + EW_], cmb[:, EW_ * i: EW_ * i + EW_],
            tks[:, 3 * NT + i: 3 * NT + i + 1], None, ALUl.mult))
    for i in range(NT):
        P.do("vector", lambda e, i=i: e.tensor_scalar(
            mskd[:, EW_ * i: EW_ * i + EW_], mskd[:, EW_ * i: EW_ * i + EW_],
            tks[:, 4 * NT + i: 4 * NT + i + 1], None, ALUl.mult))
    for i in range(NT):
        P.do("vector", lambda e, i=i: e.tensor_tensor(
            cmb[:, EW_ * i: EW_ * i + EW_], cmb[:, EW_ * i: EW_ * i + EW_],
            mskd[:, EW_ * i: EW_ * i + EW_], ALUl.add),
            inc="dv" if i == NT - 1 else None)


def phase_l0ffn(P, pf, pb, E):
    nc = P.nc
    x1_i, rw_i, rb_i, up_i, ga_i, dw_i, shg_i, shu_i, shd_i, x2_o = (
        E["x1"], E["rw"], E["rb"], E["up"], E["ga"], E["dw"], E["shg"],
        E["shu"], E["shd"], E["x2"])

    ident = _sb(nc, "ident", [128, 128], BF16)
    identf = _sb(nc, "identf", [128, 128], F32)
    ones_col = _sb(nc, "ones_col", [1, 128], F32)
    rbrow = _sb(nc, "rbrow", [1, 16], F32)
    rbb = _sb(nc, "rbb", [128, 16], F32)
    rwf_s = _sb(nc, "rwf_s", [128, 8 * 16], F32)
    xtf = _sb(nc, "xtf", [128, 1024], F32)
    lgtA = _sb(nc, "lgtA", [128, NT * 16], F32)
    wbig = [_sb(nc, "wbig%d" % i, [128, 8192], BF16) for i in range(3)]
    xh = _sb(nc, "xh", [128, 4 * 4096], BF16)
    xin = [_sb(nc, "xin%d" % i, [128, 1024], F32) for i in range(2)]
    xb = _sb(nc, "xb", [128, 1024], BF16)
    acc = _sb(nc, "acc", [128, NT * 1024], F32)
    lgt = _sb(nc, "lgt", [128, NT * 16], F32)
    cmb = _sb(nc, "cmb", [128, NT * 16], F32)
    tks = _sb(nc, "tks", [128, 5 * NT], F32)
    mskd = _sb(nc, "mskd", [128, NT * 16], F32)
    sg = _sb(nc, "sg", [128, 512], F32)
    sg2 = _sb(nc, "sg2", [128, 512], F32)
    tmpe2 = _sb(nc, "tmpe2", [128, 1024], F32)
    shact = _sb(nc, "shact", [128, 8 * 512], BF16)
    acte = _sb(nc, "acte", [128, 4 * 512], BF16)
    tmpe = _sb(nc, "tmpe", [128, 1024], F32)
    lst = _sb(nc, "lst", [128, 64], F32)
    junk = _sb(nc, "junk", [128, 1024], F32)

    rw_src = bass.AP(rw_i, 0, [[16, 128], [128 * 16, 8], [1, 16]])
    rw_dst = bass.AP(rwf_s, 0, [[128, 128], [16, 8], [1, 16]])
    P.do("sync", lambda e: e.dma_start(out=rw_dst, in_=rw_src), inc="dl", n=16)
    load_weight_cast(P, wbig[0], shg_i.ap(), 1024, 1024)
    load_weight_cast(P, wbig[1], shu_i.ap(), 1024, 1024)
    load_weight_cast(P, wbig[2], shd_i.ap(), 1024, 1024)
    dg_sh = P.cnt["dg"]
    P.do("sync", lambda e: e.dma_start(out=rbrow[:, :], in_=rb_i.ap()), inc="dl", n=16)
    dl_rb = P.cnt["dl"]
    P.do("gpsimd", lambda e: e.memset(ident[:], 0.0))
    P.do("gpsimd", lambda e: e.affine_select(
        out=ident[:], in_=ident[:], compare_op=ALU.not_equal, fill=1.0,
        base=0, pattern=[[-1, 128]], channel_multiplier=1))
    P.do("gpsimd", lambda e: e.memset(identf[:], 0.0))
    P.do("gpsimd", lambda e: e.affine_select(
        out=identf[:], in_=identf[:], compare_op=ALU.not_equal, fill=1.0,
        base=0, pattern=[[-1, 128]], channel_multiplier=1))
    P.do("gpsimd", lambda e: e.memset(ones_col[:, :], 1.0), inc="po")
    P.w("tensor", "po", P.cnt["po"])
    P.w("tensor", "dl", dl_rb)
    P.do("tensor", lambda e: e.matmul(pf[5][:, 0:16], ones_col[0:1, :],
                                      rbrow[0:1, :], start=True, stop=True), inc="pe")
    P.w("vector", "pe", P.cnt["pe"])
    P.do("vector", lambda e: e.tensor_copy(rbb[:, :], pf[5][:, 0:16]), inc="dv")
    P.barrier()

    xin_cnt = {}
    P.do("sync", lambda e: e.dma_start(out=xin[0][:, :], in_=x1_i.ap()[0:128, :]),
         inc="dx0", n=16)
    xin_cnt[0] = ("dx0", P.cnt["dx0"])
    for i in range(NT):
        c, itl = i // 4, i % 4
        if i + 1 < NT:
            P.w("sync", "dv", P.cnt["dv"])
            sem = "dx%d" % ((i + 1) % 2)
            P.do("sync", lambda e, i=i: e.dma_start(
                out=xin[(i + 1) % 2][:, :],
                in_=x1_i.ap()[128 * (i + 1): 128 * (i + 1) + 128, :]), inc=sem, n=16)
            xin_cnt[i + 1] = (sem, P.cnt[sem])
        P.w("vector", *xin_cnt[i])
        P.do("vector", lambda e, i=i: e.tensor_copy(xb[:, :], xin[i % 2][:, :]),
             inc="dv")
        P.w("tensor", "dv", P.cnt["dv"])
        for grp in range(2):
            if i > 0 or grp > 0:
                P.w("tensor", "dv", P.cnt["dv"])
            for u in range(4):
                kd = 4 * grp + u
                P.do("tensor", lambda e, kd=kd, u=u, grp=grp: e.transpose(
                    pb[grp][:, 128 * u: 128 * u + 128],
                    xb[:, 128 * kd: 128 * kd + 128], ident[:, :]),
                    inc="pe" if u == 3 else None)
            P.w("vector", "pe", P.cnt["pe"])
            for u in range(4):
                kd = 4 * grp + u
                P.do("vector", lambda e, kd=kd, u=u, grp=grp, c=c, itl=itl:
                     e.tensor_copy(
                         xh[:, 4096 * c + 512 * kd + 128 * itl:
                            4096 * c + 512 * kd + 128 * itl + 128],
                         pb[grp][:, 128 * u: 128 * u + 128]),
                     inc="dv" if u == 3 else None)
        P.w("tensor", "dv", P.cnt["dv"])
        for grp in range(2):
            psx = pf[2 + grp]
            for u in range(4):
                kd = 4 * grp + u
                P.do("tensor", lambda e, kd=kd, u=u, psx=psx, i=i: e.transpose(
                    psx[:, 128 * u: 128 * u + 128],
                    xin[i % 2][:, 128 * kd: 128 * kd + 128], identf[:, :]),
                    inc="pe" if u == 3 else None)
        P.w("vector", "pe", P.cnt["pe"])
        for grp in range(2):
            psx = pf[2 + grp]
            for u in range(4):
                kd = 4 * grp + u
                P.do("vector", lambda e, kd=kd, u=u, psx=psx: e.tensor_copy(
                    xtf[:, 128 * kd: 128 * kd + 128], psx[:, 128 * u: 128 * u + 128]),
                    inc="dv" if (grp, u) == (1, 3) else None)
        P.w("tensor", "dv", P.cnt["dv"])
        if i > 0:
            P.w("tensor", "ac", P.cnt["ac"])
        for kd in range(8):
            P.do("tensor", lambda e, kd=kd: e.matmul(
                pf[4][:, 0:16], xtf[:, 128 * kd: 128 * kd + 128],
                rwf_s[:, 16 * kd: 16 * kd + 16],
                start=(kd == 0), stop=(kd == 7)), inc="pe" if kd == 7 else None)
        P.w("scalar", "pe", P.cnt["pe"])
        P.do("scalar", lambda e, i=i: e.copy(lgtA[:, 16 * i: 16 * i + 16],
                                             pf[4][:, 0:16]), inc="ac")

    prol_ac = P.cnt["ac"]
    prol_dv0 = P.cnt["dv"]

    P.w("tensor", "dg", dg_sh)
    P.w("tensor", "ac", prol_ac)
    for c in range(NCH):
        for hb in range(8):
            if hb > 0 or c > 0:
                P.w("tensor", "dv", P.cnt["dv"])
            for kd in range(8):
                P.do("tensor", lambda e, c=c, hb=hb, kd=kd: e.matmul(
                    pf[0][:, :], wbig[0][:, 1024 * kd + 128 * hb: 1024 * kd + 128 * hb + 128],
                    xh[:, 4096 * c + 512 * kd: 4096 * c + 512 * kd + 512],
                    start=(kd == 0), stop=(kd == 7)))
            for kd in range(8):
                P.do("tensor", lambda e, c=c, hb=hb, kd=kd: e.matmul(
                    pf[1][:, :], wbig[1][:, 1024 * kd + 128 * hb: 1024 * kd + 128 * hb + 128],
                    xh[:, 4096 * c + 512 * kd: 4096 * c + 512 * kd + 512],
                    start=(kd == 0), stop=(kd == 7)), inc="pe" if kd == 7 else None)
            P.w("scalar", "pe", P.cnt["pe"])
            if hb > 0 or c > 0:
                P.w("scalar", "dv", P.cnt["dv"])
            P.do("scalar", lambda e: e.activation(sg[:, :], pf[0][:, :], AF.Silu),
                 inc="ac")
            P.w("vector", "ac", P.cnt["ac"])
            P.do("vector", lambda e, hb=hb: e.tensor_tensor(
                shact[:, 512 * hb: 512 * hb + 512], pf[1][:, :], sg[:, :],
                ALU.mult), inc="dv")
        P.w("tensor", "dv", P.cnt["dv"])
        for itl in range(4):
            i = 4 * c + itl
            if itl > 0 or c > 0:
                P.w("tensor", "ac", P.cnt["ac"])
            for hb in range(8):
                for nh in range(2):
                    P.do("tensor", lambda e, itl=itl, hb=hb, nh=nh: e.matmul(
                        pf[2 + nh][:, :],
                        shact[:, 512 * hb + 128 * itl: 512 * hb + 128 * itl + 128],
                        wbig[2][:, 1024 * hb + 512 * nh: 1024 * hb + 512 * nh + 512],
                        start=(hb == 0), stop=(hb == 7)),
                        inc="pe" if (hb, nh) == (7, 1) else None)
            P.w("scalar", "pe", P.cnt["pe"])
            for nh in range(2):
                P.do("scalar", lambda e, i=i, nh=nh: e.copy(
                    acc[:, 1024 * i + 512 * nh: 1024 * i + 512 * nh + 512],
                    pf[2 + nh][:, :]), inc="ac" if nh == 1 else None)

    EW = [
        (wbig[0], 0, wbig[0], 4096, wbig[1], 0),
        (wbig[1], 4096, wbig[2], 0, wbig[2], 4096),
    ]

    ew_cnt = {}

    def load_expert(e_, pe_guard):
        ub, uo, gb_, go, db, do_ = EW[e_ % 2]
        sem = "dw%d" % (e_ % 2)
        P.w("gpsimd", "pe", pe_guard)
        load_weight_cast(P, ub, up_i.ap(), 1024, 512, dst_off=uo,
                         src_off=e_ * 1024 * 512, inc=sem)
        load_weight_cast(P, gb_, ga_i.ap(), 1024, 512, dst_off=go,
                         src_off=e_ * 1024 * 512, inc=sem)
        load_weight_cast(P, db, dw_i.ap(), 512, 1024, dst_off=do_,
                         src_off=e_ * 512 * 1024, inc=sem)
        ew_cnt[e_] = (sem, P.cnt[sem])

    emit_top2(P, 16, lgtA, rbb, lgt, cmb, tks, mskd)

    load_expert(0, P.cnt["pe"])
    evac_dv = {}
    dn_cnt = {}
    for e_ in range(16):
        ub, uo, gb_, go, db, do_ = EW[e_ % 2]
        if e_ + 1 < 16:
            load_expert(e_ + 1, P.cnt["pe"])
        P.w("tensor", *ew_cnt[e_])
        for c in range(NCH):
            for hb in range(4):
                q = 4 * c + hb
                key = (e_, q)
                prev = (e_, q - 2) if q >= 2 else (e_ - 1, q + 14)
                pA, pB = (pf[0], pf[1]) if q % 2 == 0 else (pf[4], pf[5])
                sgq = sg if q % 2 == 0 else sg2
                if prev in evac_dv:
                    P.w("tensor", "dv", evac_dv[prev])
                elif e_ > 0 or c > 0 or hb > 0:
                    P.w("tensor", "dv", P.cnt["dv"])
                if q % 2 == 1 and hb == 1:
                    kk = (e_, c - 1, 3) if c > 0 else (e_ - 1, NCH - 1, 3)
                    if kk in dn_cnt:
                        P.w("tensor", "dv", dn_cnt[kk])
                for kd in range(8):
                    P.do("tensor", lambda e, c=c, hb=hb, kd=kd, gb_=gb_, go=go,
                         pA=pA: e.matmul(
                             pA[:, :],
                             gb_[:, go + 512 * kd + 128 * hb: go + 512 * kd + 128 * hb + 128],
                             xh[:, 4096 * c + 512 * kd: 4096 * c + 512 * kd + 512],
                             start=(kd == 0), stop=(kd == 7)))
                for kd in range(8):
                    P.do("tensor", lambda e, c=c, hb=hb, kd=kd, ub=ub, uo=uo,
                         pB=pB: e.matmul(
                             pB[:, :],
                             ub[:, uo + 512 * kd + 128 * hb: uo + 512 * kd + 128 * hb + 128],
                             xh[:, 4096 * c + 512 * kd: 4096 * c + 512 * kd + 512],
                             start=(kd == 0), stop=(kd == 7)), inc="pe" if kd == 7 else None)
                pe_q = P.cnt["pe"]
                P.w("scalar", "pe", pe_q)
                if prev in evac_dv:
                    P.w("scalar", "dv", evac_dv[prev])
                P.do("scalar", lambda e, pA=pA, sgq=sgq: e.activation(
                    sgq[:, :], pA[:, :], AF.Silu), inc="ac")
                P.w("vector", "ac", P.cnt["ac"])
                P.do("vector", lambda e, hb=hb, pB=pB, sgq=sgq: e.tensor_tensor(
                    acte[:, 512 * hb: 512 * hb + 512], pB[:, :], sgq[:, :],
                    ALU.mult), inc="dv")
                evac_dv[key] = P.cnt["dv"]
            P.w("tensor", "dv", P.cnt["dv"])
            for itl in range(4):
                i = 4 * c + itl
                pD = (pf[2], pf[3]) if itl % 2 == 0 else (pf[4], pf[5])
                kprev = (e_, c, itl - 2) if itl >= 2 else                     ((e_, c - 1, itl + 2) if c > 0 else (e_ - 1, NCH - 1, itl + 2))
                if kprev in dn_cnt:
                    P.w("tensor", "dv", dn_cnt[kprev])
                for hb in range(4):
                    for nh in range(2):
                        P.do("tensor", lambda e, itl=itl, hb=hb, nh=nh, db=db,
                             do_=do_, pD=pD: e.matmul(
                                 pD[nh][:, :],
                                 acte[:, 512 * hb + 128 * itl: 512 * hb + 128 * itl + 128],
                                 db[:, do_ + 1024 * hb + 512 * nh:
                                    do_ + 1024 * hb + 512 * nh + 512],
                                 start=(hb == 0), stop=(hb == 3)),
                             inc="pe" if (hb, nh) == (3, 1) else None)
                P.w("scalar", "pe", P.cnt["pe"])
                P.w("scalar", "dv", P.cnt["dv"])
                tm = tmpe if itl % 2 == 0 else tmpe2
                for nh in range(2):
                    P.do("scalar", lambda e, i=i, nh=nh, e_=e_, tm=tm, pD=pD:
                         e.activation(
                             tm[:, 512 * nh: 512 * nh + 512], pD[nh][:, :],
                             AF.Identity, scale=cmb[:, 16 * i + e_: 16 * i + e_ + 1]),
                         inc="ac" if nh == 1 else None)
                P.w("vector", "ac", P.cnt["ac"])
                for nh in range(2):
                    P.do("vector", lambda e, i=i, nh=nh, tm=tm: e.tensor_tensor(
                        acc[:, 1024 * i + 512 * nh: 1024 * i + 512 * nh + 512],
                        acc[:, 1024 * i + 512 * nh: 1024 * i + 512 * nh + 512],
                        tm[:, 512 * nh: 512 * nh + 512], ALU.add),
                        inc="dv" if nh == 1 else None)
                dn_cnt[(e_, c, itl)] = P.cnt["dv"]
            if e_ == 15:
                _l0ffn_epi_chunk(P, c, (xin, xin_cnt, acc, lst, junk,
                                        x1_i, x2_o, prol_dv0))

    P.wcur("sync", "st")


def _l0ffn_epi_chunk(P, c, ctx):
    xin, xin_cnt, acc, lst, junk, x1_i, x2_o, prol_dv0 = ctx
    if True:
        if c == 0:
            P.w("sync", "dv", prol_dv0)
            P.do("sync", lambda e: e.dma_start(out=xin[0][:, :],
                                               in_=x1_i.ap()[0:128, :]),
                 inc="dx0", n=16)
            xin_cnt[100] = ("dx0", P.cnt["dx0"])
        for itl in range(4):
            i = 4 * c + itl
            if i + 1 < NT:
                P.w("sync", "dv", P.cnt["dv"])
                sem = "dx%d" % ((i + 1) % 2)
                P.do("sync", lambda e, i=i: e.dma_start(
                    out=xin[(i + 1) % 2][:, :],
                    in_=x1_i.ap()[128 * (i + 1): 128 * (i + 1) + 128, :]),
                    inc=sem, n=16)
                xin_cnt[100 + i + 1] = (sem, P.cnt[sem])
            P.w("vector", *xin_cnt[100 + i])
            P.do("vector", lambda e, i=i: e.tensor_tensor(
                acc[:, 1024 * i: 1024 * i + 1024], acc[:, 1024 * i: 1024 * i + 1024],
                xin[i % 2][:, :], ALU.add), inc="dv")
        for itl in range(4):
            i = 4 * c + itl
            P.do("vector", lambda e, i=i, itl=itl: e.reduce_sum(
                lst[:, itl: itl + 1], acc[:, 1024 * i: 1024 * i + 1024], axis=AX.X))
        for itl in range(4):
            P.do("vector", lambda e, itl=itl: e.tensor_scalar(
                lst[:, itl: itl + 1], lst[:, itl: itl + 1], 1.0 / 1024, None,
                ALU.mult), inc="dv" if itl == 3 else None)
        P.w("scalar", "dv", P.cnt["dv"])
        for itl in range(4):
            i = 4 * c + itl
            P.do("scalar", lambda e, i=i, itl=itl: e.activation(
                junk[:, :], acc[:, 1024 * i: 1024 * i + 1024], AF.Square,
                accum_out=lst[:, 4 + itl: 5 + itl]), inc="ac" if itl == 3 else None)
        P.w("vector", "ac", P.cnt["ac"])
        for itl in range(4):
            P.do("vector", lambda e, itl=itl: e.tensor_tensor(
                lst[:, 8 + itl: 9 + itl], lst[:, itl: itl + 1], lst[:, itl: itl + 1],
                ALU.mult))
        for itl in range(4):
            P.do("vector", lambda e, itl=itl: e.tensor_scalar(
                lst[:, 12 + itl: 13 + itl], lst[:, 4 + itl: 5 + itl], 1.0 / 1024,
                None, ALU.mult))
        for itl in range(4):
            P.do("vector", lambda e, itl=itl: e.tensor_tensor(
                lst[:, 16 + itl: 17 + itl], lst[:, 12 + itl: 13 + itl],
                lst[:, 8 + itl: 9 + itl], ALU.subtract))
        for itl in range(4):
            P.do("vector", lambda e, itl=itl: e.tensor_scalar(
                lst[:, 16 + itl: 17 + itl], lst[:, 16 + itl: 17 + itl], EPS, None,
                ALU.add), inc="dv" if itl == 3 else None)
        P.w("scalar", "dv", P.cnt["dv"])
        for itl in range(4):
            P.do("scalar", lambda e, itl=itl: e.activation(
                lst[:, 20 + itl: 21 + itl], lst[:, 16 + itl: 17 + itl], AF.Sqrt),
                inc="ac" if itl == 3 else None)
        P.w("vector", "ac", P.cnt["ac"])
        for itl in range(4):
            P.do("vector", lambda e, itl=itl: e.reciprocal(
                lst[:, 24 + itl: 25 + itl], lst[:, 20 + itl: 21 + itl]))
        for itl in range(4):
            i = 4 * c + itl
            P.do("vector", lambda e, i=i, itl=itl: e.tensor_scalar(
                acc[:, 1024 * i: 1024 * i + 1024], acc[:, 1024 * i: 1024 * i + 1024],
                lst[:, itl: itl + 1], None, ALU.subtract))
        for itl in range(4):
            i = 4 * c + itl
            P.do("vector", lambda e, i=i, itl=itl: e.tensor_scalar(
                acc[:, 1024 * i: 1024 * i + 1024], acc[:, 1024 * i: 1024 * i + 1024],
                lst[:, 24 + itl: 25 + itl], None, ALU.mult),
                inc="dv" if itl == 3 else None)
        P.wcur("sync", "dv")
        for itl in range(4):
            i = 4 * c + itl
            P.do("sync", lambda e, i=i: e.dma_start(
                out=x2_o.ap()[128 * i: 128 * i + 128, :],
                in_=acc[:, 1024 * i: 1024 * i + 1024]), inc="st", n=16)


def build_launch2b():
    P = Prog()
    nc = P.nc
    P.sem_names("dl", "dg", "pe", "dv", "ac", "po", "st", "bar",
                "dx0", "dx1", "dc0", "dc1", "dw0", "dw1", "dw2", "dwd")
    E = dict(
        x1=nc.dram_tensor("x1", [T, 1024], F32, kind="ExternalInput"),
        rw=nc.dram_tensor("rw", [1024, 16], F32, kind="ExternalInput"),
        rb=nc.dram_tensor("rb", [1, 16], F32, kind="ExternalInput"),
        up=nc.dram_tensor("up", [16, 1024, 512], F32, kind="ExternalInput"),
        ga=nc.dram_tensor("ga", [16, 1024, 512], F32, kind="ExternalInput"),
        dw=nc.dram_tensor("dw", [16, 512, 1024], F32, kind="ExternalInput"),
        shg=nc.dram_tensor("shg", [1024, 1024], F32, kind="ExternalInput"),
        shu=nc.dram_tensor("shu", [1024, 1024], F32, kind="ExternalInput"),
        shd=nc.dram_tensor("shd", [1024, 1024], F32, kind="ExternalInput"),
        x2=nc.dram_tensor("x2", [T, 1024], F32, kind="ExternalOutput"))
    pf = [_ps(nc, "pf%d" % i, [128, 512], F32) for i in range(6)]
    pb = [_ps(nc, "pb%d" % i, [128, 1024], BF16) for i in range(2)]
    phase_l0ffn(P, pf, pb, E)
    return P.build(None)


def _dev_launch2b(inp, x1_all):
    nc = build_launch2b()
    up = np.ascontiguousarray(np.asarray(inp["L0_ffn_up_proj"]))
    ga = np.ascontiguousarray(np.asarray(inp["L0_ffn_gate_proj"]))
    dw = np.ascontiguousarray(np.asarray(inp["L0_ffn_down_proj"]))
    rw = np.ascontiguousarray(np.asarray(inp["L0_ffn_router_w"]).T)
    rb = np.ascontiguousarray(np.asarray(inp["L0_ffn_router_bias"]).reshape(1, 16))
    shg = np.ascontiguousarray(np.asarray(inp["L0_ffn_sh_gate"]).T)
    shu = np.ascontiguousarray(np.asarray(inp["L0_ffn_sh_up"]).T)
    shd = np.ascontiguousarray(np.asarray(inp["L0_ffn_sh_down"]).T)
    in_maps = []
    for c in range(8):
        in_maps.append(dict(x1=np.ascontiguousarray(x1_all[c]), rw=rw, rb=rb,
                            up=up, ga=ga, dw=dw, shg=shg, shu=shu, shd=shd))
    res = run_bass_kernel_spmd(nc, in_maps, core_ids=list(range(8)))
    return res.results


def phase_l1qkv(P, pf, pb, E):
    nc = P.nc
    x2_i, wq1_i, wk1_i, wv1_i, q1f_o, k1f_o, k1t_o, v1t_o, ds1_o = (
        E["x2"], E["wq1"], E["wk1"], E["wv1"], E["q1f"], E["k1f"], E["k1t"],
        E["v1t"], E["ds1"])

    ident = _sb(nc, "ident", [128, 128], BF16)
    wq1_s = _sb(nc, "wq1_s", [128, 8 * 512], BF16)
    wk1_s = _sb(nc, "wk1_s", [128, 8 * 512], BF16)
    wv1_s = _sb(nc, "wv1_s", [128, 8 * 512], BF16)
    xin = [_sb(nc, "xin%d" % i, [128, 1024], F32) for i in range(2)]
    xb = _sb(nc, "xb", [128, 1024], BF16)
    xh_t = _sb(nc, "xh_t", [128, 1024], BF16)
    q1f_s = _sb(nc, "q1f_s", [128, 4 * T], BF16)
    k1f_s = _sb(nc, "k1f_s", [128, 4 * T], BF16)
    k1t_s = _sb(nc, "k1t_s", [128, NT * 512], BF16)
    v1t_s = _sb(nc, "v1t_s", [128, NT * 512], BF16)
    ds1_s = _sb(nc, "ds1_s", [128, 4 * 512], F32)

    load_weight_cast(P, wq1_s, wq1_i.ap(), 1024, 512)
    load_weight_cast(P, wk1_s, wk1_i.ap(), 1024, 512)
    load_weight_cast(P, wv1_s, wv1_i.ap(), 1024, 512)
    dg_w = P.cnt["dg"]
    P.do("gpsimd", lambda e: e.memset(ident[:], 0.0))
    P.do("gpsimd", lambda e: e.affine_select(
        out=ident[:], in_=ident[:], compare_op=ALU.not_equal, fill=1.0,
        base=0, pattern=[[-1, 128]], channel_multiplier=1), inc="po")

    xin_cnt = {}
    P.do("sync", lambda e: e.dma_start(out=xin[0][:, :], in_=x2_i.ap()[0:128, :]),
         inc="dx0", n=16)
    xin_cnt[0] = ("dx0", P.cnt["dx0"])
    P.w("tensor", "dg", dg_w)
    P.w("tensor", "po", P.cnt["po"])
    for i in range(NT):
        if i + 1 < NT:
            P.w("sync", "dv", P.cnt["dv"])
            sem = "dx%d" % ((i + 1) % 2)
            P.do("sync", lambda e, i=i: e.dma_start(
                out=xin[(i + 1) % 2][:, :],
                in_=x2_i.ap()[128 * (i + 1): 128 * (i + 1) + 128, :]), inc=sem, n=16)
            xin_cnt[i + 1] = (sem, P.cnt[sem])
        P.w("vector", *xin_cnt[i])
        P.do("vector", lambda e, i=i: e.tensor_copy(xb[:, :], xin[i % 2][:, :]),
             inc="dv")
        P.w("tensor", "dv", P.cnt["dv"])
        for grp in range(2):
            if i > 0 or grp > 0:
                P.w("tensor", "dv", P.cnt["dv"])
            for u in range(4):
                kd = 4 * grp + u
                P.do("tensor", lambda e, kd=kd, u=u, grp=grp: e.transpose(
                    pb[grp][:, 128 * u: 128 * u + 128],
                    xb[:, 128 * kd: 128 * kd + 128], ident[:, :]),
                    inc="pe" if u == 3 else None)
            P.w("vector", "pe", P.cnt["pe"])
            for u in range(4):
                kd = 4 * grp + u
                P.do("vector", lambda e, kd=kd, u=u, grp=grp: e.tensor_copy(
                    xh_t[:, 128 * kd: 128 * kd + 128],
                    pb[grp][:, 128 * u: 128 * u + 128]),
                    inc="dv" if u == 3 else None)
        P.w("tensor", "dv", P.cnt["dv"])
        for (wsrc, dstf) in ((wq1_s, q1f_s), (wk1_s, k1f_s)):
            for fm in range(4):
                psx = pf[fm % 2]
                if not (wsrc is wq1_s and fm < 2 and i == 0):
                    P.w("tensor", "dv", P.cnt["dv"])
                for kd in range(8):
                    P.do("tensor", lambda e, fm=fm, kd=kd, psx=psx, wsrc=wsrc:
                         e.matmul(
                             psx[:, 0:128],
                             wsrc[:, 512 * kd + 128 * fm: 512 * kd + 128 * fm + 128],
                             xh_t[:, 128 * kd: 128 * kd + 128],
                             start=(kd == 0), stop=(kd == 7)),
                         inc="pe" if kd == 7 else None)
                P.w("vector", "pe", P.cnt["pe"])
                P.do("vector", lambda e, fm=fm, i=i, psx=psx, dstf=dstf:
                     e.tensor_copy(
                         dstf[:, T * fm + 128 * i: T * fm + 128 * i + 128],
                         psx[:, 0:128]), inc="dv")
        for (wsrc, dstt, psx) in ((wk1_s, k1t_s, pf[2]), (wv1_s, v1t_s, pf[3])):
            if i > 0:
                P.w("tensor", "dv", P.cnt["dv"])
            for kd in range(8):
                P.do("tensor", lambda e, kd=kd, psx=psx, wsrc=wsrc: e.matmul(
                    psx[:, :], xh_t[:, 128 * kd: 128 * kd + 128],
                    wsrc[:, 512 * kd: 512 * kd + 512],
                    start=(kd == 0), stop=(kd == 7)), inc="pe" if kd == 7 else None)
            P.w("vector", "pe", P.cnt["pe"])
            P.do("vector", lambda e, i=i, psx=psx, dstt=dstt: e.tensor_copy(
                dstt[:, 512 * i: 512 * i + 512], psx[:, :]), inc="dv")

    P.barrier()
    for fm in range(4):
        for g in range(NT):
            P.do("tensor", lambda e, g=g, fm=fm: e.matmul(
                pf[2 + fm][:, :], k1t_s[:, 512 * g + 128 * fm: 512 * g + 128 * fm + 128],
                v1t_s[:, 512 * g: 512 * g + 512],
                start=(g == 0), stop=(g == NT - 1)),
                inc="pe" if g == NT - 1 else None)
    P.w("vector", "pe", P.cnt["pe"])
    for fm in range(4):
        P.do("vector", lambda e, fm=fm: e.tensor_copy(
            ds1_s[:, 512 * fm: 512 * fm + 512], pf[2 + fm][:, :]),
            inc="dv" if fm == 3 else None)

    P.barrier()
    for fm in range(4):
        P.do("sync", lambda e, fm=fm: e.dma_start(
            out=q1f_o.ap()[128 * fm: 128 * fm + 128, :],
            in_=q1f_s[:, T * fm: T * fm + T]), inc="st", n=16)
        P.do("sync", lambda e, fm=fm: e.dma_start(
            out=k1f_o.ap()[128 * fm: 128 * fm + 128, :],
            in_=k1f_s[:, T * fm: T * fm + T]), inc="st", n=16)
        P.do("sync", lambda e, fm=fm: e.dma_start(
            out=ds1_o.ap()[128 * fm: 128 * fm + 128, :],
            in_=ds1_s[:, 512 * fm: 512 * fm + 512]), inc="st", n=16)
    for i in range(NT):
        P.do("sync", lambda e, i=i: e.dma_start(
            out=k1t_o.ap()[128 * i: 128 * i + 128, :],
            in_=k1t_s[:, 512 * i: 512 * i + 512]), inc="st", n=16)
        P.do("sync", lambda e, i=i: e.dma_start(
            out=v1t_o.ap()[128 * i: 128 * i + 128, :],
            in_=v1t_s[:, 512 * i: 512 * i + 512]), inc="st", n=16)
    P.wcur("sync", "st")


def build_launch2c():
    P = Prog()
    nc = P.nc
    P.sem_names("dl", "dg", "pe", "dv", "ac", "po", "st", "bar",
                "dx0", "dx1", "dc0", "dc1", "dw0", "dw1", "dw2", "dwd")
    E = dict(
        x2=nc.dram_tensor("x2", [T, 1024], F32, kind="ExternalInput"),
        wq1=nc.dram_tensor("wq1", [1024, 512], F32, kind="ExternalInput"),
        wk1=nc.dram_tensor("wk1", [1024, 512], F32, kind="ExternalInput"),
        wv1=nc.dram_tensor("wv1", [1024, 512], F32, kind="ExternalInput"),
        q1f=nc.dram_tensor("q1f", [512, T], BF16, kind="ExternalOutput"),
        k1f=nc.dram_tensor("k1f", [512, T], BF16, kind="ExternalOutput"),
        k1t=nc.dram_tensor("k1t", [T, 512], BF16, kind="ExternalOutput"),
        v1t=nc.dram_tensor("v1t", [T, 512], BF16, kind="ExternalOutput"),
        ds1=nc.dram_tensor("ds1", [512, 512], F32, kind="ExternalOutput"))
    pf = [_ps(nc, "pf%d" % i, [128, 512], F32) for i in range(6)]
    pb = [_ps(nc, "pb%d" % i, [128, 1024], BF16) for i in range(2)]
    phase_l1qkv(P, pf, pb, E)
    return P.build(None)


def phase_l1attn(P, pf, pb, E):
    nc = P.nc
    x2_i, q1f_i, k1f_i, k1t_i, v1t_i, ds1p_i, wo1_i, x3_o = (
        E["x2"], E["q1f"], E["k1f"], E["k1t"], E["v1t"], E["ds1p"],
        E["wo1"], E["x3"])

    bigms = _sb(nc, "bigms", [128, 1024], F32)
    wo1_s = _sb(nc, "wo1_s", [128, 4 * 1024], BF16)
    q1f_s = _sb(nc, "q1f_s", [128, 4 * T], BF16)
    k1f_s = _sb(nc, "k1f_s", [128, 4 * T], BF16)
    k1t_s = _sb(nc, "k1t_s", [128, NT * 512], BF16)
    v1t_s = _sb(nc, "v1t_s", [128, NT * 512], BF16)
    S1 = _sb(nc, "S1", [128, 2048], F32)
    S1b = _sb(nc, "S1b", [128, 2048], BF16)
    stg = [_sb(nc, "stg%d" % i, [128, 2048], F32) for i in range(3)]
    sbf = [_sb(nc, "sbf%d" % i, [128, 512], BF16) for i in range(4)]
    yf = _sb(nc, "yf", [128, 4 * 512], BF16)
    xin = [_sb(nc, "xin%d" % i, [128, 1024], F32) for i in range(2)]
    x3_s = _sb(nc, "x3_s", [128, NT * 1024], F32)
    lst = _sb(nc, "lst", [128, 64], F32)
    junk = _sb(nc, "junk", [128, 1024], F32)

    load_weight_cast(P, wo1_s, wo1_i.ap(), 512, 1024)
    dg_w = P.cnt["dg"]
    for fm in range(4):
        P.do("sync", lambda e, fm=fm: e.dma_start(
            out=q1f_s[:, T * fm: T * fm + T],
            in_=q1f_i.ap()[128 * fm: 128 * fm + 128, :]), inc="dl", n=16)
        P.do("sync", lambda e, fm=fm: e.dma_start(
            out=k1f_s[:, T * fm: T * fm + T],
            in_=k1f_i.ap()[128 * fm: 128 * fm + 128, :]), inc="dl", n=16)
    for i in range(NT):
        P.do("sync", lambda e, i=i: e.dma_start(
            out=k1t_s[:, 512 * i: 512 * i + 512],
            in_=k1t_i.ap()[128 * i: 128 * i + 128, :]), inc="dl", n=16)
        P.do("sync", lambda e, i=i: e.dma_start(
            out=v1t_s[:, 512 * i: 512 * i + 512],
            in_=v1t_i.ap()[128 * i: 128 * i + 128, :]), inc="dl", n=16)
    for r in range(3):
        for fm in range(4):
            P.do("sync", lambda e, r=r, fm=fm: e.dma_start(
                out=stg[r][:, 512 * fm: 512 * fm + 512],
                in_=ds1p_i.ap()[r][128 * fm: 128 * fm + 128, :]), inc="dl", n=16)
    dl_all = P.cnt["dl"]
    P.do("gpsimd", lambda e: e.memset(bigms[:, :], 1.0))
    P.do("gpsimd", lambda e: e.affine_select(
        out=bigms[:, :], in_=bigms[:, :], compare_op=ALU.is_ge, fill=0.0,
        base=-513, pattern=[[1, 1024]], channel_multiplier=-1), inc="po")
    P.w("vector", "dl", dl_all)
    P.do("vector", lambda e: e.tensor_add(S1[:, :], stg[0][:, :], stg[1][:, :]))
    P.do("vector", lambda e: e.tensor_copy(junk[:, 0:64], stg[0][:, 0:64]))
    P.do("vector", lambda e: e.tensor_add(S1[:, :], S1[:, :], stg[2][:, :]))
    P.do("vector", lambda e: e.tensor_copy(junk[:, 0:64], stg[0][:, 0:64]))
    P.do("vector", lambda e: e.tensor_copy(S1b[:, :], S1[:, :]), inc="dv")
    P.barrier()

    P.do("sync", lambda e: e.dma_start(out=xin[0][:, :], in_=x2_i.ap()[0:128, :]),
         inc="dx0", n=16)
    dl_x = {0: ("dx0", P.cnt["dx0"])}
    mask_dv = {}
    s1add_dv = {}
    resid_dv = {}
    yf_dv = {}
    for c in range(NCH):
        for jt in range(4):
            psc = pf[jt % 2]
            if jt >= 2:
                P.w("tensor", "dv", mask_dv[(c, jt - 2)])
            elif c > 0:
                P.w("tensor", "dv", resid_dv[(c - 1, 2)])
            for fm in range(4):
                P.do("tensor", lambda e, c=c, jt=jt, fm=fm, psc=psc: e.matmul(
                    psc[:, :],
                    k1f_s[:, T * fm + 512 * c + 128 * jt: T * fm + 512 * c + 128 * jt + 128],
                    q1f_s[:, T * fm + 512 * c: T * fm + 512 * c + 512],
                    start=(fm == 0), stop=(fm == 3)), inc="pe" if fm == 3 else None)
            P.w("vector", "pe", P.cnt["pe"])
            P.do("vector", lambda e, jt=jt, psc=psc: e.tensor_tensor(
                sbf[jt][:, :], psc[:, :],
                bigms[:, 512 - 128 * jt: 1024 - 128 * jt], ALU.mult), inc="dv")
            mask_dv[(c, jt)] = P.cnt["dv"]
        P.w("tensor", "dv", P.cnt["dv"])
        for dt in range(4):
            psy = pf[2 + dt]
            first = True
            for jt in range(4):
                P.do("tensor", lambda e, c=c, jt=jt, dt=dt, psy=psy, first=first:
                     e.matmul(
                         psy[:, :],
                         v1t_s[:, 512 * (4 * c + jt) + 128 * dt:
                               512 * (4 * c + jt) + 128 * dt + 128],
                         sbf[jt][:, :], start=first, stop=False))
                first = False
            for fm in range(4):
                P.do("tensor", lambda e, c=c, fm=fm, dt=dt, psy=psy: e.matmul(
                    psy[:, :], S1b[:, 512 * fm + 128 * dt: 512 * fm + 128 * dt + 128],
                    q1f_s[:, T * fm + 512 * c: T * fm + 512 * c + 512],
                    start=False, stop=(fm == 3)),
                    inc="pe" if fm == 3 else None)
        P.w("vector", "pe", P.cnt["pe"])
        for dt in range(4):
            P.do("vector", lambda e, dt=dt: e.tensor_copy(
                yf[:, 512 * dt: 512 * dt + 512], pf[2 + dt][:, :]),
                inc="dv" if dt == 3 else None)
        yf_dv[c] = P.cnt["dv"]
        for fm in range(4):
            psx = pf[fm % 2]
            if fm < 2:
                P.w("tensor", "dv", mask_dv[(c, 2 + fm)])
            else:
                P.w("tensor", "dv", s1add_dv[(c, fm - 2)])
            for jt in range(4):
                P.do("tensor", lambda e, c=c, jt=jt, fm=fm, psx=psx: e.matmul(
                    psx[:, :],
                    k1t_s[:, 512 * (4 * c + jt) + 128 * fm:
                          512 * (4 * c + jt) + 128 * fm + 128],
                    v1t_s[:, 512 * (4 * c + jt): 512 * (4 * c + jt) + 512],
                    start=(jt == 0), stop=(jt == 3)), inc="pe" if jt == 3 else None)
            P.w("vector", "pe", P.cnt["pe"])
            P.do("vector", lambda e, fm=fm, psx=psx: e.tensor_tensor(
                S1[:, 512 * fm: 512 * fm + 512], S1[:, 512 * fm: 512 * fm + 512],
                psx[:, :], ALU.add))
            P.do("vector", lambda e, fm=fm: e.tensor_copy(
                S1b[:, 512 * fm: 512 * fm + 512], S1[:, 512 * fm: 512 * fm + 512]),
                inc="dv")
            s1add_dv[(c, fm)] = P.cnt["dv"]
        for itl in range(4):
            i = 4 * c + itl
            pw = (pf[0], pf[1]) if itl % 2 == 0 else (pf[2], pf[3])
            if i + 1 < NT:
                P.w("sync", "dv", P.cnt["dv"])
                sem = "dx%d" % ((i + 1) % 2)
                P.do("sync", lambda e, i=i: e.dma_start(
                    out=xin[(i + 1) % 2][:, :],
                    in_=x2_i.ap()[128 * (i + 1): 128 * (i + 1) + 128, :]),
                    inc=sem, n=16)
                dl_x[i + 1] = (sem, P.cnt[sem])
            if itl == 0:
                P.w("tensor", "dv", s1add_dv[(c, 3)])
            elif itl == 1:
                P.w("tensor", "dv", yf_dv[c])
            else:
                P.w("tensor", "dv", resid_dv[(c, itl - 2)])
            P.w("tensor", "dg", dg_w)
            for dt in range(4):
                for nh in range(2):
                    P.do("tensor", lambda e, itl=itl, dt=dt, nh=nh, pw=pw: e.matmul(
                        pw[nh][:, :], yf[:, 512 * dt + 128 * itl: 512 * dt + 128 * itl + 128],
                        wo1_s[:, 1024 * dt + 512 * nh: 1024 * dt + 512 * nh + 512],
                        start=(dt == 0), stop=(dt == 3)),
                        inc="pe" if (dt, nh) == (3, 1) else None)
            P.w("vector", "pe", P.cnt["pe"])
            P.w("vector", *dl_x[i])
            for nh in range(2):
                P.do("vector", lambda e, i=i, nh=nh, pw=pw: e.tensor_tensor(
                    x3_s[:, 1024 * i + 512 * nh: 1024 * i + 512 * nh + 512],
                    xin[i % 2][:, 512 * nh: 512 * nh + 512],
                    pw[nh][:, :], ALU.add), inc="dv" if nh == 1 else None)
            resid_dv[(c, itl)] = P.cnt["dv"]
        for itl in range(4):
            i = 4 * c + itl
            P.do("vector", lambda e, i=i, itl=itl: e.reduce_sum(
                lst[:, itl: itl + 1], x3_s[:, 1024 * i: 1024 * i + 1024], axis=AX.X))
        for itl in range(4):
            P.do("vector", lambda e, itl=itl: e.tensor_scalar(
                lst[:, itl: itl + 1], lst[:, itl: itl + 1], 1.0 / 1024, None,
                ALU.mult), inc="dv" if itl == 3 else None)
        P.w("scalar", "dv", P.cnt["dv"])
        for itl in range(4):
            i = 4 * c + itl
            P.do("scalar", lambda e, i=i, itl=itl: e.activation(
                junk[:, :], x3_s[:, 1024 * i: 1024 * i + 1024], AF.Square,
                accum_out=lst[:, 4 + itl: 5 + itl]), inc="ac" if itl == 3 else None)
        P.w("vector", "ac", P.cnt["ac"])
        for itl in range(4):
            P.do("vector", lambda e, itl=itl: e.tensor_tensor(
                lst[:, 8 + itl: 9 + itl], lst[:, itl: itl + 1], lst[:, itl: itl + 1],
                ALU.mult))
        for itl in range(4):
            P.do("vector", lambda e, itl=itl: e.tensor_scalar(
                lst[:, 12 + itl: 13 + itl], lst[:, 4 + itl: 5 + itl], 1.0 / 1024,
                None, ALU.mult))
        for itl in range(4):
            P.do("vector", lambda e, itl=itl: e.tensor_tensor(
                lst[:, 16 + itl: 17 + itl], lst[:, 12 + itl: 13 + itl],
                lst[:, 8 + itl: 9 + itl], ALU.subtract))
        for itl in range(4):
            P.do("vector", lambda e, itl=itl: e.tensor_scalar(
                lst[:, 16 + itl: 17 + itl], lst[:, 16 + itl: 17 + itl], EPS, None,
                ALU.add), inc="dv" if itl == 3 else None)
        P.w("scalar", "dv", P.cnt["dv"])
        for itl in range(4):
            P.do("scalar", lambda e, itl=itl: e.activation(
                lst[:, 20 + itl: 21 + itl], lst[:, 16 + itl: 17 + itl], AF.Sqrt),
                inc="ac" if itl == 3 else None)
        P.w("vector", "ac", P.cnt["ac"])
        for itl in range(4):
            P.do("vector", lambda e, itl=itl: e.reciprocal(
                lst[:, 24 + itl: 25 + itl], lst[:, 20 + itl: 21 + itl]))
        for itl in range(4):
            i = 4 * c + itl
            P.do("vector", lambda e, i=i, itl=itl: e.tensor_scalar(
                x3_s[:, 1024 * i: 1024 * i + 1024], x3_s[:, 1024 * i: 1024 * i + 1024],
                lst[:, itl: itl + 1], None, ALU.subtract))
        for itl in range(4):
            i = 4 * c + itl
            P.do("vector", lambda e, i=i, itl=itl: e.tensor_scalar(
                x3_s[:, 1024 * i: 1024 * i + 1024], x3_s[:, 1024 * i: 1024 * i + 1024],
                lst[:, 24 + itl: 25 + itl], None, ALU.mult),
                inc="dv" if itl == 3 else None)

    P.barrier()
    for i in range(NT):
        P.do("sync", lambda e, i=i: e.dma_start(
            out=x3_o.ap()[128 * i: 128 * i + 128, :],
            in_=x3_s[:, 1024 * i: 1024 * i + 1024]), inc="st", n=16)
    P.wcur("sync", "st")


def build_launch3a():
    P = Prog()
    nc = P.nc
    P.sem_names("dl", "dg", "pe", "dv", "ac", "po", "st", "bar",
                "dx0", "dx1", "dc0", "dc1", "dw0", "dw1", "dw2", "dwd")
    E = dict(
        x2=nc.dram_tensor("x2", [T, 1024], F32, kind="ExternalInput"),
        q1f=nc.dram_tensor("q1f", [512, T], BF16, kind="ExternalInput"),
        k1f=nc.dram_tensor("k1f", [512, T], BF16, kind="ExternalInput"),
        k1t=nc.dram_tensor("k1t", [T, 512], BF16, kind="ExternalInput"),
        v1t=nc.dram_tensor("v1t", [T, 512], BF16, kind="ExternalInput"),
        ds1p=nc.dram_tensor("ds1p", [3, 512, 512], F32, kind="ExternalInput"),
        wo1=nc.dram_tensor("wo1", [512, 1024], F32, kind="ExternalInput"),
        x3=nc.dram_tensor("x3", [T, 1024], F32, kind="ExternalOutput"))
    pf = [_ps(nc, "pf%d" % i, [128, 512], F32) for i in range(6)]
    pb = [_ps(nc, "pb%d" % i, [128, 1024], BF16) for i in range(2)]
    phase_l1attn(P, pf, pb, E)
    return P.build(None)


def phase_l1ffn(P, pf, pb, E):
    E1 = 8
    nc = P.nc
    x3_i, wr_i, br_i, w1_i, b1_i, w2_i, b2_i, y_o = (
        E["x3"], E["wr"], E["br"], E["w1"], E["b1"], E["w2"], E["b2"], E["y"])

    ident = _sb(nc, "ident", [128, 128], BF16)
    identf = _sb(nc, "identf", [128, 128], F32)
    ones_col = _sb(nc, "ones_col", [1, 128], F32)
    onesb = _sb(nc, "onesb", [1, 512], BF16)
    rbrow = _sb(nc, "rbrow", [1, E1], F32)
    rbb = _sb(nc, "rbb", [128, E1], F32)
    rwf_s = _sb(nc, "rwf_s", [128, 8 * E1], F32)
    xtf = _sb(nc, "xtf", [128, 1024], F32)
    lgtA = _sb(nc, "lgtA", [128, NT * E1], F32)
    xh = _sb(nc, "xh", [128, 4 * 4096], BF16)
    xin = [_sb(nc, "xin%d" % i, [128, 1024], F32) for i in range(2)]
    xb = _sb(nc, "xb", [128, 1024], BF16)
    lgt = _sb(nc, "lgt", [128, NT * E1], F32)
    cmb = _sb(nc, "cmb", [128, NT * E1], F32)
    tks = _sb(nc, "tks", [128, 5 * NT], F32)
    mskd = _sb(nc, "mskd", [128, NT * E1], F32)
    bja = [_sb(nc, "bja%d" % i, [1, 512], BF16) for i in range(3)]
    bjg = [_sb(nc, "bjg%d" % i, [1, 512], BF16) for i in range(3)]
    b2e = [_sb(nc, "b2e%d" % i, [1, 1024], BF16) for i in range(2)]
    wa = [_sb(nc, "wa%d" % i, [128, 8 * 512], BF16) for i in range(3)]
    wg = [_sb(nc, "wg%d" % i, [128, 8 * 512], BF16) for i in range(3)]
    wd = [_sb(nc, "wd%d" % i, [128, 4 * 1024], BF16) for i in range(2)]
    act_s = _sb(nc, "act_s", [128, 8 * 512], BF16)
    sg = _sb(nc, "sg", [128, 512], F32)
    sg2 = _sb(nc, "sg2", [128, 512], F32)
    tmpe2 = _sb(nc, "tmpe2", [128, 1024], F32)
    acc = _sb(nc, "acc", [128, 8 * 1024], F32)
    dacc = _sb(nc, "dacc", [128, 8 * 1024], F32)
    tmpe = xtf
    lst = _sb(nc, "lst", [128, 64], F32)
    junk = _sb(nc, "junk", [128, 1024], BF16)

    NG = 2
    GT = 8

    rw_src = bass.AP(wr_i, 0, [[E1, 128], [128 * E1, 8], [1, E1]])
    rw_dst = bass.AP(rwf_s, 0, [[8 * E1, 128], [E1, 8], [1, E1]])
    P.do("sync", lambda e: e.dma_start(out=rw_dst, in_=rw_src), inc="dl", n=16)
    P.do("sync", lambda e: e.dma_start(out=rbrow[:, :], in_=br_i.ap()), inc="dl", n=16)
    dl_c = P.cnt["dl"]
    P.do("gpsimd", lambda e: e.memset(ident[:], 0.0))
    P.do("gpsimd", lambda e: e.affine_select(
        out=ident[:], in_=ident[:], compare_op=ALU.not_equal, fill=1.0,
        base=0, pattern=[[-1, 128]], channel_multiplier=1))
    P.do("gpsimd", lambda e: e.memset(identf[:], 0.0))
    P.do("gpsimd", lambda e: e.affine_select(
        out=identf[:], in_=identf[:], compare_op=ALU.not_equal, fill=1.0,
        base=0, pattern=[[-1, 128]], channel_multiplier=1))
    P.do("gpsimd", lambda e: e.memset(ones_col[:, :], 1.0))
    P.do("gpsimd", lambda e: e.memset(onesb[:, :], 1.0), inc="po")
    P.w("tensor", "po", P.cnt["po"])
    P.w("tensor", "dl", dl_c)
    P.do("tensor", lambda e: e.matmul(pf[5][:, 0:E1], ones_col[0:1, :],
                                      rbrow[0:1, :], start=True, stop=True), inc="pe")
    P.w("vector", "pe", P.cnt["pe"])
    P.do("vector", lambda e: e.tensor_copy(rbb[:, :], pf[5][:, 0:E1]), inc="dv")
    P.barrier()

    xin_cnt = {}
    P.do("sync", lambda e: e.dma_start(out=xin[0][:, :], in_=x3_i.ap()[0:128, :]),
         inc="dx0", n=16)
    xin_cnt[0] = ("dx0", P.cnt["dx0"])
    for i in range(NT):
        c, itl = i // 4, i % 4
        if i + 1 < NT:
            P.w("sync", "dv", P.cnt["dv"])
            sem = "dx%d" % ((i + 1) % 2)
            P.do("sync", lambda e, i=i: e.dma_start(
                out=xin[(i + 1) % 2][:, :],
                in_=x3_i.ap()[128 * (i + 1): 128 * (i + 1) + 128, :]), inc=sem, n=16)
            xin_cnt[i + 1] = (sem, P.cnt[sem])
        P.w("vector", *xin_cnt[i])
        P.do("vector", lambda e, i=i: e.tensor_copy(xb[:, :], xin[i % 2][:, :]),
             inc="dv")
        P.w("tensor", "dv", P.cnt["dv"])
        for grp in range(2):
            if i > 0 or grp > 0:
                P.w("tensor", "dv", P.cnt["dv"])
            for u in range(4):
                kd = 4 * grp + u
                P.do("tensor", lambda e, kd=kd, u=u, grp=grp: e.transpose(
                    pb[grp][:, 128 * u: 128 * u + 128],
                    xb[:, 128 * kd: 128 * kd + 128], ident[:, :]),
                    inc="pe" if u == 3 else None)
            P.w("vector", "pe", P.cnt["pe"])
            for u in range(4):
                kd = 4 * grp + u
                P.do("vector", lambda e, kd=kd, u=u, grp=grp, c=c, itl=itl:
                     e.tensor_copy(
                         xh[:, 4096 * c + 512 * kd + 128 * itl:
                            4096 * c + 512 * kd + 128 * itl + 128],
                         pb[grp][:, 128 * u: 128 * u + 128]),
                     inc="dv" if u == 3 else None)
        P.w("tensor", "dv", P.cnt["dv"])
        for grp in range(2):
            psx = pf[2 + grp]
            for u in range(4):
                kd = 4 * grp + u
                P.do("tensor", lambda e, kd=kd, u=u, psx=psx, i=i: e.transpose(
                    psx[:, 128 * u: 128 * u + 128],
                    xin[i % 2][:, 128 * kd: 128 * kd + 128], identf[:, :]),
                    inc="pe" if u == 3 else None)
        P.w("vector", "pe", P.cnt["pe"])
        for grp in range(2):
            psx = pf[2 + grp]
            for u in range(4):
                kd = 4 * grp + u
                P.do("vector", lambda e, kd=kd, u=u, psx=psx: e.tensor_copy(
                    xtf[:, 128 * kd: 128 * kd + 128], psx[:, 128 * u: 128 * u + 128]),
                    inc="dv" if (grp, u) == (1, 3) else None)
        P.w("tensor", "dv", P.cnt["dv"])
        if i > 0:
            P.w("tensor", "ac", P.cnt["ac"])
        for kd in range(8):
            P.do("tensor", lambda e, kd=kd: e.matmul(
                pf[4][:, 0:E1], xtf[:, 128 * kd: 128 * kd + 128],
                rwf_s[:, E1 * kd: E1 * kd + E1],
                start=(kd == 0), stop=(kd == 7)), inc="pe" if kd == 7 else None)
        P.w("scalar", "pe", P.cnt["pe"])
        P.do("scalar", lambda e, i=i: e.copy(lgtA[:, E1 * i: E1 * i + E1],
                                             pf[4][:, 0:E1]), inc="ac")

    prol_pe1 = P.cnt["pe"]
    prol_dv1 = P.cnt["dv"]
    prol_ac1 = P.cnt["ac"]

    emit_top2(P, E1, lgtA, rbb, lgt, cmb, tks, mskd)

    def load_slice(P, dstbuf, e_, col0, sem):
        src = bass.AP(w1_i, e_ * 1024 * 4096 + col0,
                      [[4096, 128], [128 * 4096, 8], [1, 512]])
        dst = bass.AP(dstbuf, 0, [[8 * 512, 128], [512, 8], [1, 512]])
        P.do("gpsimd", lambda e, src=src, dst=dst: e.dma_start(out=dst, in_=src),
             inc=sem, n=16)

    def load_w2_slice(P, dstbuf, e_, row0, sem):
        src = bass.AP(w2_i, e_ * 2048 * 1024 + row0 * 1024,
                      [[1024, 128], [128 * 1024, 4], [1, 1024]])
        dst = bass.AP(dstbuf, 0, [[4 * 1024, 128], [1024, 4], [1, 1024]])
        P.do("gpsimd", lambda e, src=src, dst=dst: e.dma_start(out=dst, in_=src),
             inc=sem, n=16)

    steps = [(g, e_, j) for g in range(NG) for e_ in range(E1) for j in range(4)]
    pe_guard = {0: 0, 1: 0}

    def load_wag(sidx):
        g, e_, j = steps[sidx]
        par = sidx % 3
        sem = "dw%d" % par
        P.w("gpsimd", "pe", pe_guard.get(sidx, 0))
        load_slice(P, wa[par], e_, 512 * j, sem)
        load_slice(P, wg[par], e_, 2048 + 512 * j, sem)
        asrc = bass.AP(b1_i, 4096 * e_ + 512 * j, [[512, 1], [1, 512]])
        gsrc = bass.AP(b1_i, 4096 * e_ + 2048 + 512 * j, [[512, 1], [1, 512]])
        P.do("gpsimd", lambda e, asrc=asrc, par=par: e.dma_start(
            out=bja[par][:, :], in_=asrc), inc=sem, n=16)
        P.do("gpsimd", lambda e, gsrc=gsrc, par=par: e.dma_start(
            out=bjg[par][:, :], in_=gsrc), inc=sem, n=16)
        return (sem, P.cnt[sem])

    def load_wdb(sidx, guard):
        g, e_, j = steps[sidx]
        par2 = sidx % 2
        P.w("gpsimd", "pe", guard)
        load_w2_slice(P, wd[par2], e_, 512 * j, "dwd")
        if j == 3:
            b2src = bass.AP(b2_i, 1024 * e_, [[1024, 1], [1, 1024]])
            P.do("gpsimd", lambda e, b2src=b2src, par2=par2: e.dma_start(
                out=b2e[par2][:, :], in_=b2src), inc="dwd", n=16)
        return ("dwd", P.cnt["dwd"])

    dg_done = {0: load_wag(0), 1: load_wag(1)}
    wdb_done = {0: load_wdb(0, 0)}
    evac_dv = {}
    silu_ac = {}
    dn_cnt = {}
    epi_state = {}
    for sidx, (g, e_, j) in enumerate(steps):
        par = sidx % 3
        par2 = sidx % 2
        cur_pe = P.cnt["pe"]
        if sidx + 1 < len(steps):
            wdb_done[sidx + 1] = load_wdb(sidx + 1, cur_pe)
        if sidx + 2 < len(steps):
            pe_guard[sidx + 2] = cur_pe
            dg_done[sidx + 2] = load_wag(sidx + 2)
        P.w("tensor", *dg_done[sidx])
        for hb in range(4):
            for cc in range(2):
                c = 2 * g + cc
                q = 2 * hb + cc
                pA, pB = (pf[0], pf[1]) if q % 2 == 0 else (pf[2], pf[3])
                sgq = sg if q % 2 == 0 else sg2
                key = (sidx, q)
                prev = (sidx, q - 2) if q >= 2 else (sidx - 1, q + 6)
                if prev in evac_dv:
                    P.w("tensor", "dv", evac_dv[prev])
                elif sidx > 0:
                    P.w("tensor", "dv", P.cnt["dv"])
                elif q == 1:
                    P.w("tensor", "dv", prol_dv1)
                if q == 1 and (sidx - 1, 7) in dn_cnt:
                    P.w("tensor", *dn_cnt[(sidx - 1, 7)])
                for kd in range(8):
                    P.do("tensor", lambda e, c=c, hb=hb, kd=kd, par=par, pA=pA:
                         e.matmul(
                             pA[:, :],
                             wa[par][:, 512 * kd + 128 * hb: 512 * kd + 128 * hb + 128],
                             xh[:, 4096 * c + 512 * kd: 4096 * c + 512 * kd + 512],
                             start=(kd == 0), stop=False))
                P.do("tensor", lambda e, hb=hb, par=par, pA=pA: e.matmul(
                    pA[:, :], bja[par][0:1, 128 * hb: 128 * hb + 128],
                    onesb[0:1, :], start=False, stop=True))
                for kd in range(8):
                    P.do("tensor", lambda e, c=c, hb=hb, kd=kd, par=par, pB=pB:
                         e.matmul(
                             pB[:, :],
                             wg[par][:, 512 * kd + 128 * hb: 512 * kd + 128 * hb + 128],
                             xh[:, 4096 * c + 512 * kd: 4096 * c + 512 * kd + 512],
                             start=(kd == 0), stop=False))
                P.do("tensor", lambda e, hb=hb, par=par, pB=pB: e.matmul(
                    pB[:, :], bjg[par][0:1, 128 * hb: 128 * hb + 128],
                    onesb[0:1, :], start=False, stop=True), inc="pe")
                pe_q = P.cnt["pe"]
                P.w("scalar", "pe", pe_q)
                if prev in evac_dv:
                    P.w("scalar", "dv", evac_dv[prev])
                P.do("scalar", lambda e, pA=pA, sgq=sgq: e.activation(
                    sgq[:, :], pA[:, :], AF.Silu), inc="ac")
                silu_ac[key] = P.cnt["ac"]
                P.w("vector", "ac", silu_ac[key])
                P.do("vector", lambda e, hb=hb, cc=cc, pB=pB, sgq=sgq:
                     e.tensor_tensor(
                         act_s[:, 512 * (2 * hb + cc): 512 * (2 * hb + cc) + 512],
                         pB[:, :], sgq[:, :], ALU.mult), inc="dv")
                evac_dv[key] = P.cnt["dv"]
        P.w("tensor", "dv", P.cnt["dv"])
        P.w("tensor", *wdb_done[sidx])
        if sidx == 0:
            P.w("tensor", "ac", prol_ac1)
        if g == 1 and e_ == 0 and j == 3:
            P.w("vector", "st", epi_state["st"])
        for t8 in range(GT):
            cc, itl = t8 // 4, t8 % 4
            i = 8 * g + t8
            pD = (pf[4], pf[5]) if t8 % 2 == 0 else (pf[2], pf[3])
            kprev = (sidx, t8 - 2) if t8 >= 2 else (sidx - 1, t8 + 6)
            if kprev in dn_cnt:
                P.w("tensor", *dn_cnt[kprev])
            last_stop = (j != 3)
            for hb in range(4):
                for nh in range(2):
                    P.do("tensor", lambda e, hb=hb, nh=nh, cc=cc, itl=itl,
                         par2=par2, last_stop=last_stop, pD=pD: e.matmul(
                             pD[nh][:, :],
                             act_s[:, 512 * (2 * hb + cc) + 128 * itl:
                                   512 * (2 * hb + cc) + 128 * itl + 128],
                             wd[par2][:, 1024 * hb + 512 * nh: 1024 * hb + 512 * nh + 512],
                             start=(hb == 0), stop=(last_stop and hb == 3)),
                         inc="pe" if (j != 3 and (hb, nh) == (3, 1)) else None)
            if j == 3:
                for nh in range(2):
                    P.do("tensor", lambda e, nh=nh, par2=par2, pD=pD: e.matmul(
                        pD[nh][:, :], onesb[0:1, 0:128],
                        b2e[par2][0:1, 512 * nh: 512 * nh + 512],
                        start=False, stop=True), inc="pe" if nh == 1 else None)
            pe_dn = P.cnt["pe"]
            if j == 0:
                P.w("scalar", "pe", pe_dn)
                P.w("scalar", "dv", P.cnt["dv"])
                for nh in range(2):
                    P.do("scalar", lambda e, t8=t8, nh=nh, pD=pD: e.copy(
                        dacc[:, 1024 * t8 + 512 * nh: 1024 * t8 + 512 * nh + 512],
                        pD[nh][:, :]), inc="ac" if nh == 1 else None)
                dn_cnt[(sidx, t8)] = ("ac", P.cnt["ac"])
            elif j < 3:
                P.w("vector", "pe", pe_dn)
                for nh in range(2):
                    P.do("vector", lambda e, t8=t8, nh=nh, pD=pD: e.tensor_tensor(
                        dacc[:, 1024 * t8 + 512 * nh: 1024 * t8 + 512 * nh + 512],
                        dacc[:, 1024 * t8 + 512 * nh: 1024 * t8 + 512 * nh + 512],
                        pD[nh][:, :], ALU.add), inc="dv" if nh == 1 else None)
                dn_cnt[(sidx, t8)] = ("dv", P.cnt["dv"])
            else:
                P.w("vector", "pe", pe_dn)
                for nh in range(2):
                    P.do("vector", lambda e, t8=t8, nh=nh, pD=pD: e.tensor_tensor(
                        dacc[:, 1024 * t8 + 512 * nh: 1024 * t8 + 512 * nh + 512],
                        dacc[:, 1024 * t8 + 512 * nh: 1024 * t8 + 512 * nh + 512],
                        pD[nh][:, :], ALU.add), inc="dv" if nh == 1 else None)
                dn_cnt[(sidx, t8)] = ("dv", P.cnt["dv"])
                P.w("scalar", "dv", P.cnt["dv"])
                tm = tmpe if t8 % 2 == 0 else tmpe2
                for nh in range(2):
                    P.do("scalar", lambda e, t8=t8, i=i, nh=nh, e_=e_, tm=tm:
                         e.activation(
                             tm[:, 512 * nh: 512 * nh + 512],
                             dacc[:, 1024 * t8 + 512 * nh: 1024 * t8 + 512 * nh + 512],
                             AF.Identity,
                             scale=cmb[:, E1 * i + e_: E1 * i + e_ + 1]),
                         inc="ac" if nh == 1 else None)
                P.w("vector", "ac", P.cnt["ac"])
                if e_ == 0:
                    for nh in range(2):
                        P.do("vector", lambda e, t8=t8, nh=nh, tm=tm: e.tensor_copy(
                            acc[:, 1024 * t8 + 512 * nh: 1024 * t8 + 512 * nh + 512],
                            tm[:, 512 * nh: 512 * nh + 512]), inc="dv" if nh == 1 else None)
                else:
                    for nh in range(2):
                        P.do("vector", lambda e, t8=t8, nh=nh, tm=tm: e.tensor_tensor(
                            acc[:, 1024 * t8 + 512 * nh: 1024 * t8 + 512 * nh + 512],
                            acc[:, 1024 * t8 + 512 * nh: 1024 * t8 + 512 * nh + 512],
                            tm[:, 512 * nh: 512 * nh + 512], ALU.add),
                            inc="dv" if nh == 1 else None)
        if e_ == E1 - 1 and j == 3:
            if g == 0:
                P.w("sync", "pe", prol_pe1)
                P.w("sync", "dv", prol_dv1)
            else:
                P.w("sync", "dv", epi_state["dv"])
            dl_g = {}
            i0 = 8 * g
            P.do("sync", lambda e, i0=i0: e.dma_start(
                out=xin[0][:, :], in_=x3_i.ap()[128 * i0: 128 * i0 + 128, :]),
                inc="dx0", n=16)
            dl_g[0] = ("dx0", P.cnt["dx0"])
            for t8 in range(GT):
                i = i0 + t8
                if t8 + 1 < GT:
                    P.w("sync", "dv", P.cnt["dv"])
                    sem = "dx%d" % ((t8 + 1) % 2)
                    P.do("sync", lambda e, i=i, t8=t8: e.dma_start(
                        out=xin[(t8 + 1) % 2][:, :],
                        in_=x3_i.ap()[128 * (i + 1): 128 * (i + 1) + 128, :]),
                        inc=sem, n=16)
                    dl_g[t8 + 1] = (sem, P.cnt[sem])
                P.w("vector", *dl_g[t8])
                P.do("vector", lambda e, t8=t8: e.tensor_tensor(
                    acc[:, 1024 * t8: 1024 * t8 + 1024],
                    acc[:, 1024 * t8: 1024 * t8 + 1024],
                    xin[t8 % 2][:, :], ALU.add), inc="dv")
            for half in range(2):
                for q4 in range(4):
                    t8 = 4 * half + q4
                    P.do("vector", lambda e, t8=t8, q4=q4: e.reduce_sum(
                        lst[:, q4: q4 + 1], acc[:, 1024 * t8: 1024 * t8 + 1024],
                        axis=AX.X))
                for q4 in range(4):
                    P.do("vector", lambda e, q4=q4: e.tensor_scalar(
                        lst[:, q4: q4 + 1], lst[:, q4: q4 + 1], 1.0 / 1024, None,
                        ALU.mult), inc="dv" if q4 == 3 else None)
                P.w("scalar", "dv", P.cnt["dv"])
                for q4 in range(4):
                    t8 = 4 * half + q4
                    P.do("scalar", lambda e, t8=t8, q4=q4: e.activation(
                        junk[:, :], acc[:, 1024 * t8: 1024 * t8 + 1024], AF.Square,
                        accum_out=lst[:, 4 + q4: 5 + q4]), inc="ac" if q4 == 3 else None)
                P.w("vector", "ac", P.cnt["ac"])
                for q4 in range(4):
                    P.do("vector", lambda e, q4=q4: e.tensor_tensor(
                        lst[:, 8 + q4: 9 + q4], lst[:, q4: q4 + 1],
                        lst[:, q4: q4 + 1], ALU.mult))
                for q4 in range(4):
                    P.do("vector", lambda e, q4=q4: e.tensor_scalar(
                        lst[:, 12 + q4: 13 + q4], lst[:, 4 + q4: 5 + q4],
                        1.0 / 1024, None, ALU.mult))
                for q4 in range(4):
                    P.do("vector", lambda e, q4=q4: e.tensor_tensor(
                        lst[:, 16 + q4: 17 + q4], lst[:, 12 + q4: 13 + q4],
                        lst[:, 8 + q4: 9 + q4], ALU.subtract))
                for q4 in range(4):
                    P.do("vector", lambda e, q4=q4: e.tensor_scalar(
                        lst[:, 16 + q4: 17 + q4], lst[:, 16 + q4: 17 + q4], EPS,
                        None, ALU.add), inc="dv" if q4 == 3 else None)
                P.w("scalar", "dv", P.cnt["dv"])
                for q4 in range(4):
                    P.do("scalar", lambda e, q4=q4: e.activation(
                        lst[:, 20 + q4: 21 + q4], lst[:, 16 + q4: 17 + q4],
                        AF.Sqrt), inc="ac" if q4 == 3 else None)
                P.w("vector", "ac", P.cnt["ac"])
                for q4 in range(4):
                    P.do("vector", lambda e, q4=q4: e.reciprocal(
                        lst[:, 24 + q4: 25 + q4], lst[:, 20 + q4: 21 + q4]))
                for q4 in range(4):
                    t8 = 4 * half + q4
                    P.do("vector", lambda e, t8=t8, q4=q4: e.tensor_scalar(
                        acc[:, 1024 * t8: 1024 * t8 + 1024],
                        acc[:, 1024 * t8: 1024 * t8 + 1024],
                        lst[:, q4: q4 + 1], None, ALU.subtract))
                for q4 in range(4):
                    t8 = 4 * half + q4
                    P.do("vector", lambda e, t8=t8, q4=q4: e.tensor_scalar(
                        acc[:, 1024 * t8: 1024 * t8 + 1024],
                        acc[:, 1024 * t8: 1024 * t8 + 1024],
                        lst[:, 24 + q4: 25 + q4], None, ALU.mult),
                        inc="dv" if q4 == 3 else None)
            P.wcur("sync", "dv")
            for t8 in range(GT):
                i = i0 + t8
                P.do("sync", lambda e, i=i, t8=t8: e.dma_start(
                    out=y_o.ap()[128 * i: 128 * i + 128, :],
                    in_=acc[:, 1024 * t8: 1024 * t8 + 1024]), inc="st", n=16)
            epi_state["dv"] = P.cnt["dv"]
            epi_state["st"] = P.cnt["st"]
    P.wcur("sync", "st")


def build_launch3b():
    P = Prog()
    nc = P.nc
    P.sem_names("dl", "dg", "pe", "dv", "ac", "po", "st", "bar",
                "dx0", "dx1", "dc0", "dc1", "dw0", "dw1", "dw2", "dwd")
    E = dict(
        x3=nc.dram_tensor("x3", [T, 1024], F32, kind="ExternalInput"),
        wr=nc.dram_tensor("wr", [1024, 8], F32, kind="ExternalInput"),
        br=nc.dram_tensor("br", [1, 8], F32, kind="ExternalInput"),
        w1=nc.dram_tensor("w1", [8, 1024, 4096], F32, kind="ExternalInput"),
        b1=nc.dram_tensor("b1", [8, 4096], F32, kind="ExternalInput"),
        w2=nc.dram_tensor("w2", [8, 2048, 1024], F32, kind="ExternalInput"),
        b2=nc.dram_tensor("b2", [8, 1024], F32, kind="ExternalInput"),
        y=nc.dram_tensor("y", [T, 1024], F32, kind="ExternalOutput"))
    pf = [_ps(nc, "pf%d" % i, [128, 512], F32) for i in range(6)]
    pb = [_ps(nc, "pb%d" % i, [128, 1024], BF16) for i in range(2)]
    phase_l1ffn(P, pf, pb, E)
    return P.build(None)


def _phase_scope(P, tag, fn, pf, pb, E):
    global _CUR_STACK, _CUR_PREFIX
    import contextlib
    with contextlib.ExitStack() as stk:
        _CUR_STACK, _CUR_PREFIX = stk, tag + "_"
        fn(P, pf, pb, E)
        _CUR_STACK, _CUR_PREFIX = None, ""
    P.wcur("sync", "st")
    P.barrier()


def build_launchB():
    P = Prog()
    nc = P.nc
    P.sem_names("dl", "dg", "pe", "dv", "ac", "po", "st", "bar",
                "dx0", "dx1", "dc0", "dc1", "dw0", "dw1", "dw2", "dwd")
    E = dict(
        qt=nc.dram_tensor("qt", [T, 512], BF16, kind="ExternalInput"),
        kt=nc.dram_tensor("kt", [T, 512], BF16, kind="ExternalInput"),
        va=nc.dram_tensor("va", [T, 1040], BF16, kind="ExternalInput"),
        ds0p=nc.dram_tensor("ds0p", [3, 128, H * 2 * 65], F32, kind="ExternalInput"),
        x=nc.dram_tensor("x", [T, 1024], F32, kind="ExternalInput"),
        wo=nc.dram_tensor("wo", [1024, 1024], F32, kind="ExternalInput"),
        rw=nc.dram_tensor("rw", [1024, 16], F32, kind="ExternalInput"),
        rb=nc.dram_tensor("rb", [1, 16], F32, kind="ExternalInput"),
        up=nc.dram_tensor("up", [16, 1024, 512], F32, kind="ExternalInput"),
        ga=nc.dram_tensor("ga", [16, 1024, 512], F32, kind="ExternalInput"),
        dw=nc.dram_tensor("dw", [16, 512, 1024], F32, kind="ExternalInput"),
        shg=nc.dram_tensor("shg", [1024, 1024], F32, kind="ExternalInput"),
        shu=nc.dram_tensor("shu", [1024, 1024], F32, kind="ExternalInput"),
        shd=nc.dram_tensor("shd", [1024, 1024], F32, kind="ExternalInput"),
        wq1=nc.dram_tensor("wq1", [1024, 512], F32, kind="ExternalInput"),
        wk1=nc.dram_tensor("wk1", [1024, 512], F32, kind="ExternalInput"),
        wv1=nc.dram_tensor("wv1", [1024, 512], F32, kind="ExternalInput"),
        x1=nc.dram_tensor("x1", [T, 1024], F32),
        x2=nc.dram_tensor("x2", [T, 1024], F32, kind="ExternalOutput"),
        q1f=nc.dram_tensor("q1f", [512, T], BF16, kind="ExternalOutput"),
        k1f=nc.dram_tensor("k1f", [512, T], BF16, kind="ExternalOutput"),
        k1t=nc.dram_tensor("k1t", [T, 512], BF16, kind="ExternalOutput"),
        v1t=nc.dram_tensor("v1t", [T, 512], BF16, kind="ExternalOutput"),
        ds1=nc.dram_tensor("ds1", [512, 512], F32, kind="ExternalOutput"))
    pf = [_ps(nc, "pf%d" % i, [128, 512], F32) for i in range(6)]
    pb = [_ps(nc, "pb%d" % i, [128, 1024], BF16) for i in range(2)]
    _phase_scope(P, "a", phase_l0attn, pf, pb, E)
    _phase_scope(P, "b", phase_l0ffn, pf, pb, E)
    _phase_scope(P, "c", phase_l1qkv, pf, pb, E)
    return P.build(None)


def build_launchC():
    P = Prog()
    nc = P.nc
    P.sem_names("dl", "dg", "pe", "dv", "ac", "po", "st", "bar",
                "dx0", "dx1", "dc0", "dc1", "dw0", "dw1", "dw2", "dwd")
    E = dict(
        x2=nc.dram_tensor("x2", [T, 1024], F32, kind="ExternalInput"),
        q1f=nc.dram_tensor("q1f", [512, T], BF16, kind="ExternalInput"),
        k1f=nc.dram_tensor("k1f", [512, T], BF16, kind="ExternalInput"),
        k1t=nc.dram_tensor("k1t", [T, 512], BF16, kind="ExternalInput"),
        v1t=nc.dram_tensor("v1t", [T, 512], BF16, kind="ExternalInput"),
        ds1p=nc.dram_tensor("ds1p", [3, 512, 512], F32, kind="ExternalInput"),
        wo1=nc.dram_tensor("wo1", [512, 1024], F32, kind="ExternalInput"),
        wr=nc.dram_tensor("wr", [1024, 8], F32, kind="ExternalInput"),
        br=nc.dram_tensor("br", [1, 8], F32, kind="ExternalInput"),
        w1=nc.dram_tensor("w1", [8, 1024, 4096], F32, kind="ExternalInput"),
        b1=nc.dram_tensor("b1", [8, 4096], F32, kind="ExternalInput"),
        w2=nc.dram_tensor("w2", [8, 2048, 1024], F32, kind="ExternalInput"),
        b2=nc.dram_tensor("b2", [8, 1024], F32, kind="ExternalInput"),
        x3=nc.dram_tensor("x3", [T, 1024], F32),
        y=nc.dram_tensor("y", [T, 1024], F32, kind="ExternalOutput"))
    pf = [_ps(nc, "pf%d" % i, [128, 512], F32) for i in range(6)]
    pb = [_ps(nc, "pb%d" % i, [128, 1024], BF16) for i in range(2)]
    _phase_scope(P, "a", phase_l1attn, pf, pb, E)
    _phase_scope(P, "b", phase_l1ffn, pf, pb, E)
    return P.build(None)


def _pad_qk_weight(w):
    out = np.zeros((1024, H * HP), dtype=np.float32)
    for h in range(H):
        out[:, HP * h: HP * h + FD] = w[FD * h: FD * h + FD, :].T
    return np.ascontiguousarray(out)


def kernel(**inp):
    x = np.asarray(inp["L0_attn_x"])
    n_cores = 8
    cores = list(range(n_cores))

    def A(v):
        return np.ascontiguousarray(np.asarray(v))

    wq = _pad_qk_weight(np.asarray(inp["L0_attn_Wq"]))
    wk = _pad_qk_weight(np.asarray(inp["L0_attn_Wk"]))
    wv = A(np.asarray(inp["L0_attn_Wv"]).T)
    gam = A(np.asarray(inp["L0_attn_gamma"]).reshape(1, FD))
    bet = A(np.asarray(inp["L0_attn_beta"]).reshape(1, FD))
    in1 = []
    for c in cores:
        b, s = c // 4, c % 4
        in1.append(dict(xT=A(x[b, T * s: T * s + T, :].T),
                        wq=wq, wk=wk, wv=wv, gam=gam, bet=bet))
    r1 = run_bass_kernel_spmd(build_launch1(), in1, core_ids=cores).results

    wo = A(np.asarray(inp["L0_attn_Wo"]).T)
    ds0 = [np.asarray(r1[c]["ds0"]) for c in cores]
    rw = A(np.asarray(inp["L0_ffn_router_w"]).T)
    rb = A(np.asarray(inp["L0_ffn_router_bias"]).reshape(1, 16))
    up = A(inp["L0_ffn_up_proj"])
    ga = A(inp["L0_ffn_gate_proj"])
    dw = A(inp["L0_ffn_down_proj"])
    shg = A(np.asarray(inp["L0_ffn_sh_gate"]).T)
    shu = A(np.asarray(inp["L0_ffn_sh_up"]).T)
    shd = A(np.asarray(inp["L0_ffn_sh_down"]).T)
    wq1 = A(np.asarray(inp["L1_attn_Wq"]).T)
    wk1 = A(np.asarray(inp["L1_attn_Wk"]).T)
    wv1 = A(np.asarray(inp["L1_attn_Wv"]).T)
    inB = []
    for c in cores:
        b, s = c // 4, c % 4
        inB.append(dict(qt=np.asarray(r1[c]["qt"]), kt=np.asarray(r1[c]["kt"]),
                        va=np.asarray(r1[c]["va"]), ds0p=_prefix_stack(ds0, c),
                        x=A(x[b, T * s: T * s + T, :]), wo=wo,
                        rw=rw, rb=rb, up=up, ga=ga, dw=dw, shg=shg, shu=shu,
                        shd=shd, wq1=wq1, wk1=wk1, wv1=wv1))
    rB = run_bass_kernel_spmd(build_launchB(), inB, core_ids=cores).results

    wo1 = A(np.asarray(inp["L1_attn_Wo"]).T)
    ds1 = [np.asarray(rB[c]["ds1"]) for c in cores]
    wr = A(inp["L1_ffn_Wr"])
    br = A(np.asarray(inp["L1_ffn_br"]).reshape(1, 8))
    w1 = A(inp["L1_ffn_W1"])
    b1 = A(inp["L1_ffn_b1"])
    w2 = A(inp["L1_ffn_W2"])
    b2 = A(inp["L1_ffn_b2"])
    inC = []
    for c in cores:
        inC.append(dict(
            x2=np.asarray(rB[c]["x2"]),
            q1f=np.asarray(rB[c]["q1f"]), k1f=np.asarray(rB[c]["k1f"]),
            k1t=np.asarray(rB[c]["k1t"]), v1t=np.asarray(rB[c]["v1t"]),
            ds1p=_prefix_stack(ds1, c), wo1=wo1, wr=wr, br=br, w1=w1, b1=b1,
            w2=w2, b2=b2))
    rC = run_bass_kernel_spmd(build_launchC(), inC, core_ids=cores).results

    out = np.zeros((2, 8192, 1024), dtype=np.float32)
    for c in cores:
        b, s = c // 4, c % 4
        out[b, T * s: T * s + T, :] = np.asarray(rC[c]["y"], np.float32)
    return out


def _dev_launch2c(inp, x2_all):
    nc = build_launch2c()
    wq1 = np.ascontiguousarray(np.asarray(inp["L1_attn_Wq"]).T)
    wk1 = np.ascontiguousarray(np.asarray(inp["L1_attn_Wk"]).T)
    wv1 = np.ascontiguousarray(np.asarray(inp["L1_attn_Wv"]).T)
    in_maps = [dict(x2=np.ascontiguousarray(x2_all[c]), wq1=wq1, wk1=wk1, wv1=wv1)
               for c in range(8)]
    return run_bass_kernel_spmd(nc, in_maps, core_ids=list(range(8))).results


def _dev_launch3a(inp, x2_all, r2c):
    nc = build_launch3a()
    wo1 = np.ascontiguousarray(np.asarray(inp["L1_attn_Wo"]).T)
    ds1 = [np.asarray(r2c[c]["ds1"]) for c in range(8)]
    in_maps = []
    for c in range(8):
        in_maps.append(dict(
            x2=np.ascontiguousarray(x2_all[c]),
            q1f=np.asarray(r2c[c]["q1f"]), k1f=np.asarray(r2c[c]["k1f"]),
            k1t=np.asarray(r2c[c]["k1t"]), v1t=np.asarray(r2c[c]["v1t"]),
            ds1p=_prefix_stack(ds1, c), wo1=wo1))
    return run_bass_kernel_spmd(nc, in_maps, core_ids=list(range(8))).results


def _dev_launch3b(inp, x3_all):
    nc = build_launch3b()
    wr = np.ascontiguousarray(np.asarray(inp["L1_ffn_Wr"]))
    br = np.ascontiguousarray(np.asarray(inp["L1_ffn_br"]).reshape(1, 8))
    w1 = np.ascontiguousarray(np.asarray(inp["L1_ffn_W1"]))
    b1 = np.ascontiguousarray(np.asarray(inp["L1_ffn_b1"]))
    w2 = np.ascontiguousarray(np.asarray(inp["L1_ffn_W2"]))
    b2 = np.ascontiguousarray(np.asarray(inp["L1_ffn_b2"]))
    in_maps = [dict(x3=np.ascontiguousarray(x3_all[c]), wr=wr, br=br,
                    w1=w1, b1=b1, w2=w2, b2=b2) for c in range(8)]
    return run_bass_kernel_spmd(nc, in_maps, core_ids=list(range(8))).results


def _prefix_stack(arrs, c, nslots=3):
    b, s = c // 4, c % 4
    preds = [arrs[4 * b + j] for j in range(s)]
    shape = arrs[0].shape
    out = np.zeros((nslots,) + shape, dtype=arrs[0].dtype)
    for j, a in enumerate(preds):
        out[j] = a
    return np.ascontiguousarray(out)


def _dev_launch2a(inp, res1):
    x = np.asarray(inp["L0_attn_x"])
    wo = np.ascontiguousarray(np.asarray(inp["L0_attn_Wo"]).T)
    ds0 = [np.asarray(res1[c]["ds0"]) for c in range(8)]
    nc2 = build_launch2a()
    in_maps = []
    for c in range(8):
        b, s = c // 4, c % 4
        in_maps.append(dict(
            qt=np.asarray(res1[c]["qt"]), kt=np.asarray(res1[c]["kt"]),
            va=np.asarray(res1[c]["va"]), ds0p=_prefix_stack(ds0, c),
            x=np.ascontiguousarray(x[b, T * s: T * s + T, :]), wo=wo))
    res = run_bass_kernel_spmd(nc2, in_maps, core_ids=list(range(8)))
    return res.results


def _dev_launch1(inp):
    x = np.asarray(inp["L0_attn_x"])
    wq = _pad_qk_weight(np.asarray(inp["L0_attn_Wq"]))
    wk = _pad_qk_weight(np.asarray(inp["L0_attn_Wk"]))
    wv = np.ascontiguousarray(np.asarray(inp["L0_attn_Wv"]).T)
    gam = np.ascontiguousarray(np.asarray(inp["L0_attn_gamma"]).reshape(1, FD))
    bet = np.ascontiguousarray(np.asarray(inp["L0_attn_beta"]).reshape(1, FD))
    nc1 = build_launch1()
    in_maps = []
    for c in range(8):
        b, s = c // 4, c % 4
        xT = np.ascontiguousarray(x[b, T * s: T * s + T, :].T)
        in_maps.append(dict(xT=xT, wq=wq, wk=wk, wv=wv, gam=gam, bet=bet))
    res = run_bass_kernel_spmd(nc1, in_maps, core_ids=list(range(8)))
    return res.results


# version 10 (final)
# speedup vs baseline: 1.0640x; 1.0640x over previous
import numpy as np
import concourse.bass as bass
import concourse.mybir as mybir
from concourse.bass_utils import run_bass_kernel_spmd

F32 = mybir.dt.float32
BF16 = mybir.dt.bfloat16
AF = mybir.ActivationFunctionType
ALU = mybir.AluOpType
AX = mybir.AxisListType

T = 2048
NT = T // 128
NCH = T // 512
H = 16
FD = 16
HP = 32
EPS = 1e-5

ENGS = ("sync", "tensor", "vector", "scalar", "gpsimd")


class Prog:

    def __init__(self):
        self.nc = bass.Bass("TRN2", target_bir_lowering=False)
        self.ops = {e: [] for e in ENGS}
        self.cnt = {}
        self._sem_names = []

    def sem_names(self, *names):
        self._sem_names = list(names)
        for n in names:
            self.cnt[n] = 0

    def inc(self, name, n=1):
        self.cnt[name] += n
        return self.cnt[name]

    def do(self, eng, fn, inc=None, n=1):
        if inc is None:
            self.ops[eng].append(lambda e, fn=fn: fn(e))
        else:
            self.inc(inc, n)
            self.ops[eng].append(
                lambda e, fn=fn, inc=inc, n=n: fn(e).then_inc(self.sems[inc], n))
        return self.cnt.get(inc)

    def w(self, eng, sem, val):
        self.ops[eng].append(lambda e, sem=sem, val=val: e.wait_ge(self.sems[sem], val))

    def wcur(self, eng, sem):
        self.w(eng, sem, self.cnt[sem])

    def barrier(self, engs=ENGS):
        base = self.cnt["bar"]
        k = len(engs)
        for e in engs:
            self.ops[e].append(lambda en: en.nop().then_inc(self.sems["bar"], 1))
        self.cnt["bar"] = base + k
        for e in engs:
            self.w(e, "bar", base + k)

    def build(self, ctxmgrs):
        import contextlib
        stack = contextlib.ExitStack()
        self.sems = {}
        for n in self._sem_names:
            self.sems[n] = stack.enter_context(self.nc.semaphore(n))
        block = stack.enter_context(self.nc.Block())
        for eng in ENGS:
            if not self.ops[eng]:
                continue
            def make(eng):
                def body(e):
                    for f in self.ops[eng]:
                        f(e)
                return body
            getattr(block, eng)(make(eng))
        stack.close()
        return self.nc


_CUR_STACK = None
_CUR_PREFIX = ""


def _sb(nc, name, shape, dtype):
    name = _CUR_PREFIX + name
    if _CUR_STACK is not None:
        return _CUR_STACK.enter_context(nc.sbuf_tensor(name, shape, dtype))
    return nc.alloc_sbuf_tensor(name, shape, dtype)


def _ps(nc, name, shape, dtype=F32):
    return nc.alloc_psum_tensor(name, shape, dtype)


def load_weight_cast(P, dst, src, rows, cols, inc="dg", dst_off=0, src_off=0,
                     dst_free=None):
    kk = rows // 128
    if dst_free is None:
        dst_free = dst.shape[1]
    src_ap = bass.AP(src.tensor if hasattr(src, "tensor") else src, src_off,
                     [[cols, 128], [128 * cols, kk], [1, cols]])
    dst_ap = bass.AP(dst, dst_off, [[dst_free, 128], [cols, kk], [1, cols]])
    return P.do("gpsimd", lambda e: e.dma_start(out=dst_ap, in_=src_ap), inc=inc, n=16)


def build_launch1():
    P = Prog()
    nc = P.nc
    P.sem_names("dl", "dg", "pe", "dv", "ac", "po", "st", "bar", "dx0", "dx1", "dc0", "dc1", "dw0", "dw1", "dw2", "dwd")

    xT = nc.dram_tensor("xT", [1024, T], F32, kind="ExternalInput")
    wq = nc.dram_tensor("wq", [1024, H * HP], F32, kind="ExternalInput")
    wk = nc.dram_tensor("wk", [1024, H * HP], F32, kind="ExternalInput")
    wv = nc.dram_tensor("wv", [1024, 1024], F32, kind="ExternalInput")
    gam = nc.dram_tensor("gam", [1, FD], F32, kind="ExternalInput")
    bet = nc.dram_tensor("bet", [1, FD], F32, kind="ExternalInput")

    qt_o = nc.dram_tensor("qt", [T, 512], BF16, kind="ExternalOutput")
    kt_o = nc.dram_tensor("kt", [T, 512], BF16, kind="ExternalOutput")
    va_o = nc.dram_tensor("va", [T, 1040], BF16, kind="ExternalOutput")
    ds0_o = nc.dram_tensor("ds0", [128, H * 2 * 65], F32, kind="ExternalOutput")

    xhat = _sb(nc, "xhat", [128, 8 * T], BF16)
    wq_s = _sb(nc, "wq_s", [128, 8 * 512], BF16)
    wk_s = _sb(nc, "wk_s", [128, 8 * 512], BF16)
    wv_s = _sb(nc, "wv_s", [128, 8 * 1024], BF16)
    qt_s = _sb(nc, "qt_s", [128, NT * 512], BF16)
    kt_s = _sb(nc, "kt_s", [128, NT * 512], BF16)
    va_s = _sb(nc, "va_s", [128, NT * 1040], BF16)
    ds0_s = _sb(nc, "ds0_s", [128, H * 2 * 65], F32)
    gb = {n: _sb(nc, "gb_" + n, [128, 512], F32) for n in ("qg", "qb", "kg", "kb")}
    grow = _sb(nc, "grow", [1, FD], F32)
    brow = _sb(nc, "brow", [1, FD], F32)
    ones_col = _sb(nc, "ones_col", [1, 128], F32)
    qraw = _sb(nc, "qraw", [128, 512], F32)
    kraw = _sb(nc, "kraw", [128, 512], F32)
    sq = _sb(nc, "sq", [128, 512], F32)
    stat = _sb(nc, "stat", [128, 16 * 10], F32)
    tmp = _sb(nc, "tmp", [128, 512], F32)
    tmpk = _sb(nc, "tmpk", [128, 512], F32)
    sqk = _sb(nc, "sqk", [128, 512], F32)
    phik = _sb(nc, "phik", [128, 2 * 256], BF16)

    pp = [_ps(nc, "pp%d" % i, [128, 512], F32) for i in range(8)]
    psS = [pp[0], pp[1], pp[2], pp[3]]

    load_weight_cast(P, xhat, xT.ap(), 1024, T)
    load_weight_cast(P, wq_s, wq.ap(), 1024, 512)
    load_weight_cast(P, wk_s, wk.ap(), 1024, 512)
    load_weight_cast(P, wv_s, wv.ap(), 1024, 1024)
    dg_all = P.cnt["dg"]
    P.do("sync", lambda e: e.dma_start(out=grow[:, :], in_=gam.ap()), inc="dl", n=16)
    P.do("sync", lambda e: e.dma_start(out=brow[:, :], in_=bet.ap()), inc="dl", n=16)
    dl_gb = P.cnt["dl"]

    for t in gb.values():
        P.do("gpsimd", lambda e, t=t: e.memset(t[:, :], 0.0))
    P.do("gpsimd", lambda e: e.memset(ones_col[:, :], 1.0))
    P.do("gpsimd", lambda e: e.memset(tmp[:, :], 0.0))
    P.do("gpsimd", lambda e: e.memset(tmpk[:, :], 0.0))
    P.do("gpsimd", lambda e: e.memset(va_s[:, :], 1.0), inc="po")
    po_init = P.cnt["po"]
    P.w("tensor", "dl", dl_gb)
    P.w("tensor", "po", po_init)
    P.do("tensor", lambda e: e.matmul(pp[0][:, 0:16], ones_col[0:1, :],
                                      grow[0:1, :], start=True, stop=True))
    P.do("tensor", lambda e: e.matmul(pp[1][:, 0:16], ones_col[0:1, :],
                                      brow[0:1, :], start=True, stop=True), inc="pe")
    pe_gb = P.cnt["pe"]
    P.w("vector", "pe", pe_gb)
    for h in range(H):
        P.do("vector", lambda e, h=h: e.tensor_copy(gb["kg"][:, HP * h:HP * h + FD],
                                                    pp[0][:, 0:16]))
        P.do("vector", lambda e, h=h: e.tensor_copy(gb["kb"][:, HP * h:HP * h + FD],
                                                    pp[1][:, 0:16]))
    P.do("vector", lambda e: e.tensor_scalar(gb["qg"][:, :], gb["kg"][:, :], 0.25,
                                             None, ALU.mult))
    P.do("vector", lambda e: e.tensor_scalar(gb["qb"][:, :], gb["kb"][:, :], 0.25,
                                             None, ALU.mult), inc="dv")
    P.barrier()

    SC = 16 * 8
    ac_tile = {}
    dvv_tile = {}
    for i in range(NT):
        psQ = pp[0] if i % 2 == 0 else pp[4]
        psK = pp[1] if i % 2 == 0 else pp[5]
        psV0 = pp[2] if i % 2 == 0 else pp[6]
        psV1 = pp[3] if i % 2 == 0 else pp[7]
        P.w("tensor", "dg", dg_all)
        if i - 2 in ac_tile:
            P.w("tensor", "ac", ac_tile[i - 2])
        elif i == 1:
            P.w("tensor", "dv", P.cnt["dv"])
        for k in range(8):
            P.do("tensor", lambda e, k=k, i=i, psQ=psQ: e.matmul(
                psQ[:, :], xhat[:, 2048 * k + 128 * i: 2048 * k + 128 * i + 128],
                wq_s[:, 512 * k: 512 * k + 512], start=(k == 0), stop=(k == 7)))
        for k in range(8):
            P.do("tensor", lambda e, k=k, i=i, psK=psK: e.matmul(
                psK[:, :], xhat[:, 2048 * k + 128 * i: 2048 * k + 128 * i + 128],
                wk_s[:, 512 * k: 512 * k + 512], start=(k == 0), stop=(k == 7)))
        for k in range(8):
            for nh in range(2):
                last = (k == 7 and nh == 1)
                pv = psV0 if nh == 0 else psV1
                P.do("tensor", lambda e, k=k, i=i, nh=nh, pv=pv: e.matmul(
                    pv[:, :],
                    xhat[:, 2048 * k + 128 * i: 2048 * k + 128 * i + 128],
                    wv_s[:, 1024 * k + 512 * nh: 1024 * k + 512 * nh + 512],
                    start=(k == 0), stop=(k == 7)),
                    inc="pe" if last else None)
        pe_proj = P.cnt["pe"]

        P.w("scalar", "pe", pe_proj)
        if i - 1 in dvv_tile:
            P.w("scalar", "dv", dvv_tile[i - 1])
        P.do("scalar", lambda e, psQ=psQ: e.copy(qraw[:, :], psQ[:, :]))
        P.do("scalar", lambda e, psK=psK: e.copy(kraw[:, :], psK[:, :]))
        for h in range(H):
            last = (h == H - 1)
            pv = psV0 if h < 8 else psV1
            off = 64 * h if h < 8 else 64 * h - 512
            P.do("scalar", lambda e, h=h, i=i, pv=pv, off=off: e.copy(
                va_s[:, 1040 * i + 65 * h: 1040 * i + 65 * h + 64],
                pv[:, off: off + 64]), inc="ac" if last else None)
        ac_tile[i] = P.cnt["ac"]
        ac_evac = P.cnt["ac"]
        P.w("vector", "ac", ac_evac)
        QK = ((qraw, 0, tmp, sq, "qg", "qb", qt_s),
              (kraw, 4, tmpk, sqk, "kg", "kb", kt_s))
        SCR2 = {0: 16 * 8, 4: 16 * 9}
        for h in range(H):
            for (raw, base, cen, sqb, gn, bn, dst) in QK:
                P.do("vector", lambda e, raw=raw, h=h, base=base: e.reduce_sum(
                    stat[:, 16 * base + h: 16 * base + h + 1],
                    raw[:, HP * h: HP * h + FD], axis=AX.X))
        for (raw, base, cen, sqb, gn, bn, dst) in QK:
            P.do("vector", lambda e, base=base: e.tensor_scalar(
                stat[:, 16 * base: 16 * base + 16],
                stat[:, 16 * base: 16 * base + 16], 1.0 / FD, None, ALU.mult))
        for h in range(H):
            for (raw, base, cen, sqb, gn, bn, dst) in QK:
                P.do("vector", lambda e, raw=raw, cen=cen, h=h, base=base:
                     e.tensor_scalar(
                         cen[:, HP * h: HP * h + FD], raw[:, HP * h: HP * h + FD],
                         stat[:, 16 * base + h: 16 * base + h + 1], None,
                         ALU.subtract))
        for (raw, base, cen, sqb, gn, bn, dst) in QK:
            P.do("vector", lambda e, cen=cen, sqb=sqb: e.tensor_tensor(
                sqb[:, :], cen[:, :], cen[:, :], ALU.mult))
        for h in range(H):
            for (raw, base, cen, sqb, gn, bn, dst) in QK:
                P.do("vector", lambda e, sqb=sqb, h=h, base=base: e.reduce_sum(
                    stat[:, 16 * (base + 2) + h: 16 * (base + 2) + h + 1],
                    sqb[:, HP * h: HP * h + FD], axis=AX.X))
        for (raw, base, cen, sqb, gn, bn, dst) in QK:
            P.do("vector", lambda e, base=base: e.tensor_scalar(
                stat[:, 16 * (base + 2): 16 * (base + 2) + 16],
                stat[:, 16 * (base + 2): 16 * (base + 2) + 16], 1.0 / FD, EPS,
                ALU.mult, ALU.add), inc="dv")
        dv_var = P.cnt["dv"]
        dvv_tile[i] = dv_var
        P.w("scalar", "dv", dv_var)
        for (raw, base, cen, sqb, gn, bn, dst) in QK:
            P.do("scalar", lambda e, base=base: e.activation(
                stat[:, SCR2[base]: SCR2[base] + 16],
                stat[:, 16 * (base + 2): 16 * (base + 2) + 16], AF.Sqrt),
                inc="ac")
        ac_sqrt = P.cnt["ac"]
        P.w("vector", "ac", ac_sqrt)
        for (raw, base, cen, sqb, gn, bn, dst) in QK:
            P.do("vector", lambda e, base=base: e.reciprocal(
                stat[:, 16 * (base + 3): 16 * (base + 3) + 16],
                stat[:, SCR2[base]: SCR2[base] + 16]))
        for h in range(H):
            for (raw, base, cen, sqb, gn, bn, dst) in QK:
                P.do("vector", lambda e, cen=cen, h=h, base=base: e.tensor_scalar(
                    cen[:, HP * h: HP * h + FD], cen[:, HP * h: HP * h + FD],
                    stat[:, 16 * (base + 3) + h: 16 * (base + 3) + h + 1], None,
                    ALU.mult))
        for (raw, base, cen, sqb, gn, bn, dst) in QK:
            P.do("vector", lambda e, cen=cen, sqb=sqb, gn=gn: e.tensor_tensor(
                sqb[:, :], cen[:, :], gb[gn][:, :], ALU.mult))
        for (raw, base, cen, sqb, gn, bn, dst) in QK:
            P.do("vector", lambda e, sqb=sqb, bn=bn, dst=dst, i=i: e.tensor_tensor(
                dst[:, 512 * i: 512 * i + 512], sqb[:, :], gb[bn][:, :],
                ALU.add), inc="dv")

    P.barrier()

    pe_phi = {}
    for h in range(H):
        pb = (h % 2) * 2
        if h >= 2:
            P.w("tensor", "dv", P.cnt["dv"])
        for i in range(NT):
            g = 16 * h + i
            pho = 256 * (g % 2)
            if g - 2 in pe_phi:
                P.w("vector", "pe", pe_phi[g - 2])
            P.do("vector", lambda e, h=h, i=i, pho=pho: e.tensor_tensor(
                phik[:, pho: pho + 256].rearrange("p (x y) -> p x y", x=16),
                bass.AP(kt_s, 512 * i + HP * h, [[NT * 512, 128], [0, 16], [1, 16]]),
                bass.AP(kt_s, 512 * i + HP * h, [[NT * 512, 128], [1, 16], [0, 16]]),
                ALU.mult), inc="dv")
            dv_phi = P.cnt["dv"]
            P.w("tensor", "dv", dv_phi)
            for half in range(2):
                P.do("tensor", lambda e, h=h, i=i, half=half, pb=pb, pho=pho: e.matmul(
                    psS[pb + half][:, 0:65],
                    phik[:, pho + 128 * half: pho + 128 * half + 128],
                    va_s[:, 1040 * i + 65 * h: 1040 * i + 65 * h + 65],
                    start=(i == 0), stop=(i == NT - 1)),
                    inc="pe" if half == 1 else None)
            pe_phi[g] = P.cnt["pe"]
        P.w("vector", "pe", pe_phi[16 * h + NT - 1])
        for half in range(2):
            P.do("vector", lambda e, h=h, half=half, pb=pb: e.tensor_copy(
                ds0_s[:, (2 * h + half) * 65: (2 * h + half) * 65 + 65],
                psS[pb + half][:, 0:65]), inc="dv")

    P.barrier()

    for i in range(NT):
        P.do("sync", lambda e, i=i: e.dma_start(
            out=qt_o.ap()[128 * i: 128 * i + 128, :],
            in_=qt_s[:, 512 * i: 512 * i + 512]), inc="st", n=16)
        P.do("sync", lambda e, i=i: e.dma_start(
            out=kt_o.ap()[128 * i: 128 * i + 128, :],
            in_=kt_s[:, 512 * i: 512 * i + 512]), inc="st", n=16)
        P.do("sync", lambda e, i=i: e.dma_start(
            out=va_o.ap()[128 * i: 128 * i + 128, :],
            in_=va_s[:, 1040 * i: 1040 * i + 1040]), inc="st", n=16)
    P.do("sync", lambda e: e.dma_start(out=ds0_o.ap(), in_=ds0_s[:, :]), inc="st", n=16)
    P.wcur("sync", "st")

    return P.build(None)


def phase_l0attn(P, pf, pb, E):
    nc = P.nc
    qt_i, kt_i, va_i, ds0p_i, x_i, wo_i, x1_o = (
        E["qt"], E["kt"], E["va"], E["ds0p"], E["x"], E["wo"], E["x1"])

    ident = _sb(nc, "ident", [128, 128], BF16)
    bigm = _sb(nc, "bigm", [128, 1024], F32)
    wo_s = _sb(nc, "wo_s", [128, 8 * 1024], BF16)
    S0 = _sb(nc, "S0", [128, H * 2 * 65], F32)
    S0b = _sb(nc, "S0b", [128, H * 2 * 65], BF16)
    stg = [_sb(nc, "stg%d" % i, [128, H * 2 * 65], F32) for i in range(3)]
    qtc = [_sb(nc, "qtc%d" % i, [128, 2048], BF16) for i in range(2)]
    ktc = [_sb(nc, "ktc%d" % i, [128, 2048], BF16) for i in range(2)]
    vac = [_sb(nc, "vac%d" % i, [128, 4 * 1040], BF16) for i in range(2)]
    qf = _sb(nc, "qf", [128, 2048], BF16)
    kf = _sb(nc, "kf", [128, 2048], BF16)
    sbf = [_sb(nc, "sbf%d" % i, [128, 512], BF16) for i in range(4)]
    sqs = _sb(nc, "sqs", [128, 512], F32)
    sqs2 = _sb(nc, "sqs2", [128, 512], F32)
    phiq4 = _sb(nc, "phiq4", [128, 4 * 256], BF16)
    phT4 = _sb(nc, "phT4", [128, 4 * 256], BF16)
    phik4 = _sb(nc, "phik4", [128, 4 * 256], BF16)
    och = _sb(nc, "och", [128, 4 * 1024], BF16)
    zc = _sb(nc, "zc", [128, 16], F32)
    zr = _sb(nc, "zr", [128, 16], F32)
    oT = _sb(nc, "oT", [128, 1024], BF16)
    xin = [_sb(nc, "xin%d" % i, [128, 1024], F32) for i in range(2)]
    x1_s = _sb(nc, "x1_s", [128, NT * 1024], F32)
    lst = _sb(nc, "lst", [128, 64], F32)
    junk = _sb(nc, "junk", [128, 1024], F32)

    load_weight_cast(P, wo_s, wo_i.ap(), 1024, 1024)
    dg_w = P.cnt["dg"]
    for r in range(3):
        P.do("sync", lambda e, r=r: e.dma_start(out=stg[r][:, :],
                                                in_=ds0p_i.ap()[r]), inc="dl", n=16)
    dl_st = P.cnt["dl"]
    P.do("gpsimd", lambda e: e.memset(ident[:], 0.0))
    P.do("gpsimd", lambda e: e.affine_select(
        out=ident[:], in_=ident[:], compare_op=ALU.not_equal, fill=1.0,
        base=0, pattern=[[-1, 128]], channel_multiplier=1))
    P.do("gpsimd", lambda e: e.memset(bigm[:, :], 1.0))
    P.do("gpsimd", lambda e: e.affine_select(
        out=bigm[:, :], in_=bigm[:, :], compare_op=ALU.is_ge, fill=0.0,
        base=-512, pattern=[[1, 1024]], channel_multiplier=-1), inc="po")
    P.w("vector", "dl", dl_st)
    P.do("vector", lambda e: e.tensor_add(S0[:, :], stg[0][:, :], stg[1][:, :]))
    P.do("vector", lambda e: e.tensor_copy(junk[:, 0:64], stg[0][:, 0:64]))
    P.do("vector", lambda e: e.tensor_add(S0[:, :], S0[:, :], stg[2][:, :]))
    P.do("vector", lambda e: e.tensor_copy(junk[:, 0:64], stg[0][:, 0:64]))
    P.do("vector", lambda e: e.tensor_copy(S0b[:, :], S0[:, :]), inc="dv")
    P.barrier()

    chunk_cnt = {}
    sq_ac_hist = []
    msk_dv_hist = []

    def load_chunk(c):
        buf = c % 2
        sem = "dc%d" % buf
        for jt in range(4):
            i = 4 * c + jt
            P.do("sync", lambda e, i=i, jt=jt, buf=buf: e.dma_start(
                out=qtc[buf][:, 512 * jt: 512 * jt + 512],
                in_=qt_i.ap()[128 * i: 128 * i + 128, :]), inc=sem, n=16)
            P.do("sync", lambda e, i=i, jt=jt, buf=buf: e.dma_start(
                out=ktc[buf][:, 512 * jt: 512 * jt + 512],
                in_=kt_i.ap()[128 * i: 128 * i + 128, :]), inc=sem, n=16)
            P.do("sync", lambda e, i=i, jt=jt, buf=buf: e.dma_start(
                out=vac[buf][:, 1040 * jt: 1040 * jt + 1040],
                in_=va_i.ap()[128 * i: 128 * i + 128, :]), inc=sem, n=16)
        chunk_cnt[c] = (sem, P.cnt[sem])

    xin_cnt = {}
    load_chunk(0)
    P.do("sync", lambda e: e.dma_start(out=xin[0][:, :],
                                       in_=x_i.ap()[0:128, :]), inc="dx0", n=16)
    xin_cnt[0] = ("dx0", P.cnt["dx0"])

    for c in range(NCH):
        buf = c % 2
        if c + 1 < NCH:
            if c >= 1:
                P.w("sync", "pe", P.cnt["pe"])
                P.w("sync", "dv", P.cnt["dv"])
            load_chunk(c + 1)

        P.w("tensor", *chunk_cnt[c])
        P.w("tensor", "dg", dg_w)
        if c > 0:
            P.w("tensor", "dv", P.cnt["dv"])
        gidx = 0
        for (src, dstf) in ((qtc[buf], qf), (ktc[buf], kf)):
            for grp in range(4):
                pbx = pb[gidx % 2]
                if gidx >= 2:
                    P.w("tensor", "dv", P.cnt["dv"])
                gidx += 1
                for u in range(4):
                    tr = 4 * grp + u
                    jt, ft = tr // 4, tr % 4
                    P.do("tensor", lambda e, src=src, jt=jt, ft=ft, u=u, pbx=pbx:
                         e.transpose(
                             pbx[:, 128 * u: 128 * u + 128],
                             src[:, 512 * jt + 128 * ft: 512 * jt + 128 * ft + 128],
                             ident[:, :]), inc="pe" if u == 3 else None)
                pe_tr = P.cnt["pe"]
                P.w("vector", "pe", pe_tr)
                for u in range(4):
                    tr = 4 * grp + u
                    jt, ft = tr // 4, tr % 4
                    P.do("vector", lambda e, dstf=dstf, jt=jt, ft=ft, u=u, pbx=pbx:
                         e.tensor_copy(
                             dstf[:, 512 * ft + 128 * jt: 512 * ft + 128 * jt + 128],
                             pbx[:, 128 * u: 128 * u + 128]),
                         inc="dv" if u == 3 else None)

        P.w("tensor", "dv", P.cnt["dv"])
        for h in range(H):
            ft, rb = h // 4, 32 * (h % 4)
            tp = (rb, 0) if rb == 96 else None
            for jt in range(4):
                psc = pf[jt % 2]
                if len(sq_ac_hist) >= 2:
                    P.w("tensor", "ac", sq_ac_hist[-2])
                def mm_sc(e, jt=jt, ft=ft, rb=rb, c=c, psc=psc, tp=tp):
                    lhsT = kf[rb: rb + 32, 512 * ft + 128 * jt: 512 * ft + 128 * jt + 128]
                    rhs = qf[rb: rb + 32, 512 * ft: 512 * ft + 512]
                    if tp is not None:
                        e.ldweights(lhsT, tile_position=tp)
                        return e.matmul(psc[:, :], lhsT, rhs, start=True, stop=True,
                                        tile_position=tp)
                    return e.matmul(psc[:, :], lhsT, rhs, start=True, stop=True)
                P.do("tensor", mm_sc, inc="pe")
                pe_sc = P.cnt["pe"]
                P.w("scalar", "pe", pe_sc)
                sq_buf = sqs if jt % 2 == 0 else sqs2
                if len(msk_dv_hist) >= 2:
                    P.w("scalar", "dv", msk_dv_hist[-2])
                P.do("scalar", lambda e, psc=psc, sq_buf=sq_buf: e.activation(
                    sq_buf[:, :], psc[:, :], AF.Square), inc="ac")
                sq_ac_hist.append(P.cnt["ac"])
                P.w("vector", "ac", P.cnt["ac"])
                P.do("vector", lambda e, jt=jt, sq_buf=sq_buf: e.tensor_tensor(
                    sbf[jt][:, :], sq_buf[:, :],
                    bigm[:, 512 - 128 * jt: 1024 - 128 * jt], ALU.mult),
                    inc="dv")
                msk_dv_hist.append(P.cnt["dv"])
            dv_sbf = P.cnt["dv"]
            P.w("tensor", "dv", dv_sbf)
            for it in range(4):
                P.do("vector", lambda e, it=it, h=h, buf=buf: e.tensor_tensor(
                    phiq4[:, 256 * it: 256 * it + 256]
                    .rearrange("p (x y) -> p x y", x=16),
                    bass.AP(qtc[buf], 512 * it + HP * h, [[2048, 128], [0, 16], [1, 16]]),
                    bass.AP(qtc[buf], 512 * it + HP * h, [[2048, 128], [1, 16], [0, 16]]),
                    ALU.mult), inc="dv" if it == 3 else None)
            P.w("tensor", "dv", P.cnt["dv"])
            for it in range(4):
                for half in range(2):
                    P.do("tensor", lambda e, it=it, half=half: e.transpose(
                        pb[half][:, 128 * it: 128 * it + 128],
                        phiq4[:, 256 * it + 128 * half: 256 * it + 128 * half + 128],
                        ident[:, :]), inc="pe" if (it, half) == (3, 1) else None)
            P.w("vector", "pe", P.cnt["pe"])
            for half in range(2):
                P.do("vector", lambda e, half=half: e.tensor_copy(
                    bass.AP(phT4, 128 * half, [[1024, 128], [256, 4], [1, 128]]),
                    pb[half][:, 0:512].rearrange("p (i x) -> p i x", i=4)),
                    inc="dv" if half == 1 else None)
            P.w("tensor", "dv", P.cnt["dv"])
            for it in range(4):
                pso = pf[2 + it]
                first = True
                for jt in range(it + 1):
                    P.do("tensor", lambda e, jt=jt, it=it, h=h, buf=buf, pso=pso,
                         first=first: e.matmul(
                             pso[:, 0:65], sbf[jt][:, 128 * it: 128 * it + 128],
                             vac[buf][:, 1040 * jt + 65 * h: 1040 * jt + 65 * h + 65],
                             start=first, stop=False))
                    first = False
                P.do("tensor", lambda e, h=h, it=it, pso=pso: e.matmul(
                    pso[:, 0:65], phT4[:, 256 * it: 256 * it + 128],
                    S0b[:, (2 * h) * 65: (2 * h) * 65 + 65], start=False, stop=False))
                P.do("tensor", lambda e, h=h, it=it, pso=pso: e.matmul(
                    pso[:, 0:65], phT4[:, 256 * it + 128: 256 * it + 256],
                    S0b[:, (2 * h + 1) * 65: (2 * h + 1) * 65 + 65],
                    start=False, stop=True), inc="pe")
            pe_o = P.cnt["pe"]
            P.w("vector", "pe", pe_o)
            for it in range(4):
                P.do("vector", lambda e, it=it: e.tensor_scalar(
                    zc[:, it: it + 1], pf[2 + it][:, 64:65], EPS, None, ALU.add))
            for it in range(4):
                P.do("vector", lambda e, it=it: e.reciprocal(
                    zr[:, it: it + 1], zc[:, it: it + 1]))
            for it in range(4):
                P.do("vector", lambda e, it=it, h=h: e.tensor_scalar(
                    och[:, 1024 * it + 64 * h: 1024 * it + 64 * h + 64],
                    pf[2 + it][:, 0:64], zr[:, it: it + 1], None, ALU.mult),
                    inc="dv" if it == 3 else None)
            P.w("tensor", "dv", P.cnt["dv"])
            for jt in range(4):
                P.do("vector", lambda e, jt=jt, h=h, buf=buf: e.tensor_tensor(
                    phik4[:, 256 * jt: 256 * jt + 256]
                    .rearrange("p (x y) -> p x y", x=16),
                    bass.AP(ktc[buf], 512 * jt + HP * h, [[2048, 128], [0, 16], [1, 16]]),
                    bass.AP(ktc[buf], 512 * jt + HP * h, [[2048, 128], [1, 16], [0, 16]]),
                    ALU.mult), inc="dv" if jt == 3 else None)
            P.w("tensor", "dv", P.cnt["dv"])
            for jt in range(4):
                for half in range(2):
                    P.do("tensor", lambda e, jt=jt, h=h, half=half, buf=buf: e.matmul(
                        pf[half][:, 0:65],
                        phik4[:, 256 * jt + 128 * half: 256 * jt + 128 * half + 128],
                        vac[buf][:, 1040 * jt + 65 * h: 1040 * jt + 65 * h + 65],
                        start=(jt == 0), stop=(jt == 3)),
                        inc="pe" if (jt, half) == (3, 1) else None)
            P.w("vector", "pe", P.cnt["pe"])
            for half in range(2):
                P.do("vector", lambda e, h=h, half=half: e.tensor_tensor(
                    S0[:, (2 * h + half) * 65: (2 * h + half) * 65 + 65],
                    S0[:, (2 * h + half) * 65: (2 * h + half) * 65 + 65],
                    pf[half][:, 0:65], ALU.add))
            for half in range(2):
                P.do("vector", lambda e, h=h, half=half: e.tensor_copy(
                    S0b[:, (2 * h + half) * 65: (2 * h + half) * 65 + 65],
                    S0[:, (2 * h + half) * 65: (2 * h + half) * 65 + 65]),
                    inc="dv" if half == 1 else None)
            P.w("tensor", "dv", P.cnt["dv"])

        for it in range(4):
            i = 4 * c + it
            if i + 1 < NT:
                P.w("sync", "dv", P.cnt["dv"])
                sem = "dx%d" % ((i + 1) % 2)
                P.do("sync", lambda e, i=i, sem=sem: e.dma_start(
                    out=xin[(i + 1) % 2][:, :],
                    in_=x_i.ap()[128 * (i + 1): 128 * (i + 1) + 128, :]),
                    inc=sem, n=16)
                xin_cnt[i + 1] = (sem, P.cnt[sem])
            P.w("tensor", "dv", P.cnt["dv"])
            for grp in range(2):
                pbx = pb[grp]
                for u in range(4):
                    kf8 = 4 * grp + u
                    P.do("tensor", lambda e, it=it, kf8=kf8, u=u, pbx=pbx: e.transpose(
                        pbx[:, 128 * u: 128 * u + 128],
                        och[:, 1024 * it + 128 * kf8: 1024 * it + 128 * kf8 + 128],
                        ident[:, :]), inc="pe" if u == 3 else None)
            pe_tr = P.cnt["pe"]
            P.w("vector", "pe", pe_tr)
            for grp in range(2):
                pbx = pb[grp]
                for u in range(4):
                    kf8 = 4 * grp + u
                    P.do("vector", lambda e, kf8=kf8, u=u, pbx=pbx: e.tensor_copy(
                        oT[:, 128 * kf8: 128 * kf8 + 128], pbx[:, 128 * u: 128 * u + 128]),
                        inc="dv" if (grp, u) == (1, 3) else None)
            P.w("tensor", "dv", P.cnt["dv"])
            for k8 in range(8):
                for nh in range(2):
                    P.do("tensor", lambda e, k8=k8, nh=nh: e.matmul(
                        pf[nh][:, :], oT[:, 128 * k8: 128 * k8 + 128],
                        wo_s[:, 1024 * k8 + 512 * nh: 1024 * k8 + 512 * nh + 512],
                        start=(k8 == 0), stop=(k8 == 7)),
                        inc="pe" if (k8, nh) == (7, 1) else None)
            pe_wo = P.cnt["pe"]
            P.w("vector", "pe", pe_wo)
            P.w("vector", *xin_cnt[i])
            for nh in range(2):
                P.do("vector", lambda e, i=i, nh=nh: e.tensor_tensor(
                    x1_s[:, 1024 * i + 512 * nh: 1024 * i + 512 * nh + 512],
                    xin[i % 2][:, 512 * nh: 512 * nh + 512],
                    pf[nh][:, :], ALU.add), inc="dv" if nh == 1 else None)
            P.w("tensor", "dv", P.cnt["dv"])

        lnbase = P.cnt["dv"]
        NLN = 4
        for it in range(4):
            i = 4 * c + it
            P.do("vector", lambda e, i=i, it=it: e.reduce_sum(
                lst[:, it: it + 1], x1_s[:, 1024 * i: 1024 * i + 1024], axis=AX.X))
        for it in range(4):
            P.do("vector", lambda e, it=it: e.tensor_scalar(
                lst[:, it: it + 1], lst[:, it: it + 1], 1.0 / 1024, None, ALU.mult),
                inc="dv" if it == 3 else None)
        P.w("scalar", "dv", P.cnt["dv"])
        for it in range(4):
            i = 4 * c + it
            P.do("scalar", lambda e, i=i, it=it: e.activation(
                junk[:, :], x1_s[:, 1024 * i: 1024 * i + 1024], AF.Square,
                accum_out=lst[:, 4 + it: 5 + it]), inc="ac" if it == 3 else None)
        P.w("vector", "ac", P.cnt["ac"])
        for it in range(4):
            P.do("vector", lambda e, it=it: e.tensor_tensor(
                lst[:, 8 + it: 9 + it], lst[:, it: it + 1], lst[:, it: it + 1],
                ALU.mult))
        for it in range(4):
            P.do("vector", lambda e, it=it: e.tensor_scalar(
                lst[:, 12 + it: 13 + it], lst[:, 4 + it: 5 + it], 1.0 / 1024, None,
                ALU.mult))
        for it in range(4):
            P.do("vector", lambda e, it=it: e.tensor_tensor(
                lst[:, 16 + it: 17 + it], lst[:, 12 + it: 13 + it],
                lst[:, 8 + it: 9 + it], ALU.subtract))
        for it in range(4):
            P.do("vector", lambda e, it=it: e.tensor_scalar(
                lst[:, 16 + it: 17 + it], lst[:, 16 + it: 17 + it], EPS, None,
                ALU.add), inc="dv" if it == 3 else None)
        P.w("scalar", "dv", P.cnt["dv"])
        for it in range(4):
            P.do("scalar", lambda e, it=it: e.activation(
                lst[:, 20 + it: 21 + it], lst[:, 16 + it: 17 + it], AF.Sqrt),
                inc="ac" if it == 3 else None)
        P.w("vector", "ac", P.cnt["ac"])
        for it in range(4):
            P.do("vector", lambda e, it=it: e.reciprocal(
                lst[:, 24 + it: 25 + it], lst[:, 20 + it: 21 + it]))
        for it in range(4):
            i = 4 * c + it
            P.do("vector", lambda e, i=i, it=it: e.tensor_scalar(
                x1_s[:, 1024 * i: 1024 * i + 1024],
                x1_s[:, 1024 * i: 1024 * i + 1024],
                lst[:, it: it + 1], None, ALU.subtract))
        for it in range(4):
            i = 4 * c + it
            P.do("vector", lambda e, i=i, it=it: e.tensor_scalar(
                x1_s[:, 1024 * i: 1024 * i + 1024],
                x1_s[:, 1024 * i: 1024 * i + 1024],
                lst[:, 24 + it: 25 + it], None, ALU.mult),
                inc="dv" if it == 3 else None)

    P.barrier()
    for i in range(NT):
        P.do("sync", lambda e, i=i: e.dma_start(
            out=x1_o.ap()[128 * i: 128 * i + 128, :],
            in_=x1_s[:, 1024 * i: 1024 * i + 1024]), inc="st", n=16)
    P.wcur("sync", "st")


def build_launch2a():
    P = Prog()
    nc = P.nc
    P.sem_names("dl", "dg", "pe", "dv", "ac", "po", "st", "bar",
                "dx0", "dx1", "dc0", "dc1", "dw0", "dw1", "dw2", "dwd")
    E = dict(
        qt=nc.dram_tensor("qt", [T, 512], BF16, kind="ExternalInput"),
        kt=nc.dram_tensor("kt", [T, 512], BF16, kind="ExternalInput"),
        va=nc.dram_tensor("va", [T, 1040], BF16, kind="ExternalInput"),
        ds0p=nc.dram_tensor("ds0p", [3, 128, H * 2 * 65], F32, kind="ExternalInput"),
        x=nc.dram_tensor("x", [T, 1024], F32, kind="ExternalInput"),
        wo=nc.dram_tensor("wo", [1024, 1024], F32, kind="ExternalInput"),
        x1=nc.dram_tensor("x1", [T, 1024], F32, kind="ExternalOutput"))
    pf = [_ps(nc, "pf%d" % i, [128, 512], F32) for i in range(6)]
    pb = [_ps(nc, "pb%d" % i, [128, 1024], BF16) for i in range(2)]
    phase_l0attn(P, pf, pb, E)
    return P.build(None)



def emit_top2(P, EW_, lgtA, rbb, lgt, cmb, tks, mskd):
    AFl, ALUl, AXl = AF, ALU, AX
    P.w("vector", "ac", P.cnt["ac"])
    for i in range(NT):
        P.do("vector", lambda e, i=i: e.tensor_tensor(
            lgt[:, EW_ * i: EW_ * i + EW_], lgtA[:, EW_ * i: EW_ * i + EW_],
            rbb[:, :], ALUl.add))
    for i in range(NT):
        P.do("vector", lambda e, i=i: e.reduce_max(
            tks[:, i: i + 1], lgt[:, EW_ * i: EW_ * i + EW_], axis=AXl.X))
    for i in range(NT):
        P.do("vector", lambda e, i=i: e.tensor_scalar(
            cmb[:, EW_ * i: EW_ * i + EW_], lgt[:, EW_ * i: EW_ * i + EW_],
            tks[:, i: i + 1], None, ALUl.is_equal))
    for i in range(NT):
        P.do("vector", lambda e, i=i: e.scalar_tensor_tensor(
            mskd[:, EW_ * i: EW_ * i + EW_], cmb[:, EW_ * i: EW_ * i + EW_],
            -1e30, lgt[:, EW_ * i: EW_ * i + EW_], ALUl.mult, ALUl.add))
    for i in range(NT):
        P.do("vector", lambda e, i=i: e.reduce_max(
            tks[:, NT + i: NT + i + 1], mskd[:, EW_ * i: EW_ * i + EW_],
            axis=AXl.X))
    for i in range(NT):
        P.do("vector", lambda e, i=i: e.tensor_tensor(
            tks[:, 2 * NT + i: 2 * NT + i + 1], tks[:, i: i + 1],
            tks[:, NT + i: NT + i + 1], ALUl.subtract),
            inc="dv" if i == NT - 1 else None)
    P.w("scalar", "dv", P.cnt["dv"])
    for i in range(NT):
        P.do("scalar", lambda e, i=i: e.activation(
            tks[:, 3 * NT + i: 3 * NT + i + 1],
            tks[:, 2 * NT + i: 2 * NT + i + 1], AFl.Sigmoid),
            inc="ac" if i == NT - 1 else None)
    P.w("vector", "ac", P.cnt["ac"])
    for i in range(NT):
        P.do("vector", lambda e, i=i: e.tensor_scalar(
            tks[:, 4 * NT + i: 4 * NT + i + 1],
            tks[:, 3 * NT + i: 3 * NT + i + 1], -1.0, 1.0, ALUl.mult, ALUl.add))
    for i in range(NT):
        P.do("vector", lambda e, i=i: e.tensor_scalar(
            mskd[:, EW_ * i: EW_ * i + EW_], mskd[:, EW_ * i: EW_ * i + EW_],
            tks[:, NT + i: NT + i + 1], None, ALUl.is_equal))
    for i in range(NT):
        P.do("vector", lambda e, i=i: e.tensor_scalar(
            cmb[:, EW_ * i: EW_ * i + EW_], cmb[:, EW_ * i: EW_ * i + EW_],
            tks[:, 3 * NT + i: 3 * NT + i + 1], None, ALUl.mult))
    for i in range(NT):
        P.do("vector", lambda e, i=i: e.tensor_scalar(
            mskd[:, EW_ * i: EW_ * i + EW_], mskd[:, EW_ * i: EW_ * i + EW_],
            tks[:, 4 * NT + i: 4 * NT + i + 1], None, ALUl.mult))
    for i in range(NT):
        P.do("vector", lambda e, i=i: e.tensor_tensor(
            cmb[:, EW_ * i: EW_ * i + EW_], cmb[:, EW_ * i: EW_ * i + EW_],
            mskd[:, EW_ * i: EW_ * i + EW_], ALUl.add),
            inc="dv" if i == NT - 1 else None)


def phase_l0ffn(P, pf, pb, E):
    nc = P.nc
    x1_i, rw_i, rb_i, up_i, ga_i, dw_i, shg_i, shu_i, shd_i, x2_o = (
        E["x1"], E["rw"], E["rb"], E["up"], E["ga"], E["dw"], E["shg"],
        E["shu"], E["shd"], E["x2"])

    ident = _sb(nc, "ident", [128, 128], BF16)
    identf = _sb(nc, "identf", [128, 128], F32)
    ones_col = _sb(nc, "ones_col", [1, 128], F32)
    rbrow = _sb(nc, "rbrow", [1, 16], F32)
    rbb = _sb(nc, "rbb", [128, 16], F32)
    rwf_s = _sb(nc, "rwf_s", [128, 8 * 16], F32)
    xtf = _sb(nc, "xtf", [128, 1024], F32)
    lgtA = _sb(nc, "lgtA", [128, NT * 16], F32)
    wbig = [_sb(nc, "wbig%d" % i, [128, 8192], BF16) for i in range(3)]
    xh = _sb(nc, "xh", [128, 4 * 4096], BF16)
    xin = [_sb(nc, "xin%d" % i, [128, 1024], F32) for i in range(2)]
    xb = _sb(nc, "xb", [128, 1024], BF16)
    acc = _sb(nc, "acc", [128, NT * 1024], F32)
    lgt = _sb(nc, "lgt", [128, NT * 16], F32)
    cmb = _sb(nc, "cmb", [128, NT * 16], F32)
    tks = _sb(nc, "tks", [128, 5 * NT], F32)
    mskd = _sb(nc, "mskd", [128, NT * 16], F32)
    sg = _sb(nc, "sg", [128, 512], F32)
    sg2 = _sb(nc, "sg2", [128, 512], F32)
    tmpe2 = _sb(nc, "tmpe2", [128, 1024], F32)
    shact = _sb(nc, "shact", [128, 8 * 512], BF16)
    acte = _sb(nc, "acte", [128, 4 * 512], BF16)
    tmpe = _sb(nc, "tmpe", [128, 1024], F32)
    lst = _sb(nc, "lst", [128, 64], F32)
    junk = _sb(nc, "junk", [128, 1024], F32)

    rw_src = bass.AP(rw_i, 0, [[16, 128], [128 * 16, 8], [1, 16]])
    rw_dst = bass.AP(rwf_s, 0, [[128, 128], [16, 8], [1, 16]])
    P.do("sync", lambda e: e.dma_start(out=rw_dst, in_=rw_src), inc="dl", n=16)
    load_weight_cast(P, wbig[0], shg_i.ap(), 1024, 1024)
    load_weight_cast(P, wbig[1], shu_i.ap(), 1024, 1024)
    load_weight_cast(P, wbig[2], shd_i.ap(), 1024, 1024)
    dg_sh = P.cnt["dg"]
    P.do("sync", lambda e: e.dma_start(out=rbrow[:, :], in_=rb_i.ap()), inc="dl", n=16)
    dl_rb = P.cnt["dl"]
    P.do("gpsimd", lambda e: e.memset(ident[:], 0.0))
    P.do("gpsimd", lambda e: e.affine_select(
        out=ident[:], in_=ident[:], compare_op=ALU.not_equal, fill=1.0,
        base=0, pattern=[[-1, 128]], channel_multiplier=1))
    P.do("gpsimd", lambda e: e.memset(identf[:], 0.0))
    P.do("gpsimd", lambda e: e.affine_select(
        out=identf[:], in_=identf[:], compare_op=ALU.not_equal, fill=1.0,
        base=0, pattern=[[-1, 128]], channel_multiplier=1))
    P.do("gpsimd", lambda e: e.memset(ones_col[:, :], 1.0), inc="po")
    P.w("tensor", "po", P.cnt["po"])
    P.w("tensor", "dl", dl_rb)
    P.do("tensor", lambda e: e.matmul(pf[5][:, 0:16], ones_col[0:1, :],
                                      rbrow[0:1, :], start=True, stop=True), inc="pe")
    P.w("vector", "pe", P.cnt["pe"])
    P.do("vector", lambda e: e.tensor_copy(rbb[:, :], pf[5][:, 0:16]), inc="dv")
    P.barrier()

    xin_cnt = {}
    P.do("sync", lambda e: e.dma_start(out=xin[0][:, :], in_=x1_i.ap()[0:128, :]),
         inc="dx0", n=16)
    xin_cnt[0] = ("dx0", P.cnt["dx0"])
    for i in range(NT):
        c, itl = i // 4, i % 4
        if i + 1 < NT:
            P.w("sync", "dv", P.cnt["dv"])
            sem = "dx%d" % ((i + 1) % 2)
            P.do("sync", lambda e, i=i: e.dma_start(
                out=xin[(i + 1) % 2][:, :],
                in_=x1_i.ap()[128 * (i + 1): 128 * (i + 1) + 128, :]), inc=sem, n=16)
            xin_cnt[i + 1] = (sem, P.cnt[sem])
        P.w("vector", *xin_cnt[i])
        P.do("vector", lambda e, i=i: e.tensor_copy(xb[:, :], xin[i % 2][:, :]),
             inc="dv")
        P.w("tensor", "dv", P.cnt["dv"])
        for grp in range(2):
            if i > 0 or grp > 0:
                P.w("tensor", "dv", P.cnt["dv"])
            for u in range(4):
                kd = 4 * grp + u
                P.do("tensor", lambda e, kd=kd, u=u, grp=grp: e.transpose(
                    pb[grp][:, 128 * u: 128 * u + 128],
                    xb[:, 128 * kd: 128 * kd + 128], ident[:, :]),
                    inc="pe" if u == 3 else None)
            P.w("vector", "pe", P.cnt["pe"])
            for u in range(4):
                kd = 4 * grp + u
                P.do("vector", lambda e, kd=kd, u=u, grp=grp, c=c, itl=itl:
                     e.tensor_copy(
                         xh[:, 4096 * c + 512 * kd + 128 * itl:
                            4096 * c + 512 * kd + 128 * itl + 128],
                         pb[grp][:, 128 * u: 128 * u + 128]),
                     inc="dv" if u == 3 else None)
        P.w("tensor", "dv", P.cnt["dv"])
        for grp in range(2):
            psx = pf[2 + grp]
            for u in range(4):
                kd = 4 * grp + u
                P.do("tensor", lambda e, kd=kd, u=u, psx=psx, i=i: e.transpose(
                    psx[:, 128 * u: 128 * u + 128],
                    xin[i % 2][:, 128 * kd: 128 * kd + 128], identf[:, :]),
                    inc="pe" if u == 3 else None)
        P.w("vector", "pe", P.cnt["pe"])
        for grp in range(2):
            psx = pf[2 + grp]
            for u in range(4):
                kd = 4 * grp + u
                P.do("vector", lambda e, kd=kd, u=u, psx=psx: e.tensor_copy(
                    xtf[:, 128 * kd: 128 * kd + 128], psx[:, 128 * u: 128 * u + 128]),
                    inc="dv" if (grp, u) == (1, 3) else None)
        P.w("tensor", "dv", P.cnt["dv"])
        if i > 0:
            P.w("tensor", "ac", P.cnt["ac"])
        for kd in range(8):
            P.do("tensor", lambda e, kd=kd: e.matmul(
                pf[4][:, 0:16], xtf[:, 128 * kd: 128 * kd + 128],
                rwf_s[:, 16 * kd: 16 * kd + 16],
                start=(kd == 0), stop=(kd == 7)), inc="pe" if kd == 7 else None)
        P.w("scalar", "pe", P.cnt["pe"])
        P.do("scalar", lambda e, i=i: e.copy(lgtA[:, 16 * i: 16 * i + 16],
                                             pf[4][:, 0:16]), inc="ac")

    prol_ac = P.cnt["ac"]
    prol_dv0 = P.cnt["dv"]

    P.w("tensor", "dg", dg_sh)
    P.w("tensor", "ac", prol_ac)
    for c in range(NCH):
        for hb in range(8):
            if hb > 0 or c > 0:
                P.w("tensor", "dv", P.cnt["dv"])
            for kd in range(8):
                P.do("tensor", lambda e, c=c, hb=hb, kd=kd: e.matmul(
                    pf[0][:, :], wbig[0][:, 1024 * kd + 128 * hb: 1024 * kd + 128 * hb + 128],
                    xh[:, 4096 * c + 512 * kd: 4096 * c + 512 * kd + 512],
                    start=(kd == 0), stop=(kd == 7)))
            for kd in range(8):
                P.do("tensor", lambda e, c=c, hb=hb, kd=kd: e.matmul(
                    pf[1][:, :], wbig[1][:, 1024 * kd + 128 * hb: 1024 * kd + 128 * hb + 128],
                    xh[:, 4096 * c + 512 * kd: 4096 * c + 512 * kd + 512],
                    start=(kd == 0), stop=(kd == 7)), inc="pe" if kd == 7 else None)
            P.w("scalar", "pe", P.cnt["pe"])
            if hb > 0 or c > 0:
                P.w("scalar", "dv", P.cnt["dv"])
            P.do("scalar", lambda e: e.activation(sg[:, :], pf[0][:, :], AF.Silu),
                 inc="ac")
            P.w("vector", "ac", P.cnt["ac"])
            P.do("vector", lambda e, hb=hb: e.tensor_tensor(
                shact[:, 512 * hb: 512 * hb + 512], pf[1][:, :], sg[:, :],
                ALU.mult), inc="dv")
        P.w("tensor", "dv", P.cnt["dv"])
        for itl in range(4):
            i = 4 * c + itl
            if itl > 0 or c > 0:
                P.w("tensor", "ac", P.cnt["ac"])
            for hb in range(8):
                for nh in range(2):
                    P.do("tensor", lambda e, itl=itl, hb=hb, nh=nh: e.matmul(
                        pf[2 + nh][:, :],
                        shact[:, 512 * hb + 128 * itl: 512 * hb + 128 * itl + 128],
                        wbig[2][:, 1024 * hb + 512 * nh: 1024 * hb + 512 * nh + 512],
                        start=(hb == 0), stop=(hb == 7)),
                        inc="pe" if (hb, nh) == (7, 1) else None)
            P.w("scalar", "pe", P.cnt["pe"])
            for nh in range(2):
                P.do("scalar", lambda e, i=i, nh=nh: e.copy(
                    acc[:, 1024 * i + 512 * nh: 1024 * i + 512 * nh + 512],
                    pf[2 + nh][:, :]), inc="ac" if nh == 1 else None)

    EW = [
        (wbig[0], 0, wbig[0], 4096, wbig[1], 0),
        (wbig[1], 4096, wbig[2], 0, wbig[2], 4096),
    ]

    ew_cnt = {}

    def load_expert(e_, pe_guard):
        ub, uo, gb_, go, db, do_ = EW[e_ % 2]
        sem = "dw%d" % (e_ % 2)
        P.w("gpsimd", "pe", pe_guard)
        load_weight_cast(P, ub, up_i.ap(), 1024, 512, dst_off=uo,
                         src_off=e_ * 1024 * 512, inc=sem)
        load_weight_cast(P, gb_, ga_i.ap(), 1024, 512, dst_off=go,
                         src_off=e_ * 1024 * 512, inc=sem)
        load_weight_cast(P, db, dw_i.ap(), 512, 1024, dst_off=do_,
                         src_off=e_ * 512 * 1024, inc=sem)
        ew_cnt[e_] = (sem, P.cnt[sem])

    emit_top2(P, 16, lgtA, rbb, lgt, cmb, tks, mskd)

    load_expert(0, P.cnt["pe"])
    evac_dv = {}
    dn_cnt = {}
    for e_ in range(16):
        ub, uo, gb_, go, db, do_ = EW[e_ % 2]
        if e_ + 1 < 16:
            load_expert(e_ + 1, P.cnt["pe"])
        P.w("tensor", *ew_cnt[e_])
        for c in range(NCH):
            for hb in range(4):
                q = 4 * c + hb
                key = (e_, q)
                prev = (e_, q - 2) if q >= 2 else (e_ - 1, q + 14)
                pA, pB = (pf[0], pf[1]) if q % 2 == 0 else (pf[4], pf[5])
                sgq = sg if q % 2 == 0 else sg2
                if prev in evac_dv:
                    P.w("tensor", "dv", evac_dv[prev])
                elif e_ > 0 or c > 0 or hb > 0:
                    P.w("tensor", "dv", P.cnt["dv"])
                if q % 2 == 1 and hb == 1:
                    kk = (e_, c - 1, 3) if c > 0 else (e_ - 1, NCH - 1, 3)
                    if kk in dn_cnt:
                        P.w("tensor", "dv", dn_cnt[kk])
                for kd in range(8):
                    P.do("tensor", lambda e, c=c, hb=hb, kd=kd, gb_=gb_, go=go,
                         pA=pA: e.matmul(
                             pA[:, :],
                             gb_[:, go + 512 * kd + 128 * hb: go + 512 * kd + 128 * hb + 128],
                             xh[:, 4096 * c + 512 * kd: 4096 * c + 512 * kd + 512],
                             start=(kd == 0), stop=(kd == 7)))
                for kd in range(8):
                    P.do("tensor", lambda e, c=c, hb=hb, kd=kd, ub=ub, uo=uo,
                         pB=pB: e.matmul(
                             pB[:, :],
                             ub[:, uo + 512 * kd + 128 * hb: uo + 512 * kd + 128 * hb + 128],
                             xh[:, 4096 * c + 512 * kd: 4096 * c + 512 * kd + 512],
                             start=(kd == 0), stop=(kd == 7)), inc="pe" if kd == 7 else None)
                pe_q = P.cnt["pe"]
                P.w("scalar", "pe", pe_q)
                if prev in evac_dv:
                    P.w("scalar", "dv", evac_dv[prev])
                P.do("scalar", lambda e, pA=pA, sgq=sgq: e.activation(
                    sgq[:, :], pA[:, :], AF.Silu), inc="ac")
                P.w("vector", "ac", P.cnt["ac"])
                P.do("vector", lambda e, hb=hb, pB=pB, sgq=sgq: e.tensor_tensor(
                    acte[:, 512 * hb: 512 * hb + 512], pB[:, :], sgq[:, :],
                    ALU.mult), inc="dv")
                evac_dv[key] = P.cnt["dv"]
            P.w("tensor", "dv", P.cnt["dv"])
            for itl in range(4):
                i = 4 * c + itl
                pD = (pf[2], pf[3]) if itl % 2 == 0 else (pf[4], pf[5])
                kprev = (e_, c, itl - 2) if itl >= 2 else                     ((e_, c - 1, itl + 2) if c > 0 else (e_ - 1, NCH - 1, itl + 2))
                if kprev in dn_cnt:
                    P.w("tensor", "dv", dn_cnt[kprev])
                for hb in range(4):
                    for nh in range(2):
                        P.do("tensor", lambda e, itl=itl, hb=hb, nh=nh, db=db,
                             do_=do_, pD=pD: e.matmul(
                                 pD[nh][:, :],
                                 acte[:, 512 * hb + 128 * itl: 512 * hb + 128 * itl + 128],
                                 db[:, do_ + 1024 * hb + 512 * nh:
                                    do_ + 1024 * hb + 512 * nh + 512],
                                 start=(hb == 0), stop=(hb == 3)),
                             inc="pe" if (hb, nh) == (3, 1) else None)
                P.w("scalar", "pe", P.cnt["pe"])
                P.w("scalar", "dv", P.cnt["dv"])
                tm = tmpe if itl % 2 == 0 else tmpe2
                for nh in range(2):
                    P.do("scalar", lambda e, i=i, nh=nh, e_=e_, tm=tm, pD=pD:
                         e.activation(
                             tm[:, 512 * nh: 512 * nh + 512], pD[nh][:, :],
                             AF.Identity, scale=cmb[:, 16 * i + e_: 16 * i + e_ + 1]),
                         inc="ac" if nh == 1 else None)
                P.w("vector", "ac", P.cnt["ac"])
                for nh in range(2):
                    P.do("vector", lambda e, i=i, nh=nh, tm=tm: e.tensor_tensor(
                        acc[:, 1024 * i + 512 * nh: 1024 * i + 512 * nh + 512],
                        acc[:, 1024 * i + 512 * nh: 1024 * i + 512 * nh + 512],
                        tm[:, 512 * nh: 512 * nh + 512], ALU.add),
                        inc="dv" if nh == 1 else None)
                dn_cnt[(e_, c, itl)] = P.cnt["dv"]
            if e_ == 15:
                _l0ffn_epi_chunk(P, c, (xin, xin_cnt, acc, lst, junk,
                                        x1_i, x2_o, prol_dv0))

    P.wcur("sync", "st")


def _l0ffn_epi_chunk(P, c, ctx):
    xin, xin_cnt, acc, lst, junk, x1_i, x2_o, prol_dv0 = ctx
    if True:
        if c == 0:
            P.w("sync", "dv", prol_dv0)
            P.do("sync", lambda e: e.dma_start(out=xin[0][:, :],
                                               in_=x1_i.ap()[0:128, :]),
                 inc="dx0", n=16)
            xin_cnt[100] = ("dx0", P.cnt["dx0"])
        for itl in range(4):
            i = 4 * c + itl
            if i + 1 < NT:
                P.w("sync", "dv", P.cnt["dv"])
                sem = "dx%d" % ((i + 1) % 2)
                P.do("sync", lambda e, i=i: e.dma_start(
                    out=xin[(i + 1) % 2][:, :],
                    in_=x1_i.ap()[128 * (i + 1): 128 * (i + 1) + 128, :]),
                    inc=sem, n=16)
                xin_cnt[100 + i + 1] = (sem, P.cnt[sem])
            P.w("vector", *xin_cnt[100 + i])
            P.do("vector", lambda e, i=i: e.tensor_tensor(
                acc[:, 1024 * i: 1024 * i + 1024], acc[:, 1024 * i: 1024 * i + 1024],
                xin[i % 2][:, :], ALU.add), inc="dv")
        for itl in range(4):
            i = 4 * c + itl
            P.do("vector", lambda e, i=i, itl=itl: e.reduce_sum(
                lst[:, itl: itl + 1], acc[:, 1024 * i: 1024 * i + 1024], axis=AX.X))
        for itl in range(4):
            P.do("vector", lambda e, itl=itl: e.tensor_scalar(
                lst[:, itl: itl + 1], lst[:, itl: itl + 1], 1.0 / 1024, None,
                ALU.mult), inc="dv" if itl == 3 else None)
        P.w("scalar", "dv", P.cnt["dv"])
        for itl in range(4):
            i = 4 * c + itl
            P.do("scalar", lambda e, i=i, itl=itl: e.activation(
                junk[:, :], acc[:, 1024 * i: 1024 * i + 1024], AF.Square,
                accum_out=lst[:, 4 + itl: 5 + itl]), inc="ac" if itl == 3 else None)
        P.w("vector", "ac", P.cnt["ac"])
        for itl in range(4):
            P.do("vector", lambda e, itl=itl: e.tensor_tensor(
                lst[:, 8 + itl: 9 + itl], lst[:, itl: itl + 1], lst[:, itl: itl + 1],
                ALU.mult))
        for itl in range(4):
            P.do("vector", lambda e, itl=itl: e.tensor_scalar(
                lst[:, 12 + itl: 13 + itl], lst[:, 4 + itl: 5 + itl], 1.0 / 1024,
                None, ALU.mult))
        for itl in range(4):
            P.do("vector", lambda e, itl=itl: e.tensor_tensor(
                lst[:, 16 + itl: 17 + itl], lst[:, 12 + itl: 13 + itl],
                lst[:, 8 + itl: 9 + itl], ALU.subtract))
        for itl in range(4):
            P.do("vector", lambda e, itl=itl: e.tensor_scalar(
                lst[:, 16 + itl: 17 + itl], lst[:, 16 + itl: 17 + itl], EPS, None,
                ALU.add), inc="dv" if itl == 3 else None)
        P.w("scalar", "dv", P.cnt["dv"])
        for itl in range(4):
            P.do("scalar", lambda e, itl=itl: e.activation(
                lst[:, 20 + itl: 21 + itl], lst[:, 16 + itl: 17 + itl], AF.Sqrt),
                inc="ac" if itl == 3 else None)
        P.w("vector", "ac", P.cnt["ac"])
        for itl in range(4):
            P.do("vector", lambda e, itl=itl: e.reciprocal(
                lst[:, 24 + itl: 25 + itl], lst[:, 20 + itl: 21 + itl]))
        for itl in range(4):
            i = 4 * c + itl
            P.do("vector", lambda e, i=i, itl=itl: e.tensor_scalar(
                acc[:, 1024 * i: 1024 * i + 1024], acc[:, 1024 * i: 1024 * i + 1024],
                lst[:, itl: itl + 1], None, ALU.subtract))
        for itl in range(4):
            i = 4 * c + itl
            P.do("vector", lambda e, i=i, itl=itl: e.tensor_scalar(
                acc[:, 1024 * i: 1024 * i + 1024], acc[:, 1024 * i: 1024 * i + 1024],
                lst[:, 24 + itl: 25 + itl], None, ALU.mult),
                inc="dv" if itl == 3 else None)
        P.wcur("sync", "dv")
        for itl in range(4):
            i = 4 * c + itl
            P.do("sync", lambda e, i=i: e.dma_start(
                out=x2_o.ap()[128 * i: 128 * i + 128, :],
                in_=acc[:, 1024 * i: 1024 * i + 1024]), inc="st", n=16)


def build_launch2b():
    P = Prog()
    nc = P.nc
    P.sem_names("dl", "dg", "pe", "dv", "ac", "po", "st", "bar",
                "dx0", "dx1", "dc0", "dc1", "dw0", "dw1", "dw2", "dwd")
    E = dict(
        x1=nc.dram_tensor("x1", [T, 1024], F32, kind="ExternalInput"),
        rw=nc.dram_tensor("rw", [1024, 16], F32, kind="ExternalInput"),
        rb=nc.dram_tensor("rb", [1, 16], F32, kind="ExternalInput"),
        up=nc.dram_tensor("up", [16, 1024, 512], F32, kind="ExternalInput"),
        ga=nc.dram_tensor("ga", [16, 1024, 512], F32, kind="ExternalInput"),
        dw=nc.dram_tensor("dw", [16, 512, 1024], F32, kind="ExternalInput"),
        shg=nc.dram_tensor("shg", [1024, 1024], F32, kind="ExternalInput"),
        shu=nc.dram_tensor("shu", [1024, 1024], F32, kind="ExternalInput"),
        shd=nc.dram_tensor("shd", [1024, 1024], F32, kind="ExternalInput"),
        x2=nc.dram_tensor("x2", [T, 1024], F32, kind="ExternalOutput"))
    pf = [_ps(nc, "pf%d" % i, [128, 512], F32) for i in range(6)]
    pb = [_ps(nc, "pb%d" % i, [128, 1024], BF16) for i in range(2)]
    phase_l0ffn(P, pf, pb, E)
    return P.build(None)


def _dev_launch2b(inp, x1_all):
    nc = build_launch2b()
    up = np.ascontiguousarray(np.asarray(inp["L0_ffn_up_proj"]))
    ga = np.ascontiguousarray(np.asarray(inp["L0_ffn_gate_proj"]))
    dw = np.ascontiguousarray(np.asarray(inp["L0_ffn_down_proj"]))
    rw = np.ascontiguousarray(np.asarray(inp["L0_ffn_router_w"]).T)
    rb = np.ascontiguousarray(np.asarray(inp["L0_ffn_router_bias"]).reshape(1, 16))
    shg = np.ascontiguousarray(np.asarray(inp["L0_ffn_sh_gate"]).T)
    shu = np.ascontiguousarray(np.asarray(inp["L0_ffn_sh_up"]).T)
    shd = np.ascontiguousarray(np.asarray(inp["L0_ffn_sh_down"]).T)
    in_maps = []
    for c in range(8):
        in_maps.append(dict(x1=np.ascontiguousarray(x1_all[c]), rw=rw, rb=rb,
                            up=up, ga=ga, dw=dw, shg=shg, shu=shu, shd=shd))
    res = run_bass_kernel_spmd(nc, in_maps, core_ids=list(range(8)))
    return res.results


def phase_l1qkv(P, pf, pb, E):
    nc = P.nc
    x2_i, wq1_i, wk1_i, wv1_i, q1f_o, k1f_o, k1t_o, v1t_o, ds1_o = (
        E["x2"], E["wq1"], E["wk1"], E["wv1"], E["q1f"], E["k1f"], E["k1t"],
        E["v1t"], E["ds1"])

    ident = _sb(nc, "ident", [128, 128], BF16)
    wq1_s = _sb(nc, "wq1_s", [128, 8 * 512], BF16)
    wk1_s = _sb(nc, "wk1_s", [128, 8 * 512], BF16)
    wv1_s = _sb(nc, "wv1_s", [128, 8 * 512], BF16)
    xin = [_sb(nc, "xin%d" % i, [128, 1024], F32) for i in range(2)]
    xb = _sb(nc, "xb", [128, 1024], BF16)
    xh_t = _sb(nc, "xh_t", [128, 1024], BF16)
    q1f_s = _sb(nc, "q1f_s", [128, 4 * T], BF16)
    k1f_s = _sb(nc, "k1f_s", [128, 4 * T], BF16)
    k1t_s = _sb(nc, "k1t_s", [128, NT * 512], BF16)
    v1t_s = _sb(nc, "v1t_s", [128, NT * 512], BF16)
    ds1_s = _sb(nc, "ds1_s", [128, 4 * 512], F32)

    load_weight_cast(P, wq1_s, wq1_i.ap(), 1024, 512)
    load_weight_cast(P, wk1_s, wk1_i.ap(), 1024, 512)
    load_weight_cast(P, wv1_s, wv1_i.ap(), 1024, 512)
    dg_w = P.cnt["dg"]
    P.do("gpsimd", lambda e: e.memset(ident[:], 0.0))
    P.do("gpsimd", lambda e: e.affine_select(
        out=ident[:], in_=ident[:], compare_op=ALU.not_equal, fill=1.0,
        base=0, pattern=[[-1, 128]], channel_multiplier=1), inc="po")

    xin_cnt = {}
    P.do("sync", lambda e: e.dma_start(out=xin[0][:, :], in_=x2_i.ap()[0:128, :]),
         inc="dx0", n=16)
    xin_cnt[0] = ("dx0", P.cnt["dx0"])
    P.w("tensor", "dg", dg_w)
    P.w("tensor", "po", P.cnt["po"])
    for i in range(NT):
        if i + 1 < NT:
            P.w("sync", "dv", P.cnt["dv"])
            sem = "dx%d" % ((i + 1) % 2)
            P.do("sync", lambda e, i=i: e.dma_start(
                out=xin[(i + 1) % 2][:, :],
                in_=x2_i.ap()[128 * (i + 1): 128 * (i + 1) + 128, :]), inc=sem, n=16)
            xin_cnt[i + 1] = (sem, P.cnt[sem])
        P.w("vector", *xin_cnt[i])
        P.do("vector", lambda e, i=i: e.tensor_copy(xb[:, :], xin[i % 2][:, :]),
             inc="dv")
        P.w("tensor", "dv", P.cnt["dv"])
        for grp in range(2):
            if i > 0 or grp > 0:
                P.w("tensor", "dv", P.cnt["dv"])
            for u in range(4):
                kd = 4 * grp + u
                P.do("tensor", lambda e, kd=kd, u=u, grp=grp: e.transpose(
                    pb[grp][:, 128 * u: 128 * u + 128],
                    xb[:, 128 * kd: 128 * kd + 128], ident[:, :]),
                    inc="pe" if u == 3 else None)
            P.w("vector", "pe", P.cnt["pe"])
            for u in range(4):
                kd = 4 * grp + u
                P.do("vector", lambda e, kd=kd, u=u, grp=grp: e.tensor_copy(
                    xh_t[:, 128 * kd: 128 * kd + 128],
                    pb[grp][:, 128 * u: 128 * u + 128]),
                    inc="dv" if u == 3 else None)
        P.w("tensor", "dv", P.cnt["dv"])
        for (wsrc, dstf) in ((wq1_s, q1f_s), (wk1_s, k1f_s)):
            for fm in range(4):
                psx = pf[fm % 2]
                if not (wsrc is wq1_s and fm < 2 and i == 0):
                    P.w("tensor", "dv", P.cnt["dv"])
                for kd in range(8):
                    P.do("tensor", lambda e, fm=fm, kd=kd, psx=psx, wsrc=wsrc:
                         e.matmul(
                             psx[:, 0:128],
                             wsrc[:, 512 * kd + 128 * fm: 512 * kd + 128 * fm + 128],
                             xh_t[:, 128 * kd: 128 * kd + 128],
                             start=(kd == 0), stop=(kd == 7)),
                         inc="pe" if kd == 7 else None)
                P.w("vector", "pe", P.cnt["pe"])
                P.do("vector", lambda e, fm=fm, i=i, psx=psx, dstf=dstf:
                     e.tensor_copy(
                         dstf[:, T * fm + 128 * i: T * fm + 128 * i + 128],
                         psx[:, 0:128]), inc="dv")
        for (wsrc, dstt, psx) in ((wk1_s, k1t_s, pf[2]), (wv1_s, v1t_s, pf[3])):
            if i > 0:
                P.w("tensor", "dv", P.cnt["dv"])
            for kd in range(8):
                P.do("tensor", lambda e, kd=kd, psx=psx, wsrc=wsrc: e.matmul(
                    psx[:, :], xh_t[:, 128 * kd: 128 * kd + 128],
                    wsrc[:, 512 * kd: 512 * kd + 512],
                    start=(kd == 0), stop=(kd == 7)), inc="pe" if kd == 7 else None)
            P.w("vector", "pe", P.cnt["pe"])
            P.do("vector", lambda e, i=i, psx=psx, dstt=dstt: e.tensor_copy(
                dstt[:, 512 * i: 512 * i + 512], psx[:, :]), inc="dv")

    P.barrier()
    for fm in range(4):
        for g in range(NT):
            P.do("tensor", lambda e, g=g, fm=fm: e.matmul(
                pf[2 + fm][:, :], k1t_s[:, 512 * g + 128 * fm: 512 * g + 128 * fm + 128],
                v1t_s[:, 512 * g: 512 * g + 512],
                start=(g == 0), stop=(g == NT - 1)),
                inc="pe" if g == NT - 1 else None)
    P.w("vector", "pe", P.cnt["pe"])
    for fm in range(4):
        P.do("vector", lambda e, fm=fm: e.tensor_copy(
            ds1_s[:, 512 * fm: 512 * fm + 512], pf[2 + fm][:, :]),
            inc="dv" if fm == 3 else None)

    P.barrier()
    for fm in range(4):
        P.do("sync", lambda e, fm=fm: e.dma_start(
            out=q1f_o.ap()[128 * fm: 128 * fm + 128, :],
            in_=q1f_s[:, T * fm: T * fm + T]), inc="st", n=16)
        P.do("sync", lambda e, fm=fm: e.dma_start(
            out=k1f_o.ap()[128 * fm: 128 * fm + 128, :],
            in_=k1f_s[:, T * fm: T * fm + T]), inc="st", n=16)
        P.do("sync", lambda e, fm=fm: e.dma_start(
            out=ds1_o.ap()[128 * fm: 128 * fm + 128, :],
            in_=ds1_s[:, 512 * fm: 512 * fm + 512]), inc="st", n=16)
    for i in range(NT):
        P.do("sync", lambda e, i=i: e.dma_start(
            out=k1t_o.ap()[128 * i: 128 * i + 128, :],
            in_=k1t_s[:, 512 * i: 512 * i + 512]), inc="st", n=16)
        P.do("sync", lambda e, i=i: e.dma_start(
            out=v1t_o.ap()[128 * i: 128 * i + 128, :],
            in_=v1t_s[:, 512 * i: 512 * i + 512]), inc="st", n=16)
    P.wcur("sync", "st")


def build_launch2c():
    P = Prog()
    nc = P.nc
    P.sem_names("dl", "dg", "pe", "dv", "ac", "po", "st", "bar",
                "dx0", "dx1", "dc0", "dc1", "dw0", "dw1", "dw2", "dwd")
    E = dict(
        x2=nc.dram_tensor("x2", [T, 1024], F32, kind="ExternalInput"),
        wq1=nc.dram_tensor("wq1", [1024, 512], F32, kind="ExternalInput"),
        wk1=nc.dram_tensor("wk1", [1024, 512], F32, kind="ExternalInput"),
        wv1=nc.dram_tensor("wv1", [1024, 512], F32, kind="ExternalInput"),
        q1f=nc.dram_tensor("q1f", [512, T], BF16, kind="ExternalOutput"),
        k1f=nc.dram_tensor("k1f", [512, T], BF16, kind="ExternalOutput"),
        k1t=nc.dram_tensor("k1t", [T, 512], BF16, kind="ExternalOutput"),
        v1t=nc.dram_tensor("v1t", [T, 512], BF16, kind="ExternalOutput"),
        ds1=nc.dram_tensor("ds1", [512, 512], F32, kind="ExternalOutput"))
    pf = [_ps(nc, "pf%d" % i, [128, 512], F32) for i in range(6)]
    pb = [_ps(nc, "pb%d" % i, [128, 1024], BF16) for i in range(2)]
    phase_l1qkv(P, pf, pb, E)
    return P.build(None)


def phase_l1attn(P, pf, pb, E):
    nc = P.nc
    x2_i, q1f_i, k1f_i, k1t_i, v1t_i, ds1p_i, wo1_i, x3_o = (
        E["x2"], E["q1f"], E["k1f"], E["k1t"], E["v1t"], E["ds1p"],
        E["wo1"], E["x3"])

    bigms = _sb(nc, "bigms", [128, 1024], F32)
    wo1_s = _sb(nc, "wo1_s", [128, 4 * 1024], BF16)
    q1f_s = _sb(nc, "q1f_s", [128, 4 * T], BF16)
    k1f_s = _sb(nc, "k1f_s", [128, 4 * T], BF16)
    k1t_s = _sb(nc, "k1t_s", [128, NT * 512], BF16)
    v1t_s = _sb(nc, "v1t_s", [128, NT * 512], BF16)
    S1 = _sb(nc, "S1", [128, 2048], F32)
    S1b = _sb(nc, "S1b", [128, 2048], BF16)
    stg = [_sb(nc, "stg%d" % i, [128, 2048], F32) for i in range(3)]
    sbf = [_sb(nc, "sbf%d" % i, [128, 512], BF16) for i in range(4)]
    yf = _sb(nc, "yf", [128, 4 * 512], BF16)
    xin = [_sb(nc, "xin%d" % i, [128, 1024], F32) for i in range(2)]
    x3_s = _sb(nc, "x3_s", [128, NT * 1024], F32)
    lst = _sb(nc, "lst", [128, 64], F32)
    junk = _sb(nc, "junk", [128, 1024], F32)

    load_weight_cast(P, wo1_s, wo1_i.ap(), 512, 1024)
    dg_w = P.cnt["dg"]
    for fm in range(4):
        P.do("sync", lambda e, fm=fm: e.dma_start(
            out=q1f_s[:, T * fm: T * fm + T],
            in_=q1f_i.ap()[128 * fm: 128 * fm + 128, :]), inc="dl", n=16)
        P.do("sync", lambda e, fm=fm: e.dma_start(
            out=k1f_s[:, T * fm: T * fm + T],
            in_=k1f_i.ap()[128 * fm: 128 * fm + 128, :]), inc="dl", n=16)
    for i in range(NT):
        P.do("sync", lambda e, i=i: e.dma_start(
            out=k1t_s[:, 512 * i: 512 * i + 512],
            in_=k1t_i.ap()[128 * i: 128 * i + 128, :]), inc="dl", n=16)
        P.do("sync", lambda e, i=i: e.dma_start(
            out=v1t_s[:, 512 * i: 512 * i + 512],
            in_=v1t_i.ap()[128 * i: 128 * i + 128, :]), inc="dl", n=16)
    for r in range(3):
        for fm in range(4):
            P.do("sync", lambda e, r=r, fm=fm: e.dma_start(
                out=stg[r][:, 512 * fm: 512 * fm + 512],
                in_=ds1p_i.ap()[r][128 * fm: 128 * fm + 128, :]), inc="dl", n=16)
    dl_all = P.cnt["dl"]
    P.do("gpsimd", lambda e: e.memset(bigms[:, :], 1.0))
    P.do("gpsimd", lambda e: e.affine_select(
        out=bigms[:, :], in_=bigms[:, :], compare_op=ALU.is_ge, fill=0.0,
        base=-513, pattern=[[1, 1024]], channel_multiplier=-1), inc="po")
    P.w("vector", "dl", dl_all)
    P.do("vector", lambda e: e.tensor_add(S1[:, :], stg[0][:, :], stg[1][:, :]))
    P.do("vector", lambda e: e.tensor_copy(junk[:, 0:64], stg[0][:, 0:64]))
    P.do("vector", lambda e: e.tensor_add(S1[:, :], S1[:, :], stg[2][:, :]))
    P.do("vector", lambda e: e.tensor_copy(junk[:, 0:64], stg[0][:, 0:64]))
    P.do("vector", lambda e: e.tensor_copy(S1b[:, :], S1[:, :]), inc="dv")
    P.barrier()

    P.do("sync", lambda e: e.dma_start(out=xin[0][:, :], in_=x2_i.ap()[0:128, :]),
         inc="dx0", n=16)
    dl_x = {0: ("dx0", P.cnt["dx0"])}
    mask_dv = {}
    s1add_dv = {}
    resid_dv = {}
    yf_dv = {}
    for c in range(NCH):
        for jt in range(4):
            psc = pf[jt % 2]
            if jt >= 2:
                P.w("tensor", "dv", mask_dv[(c, jt - 2)])
            elif c > 0:
                P.w("tensor", "dv", resid_dv[(c - 1, 2)])
            for fm in range(4):
                P.do("tensor", lambda e, c=c, jt=jt, fm=fm, psc=psc: e.matmul(
                    psc[:, :],
                    k1f_s[:, T * fm + 512 * c + 128 * jt: T * fm + 512 * c + 128 * jt + 128],
                    q1f_s[:, T * fm + 512 * c: T * fm + 512 * c + 512],
                    start=(fm == 0), stop=(fm == 3)), inc="pe" if fm == 3 else None)
            P.w("vector", "pe", P.cnt["pe"])
            P.do("vector", lambda e, jt=jt, psc=psc: e.tensor_tensor(
                sbf[jt][:, :], psc[:, :],
                bigms[:, 512 - 128 * jt: 1024 - 128 * jt], ALU.mult), inc="dv")
            mask_dv[(c, jt)] = P.cnt["dv"]
        P.w("tensor", "dv", P.cnt["dv"])
        for dt in range(4):
            psy = pf[2 + dt]
            first = True
            for jt in range(4):
                P.do("tensor", lambda e, c=c, jt=jt, dt=dt, psy=psy, first=first:
                     e.matmul(
                         psy[:, :],
                         v1t_s[:, 512 * (4 * c + jt) + 128 * dt:
                               512 * (4 * c + jt) + 128 * dt + 128],
                         sbf[jt][:, :], start=first, stop=False))
                first = False
            for fm in range(4):
                P.do("tensor", lambda e, c=c, fm=fm, dt=dt, psy=psy: e.matmul(
                    psy[:, :], S1b[:, 512 * fm + 128 * dt: 512 * fm + 128 * dt + 128],
                    q1f_s[:, T * fm + 512 * c: T * fm + 512 * c + 512],
                    start=False, stop=(fm == 3)),
                    inc="pe" if fm == 3 else None)
        P.w("vector", "pe", P.cnt["pe"])
        for dt in range(4):
            P.do("vector", lambda e, dt=dt: e.tensor_copy(
                yf[:, 512 * dt: 512 * dt + 512], pf[2 + dt][:, :]),
                inc="dv" if dt == 3 else None)
        yf_dv[c] = P.cnt["dv"]
        for fm in range(4):
            psx = pf[fm % 2]
            if fm < 2:
                P.w("tensor", "dv", mask_dv[(c, 2 + fm)])
            else:
                P.w("tensor", "dv", s1add_dv[(c, fm - 2)])
            for jt in range(4):
                P.do("tensor", lambda e, c=c, jt=jt, fm=fm, psx=psx: e.matmul(
                    psx[:, :],
                    k1t_s[:, 512 * (4 * c + jt) + 128 * fm:
                          512 * (4 * c + jt) + 128 * fm + 128],
                    v1t_s[:, 512 * (4 * c + jt): 512 * (4 * c + jt) + 512],
                    start=(jt == 0), stop=(jt == 3)), inc="pe" if jt == 3 else None)
            P.w("vector", "pe", P.cnt["pe"])
            P.do("vector", lambda e, fm=fm, psx=psx: e.tensor_tensor(
                S1[:, 512 * fm: 512 * fm + 512], S1[:, 512 * fm: 512 * fm + 512],
                psx[:, :], ALU.add))
            P.do("vector", lambda e, fm=fm: e.tensor_copy(
                S1b[:, 512 * fm: 512 * fm + 512], S1[:, 512 * fm: 512 * fm + 512]),
                inc="dv")
            s1add_dv[(c, fm)] = P.cnt["dv"]
        for itl in range(4):
            i = 4 * c + itl
            pw = (pf[0], pf[1]) if itl % 2 == 0 else (pf[2], pf[3])
            if i + 1 < NT:
                P.w("sync", "dv", P.cnt["dv"])
                sem = "dx%d" % ((i + 1) % 2)
                P.do("sync", lambda e, i=i: e.dma_start(
                    out=xin[(i + 1) % 2][:, :],
                    in_=x2_i.ap()[128 * (i + 1): 128 * (i + 1) + 128, :]),
                    inc=sem, n=16)
                dl_x[i + 1] = (sem, P.cnt[sem])
            if itl == 0:
                P.w("tensor", "dv", s1add_dv[(c, 3)])
            elif itl == 1:
                P.w("tensor", "dv", yf_dv[c])
            else:
                P.w("tensor", "dv", resid_dv[(c, itl - 2)])
            P.w("tensor", "dg", dg_w)
            for dt in range(4):
                for nh in range(2):
                    P.do("tensor", lambda e, itl=itl, dt=dt, nh=nh, pw=pw: e.matmul(
                        pw[nh][:, :], yf[:, 512 * dt + 128 * itl: 512 * dt + 128 * itl + 128],
                        wo1_s[:, 1024 * dt + 512 * nh: 1024 * dt + 512 * nh + 512],
                        start=(dt == 0), stop=(dt == 3)),
                        inc="pe" if (dt, nh) == (3, 1) else None)
            P.w("vector", "pe", P.cnt["pe"])
            P.w("vector", *dl_x[i])
            for nh in range(2):
                P.do("vector", lambda e, i=i, nh=nh, pw=pw: e.tensor_tensor(
                    x3_s[:, 1024 * i + 512 * nh: 1024 * i + 512 * nh + 512],
                    xin[i % 2][:, 512 * nh: 512 * nh + 512],
                    pw[nh][:, :], ALU.add), inc="dv" if nh == 1 else None)
            resid_dv[(c, itl)] = P.cnt["dv"]
        for itl in range(4):
            i = 4 * c + itl
            P.do("vector", lambda e, i=i, itl=itl: e.reduce_sum(
                lst[:, itl: itl + 1], x3_s[:, 1024 * i: 1024 * i + 1024], axis=AX.X))
        for itl in range(4):
            P.do("vector", lambda e, itl=itl: e.tensor_scalar(
                lst[:, itl: itl + 1], lst[:, itl: itl + 1], 1.0 / 1024, None,
                ALU.mult), inc="dv" if itl == 3 else None)
        P.w("scalar", "dv", P.cnt["dv"])
        for itl in range(4):
            i = 4 * c + itl
            P.do("scalar", lambda e, i=i, itl=itl: e.activation(
                junk[:, :], x3_s[:, 1024 * i: 1024 * i + 1024], AF.Square,
                accum_out=lst[:, 4 + itl: 5 + itl]), inc="ac" if itl == 3 else None)
        P.w("vector", "ac", P.cnt["ac"])
        for itl in range(4):
            P.do("vector", lambda e, itl=itl: e.tensor_tensor(
                lst[:, 8 + itl: 9 + itl], lst[:, itl: itl + 1], lst[:, itl: itl + 1],
                ALU.mult))
        for itl in range(4):
            P.do("vector", lambda e, itl=itl: e.tensor_scalar(
                lst[:, 12 + itl: 13 + itl], lst[:, 4 + itl: 5 + itl], 1.0 / 1024,
                None, ALU.mult))
        for itl in range(4):
            P.do("vector", lambda e, itl=itl: e.tensor_tensor(
                lst[:, 16 + itl: 17 + itl], lst[:, 12 + itl: 13 + itl],
                lst[:, 8 + itl: 9 + itl], ALU.subtract))
        for itl in range(4):
            P.do("vector", lambda e, itl=itl: e.tensor_scalar(
                lst[:, 16 + itl: 17 + itl], lst[:, 16 + itl: 17 + itl], EPS, None,
                ALU.add), inc="dv" if itl == 3 else None)
        P.w("scalar", "dv", P.cnt["dv"])
        for itl in range(4):
            P.do("scalar", lambda e, itl=itl: e.activation(
                lst[:, 20 + itl: 21 + itl], lst[:, 16 + itl: 17 + itl], AF.Sqrt),
                inc="ac" if itl == 3 else None)
        P.w("vector", "ac", P.cnt["ac"])
        for itl in range(4):
            P.do("vector", lambda e, itl=itl: e.reciprocal(
                lst[:, 24 + itl: 25 + itl], lst[:, 20 + itl: 21 + itl]))
        for itl in range(4):
            i = 4 * c + itl
            P.do("vector", lambda e, i=i, itl=itl: e.tensor_scalar(
                x3_s[:, 1024 * i: 1024 * i + 1024], x3_s[:, 1024 * i: 1024 * i + 1024],
                lst[:, itl: itl + 1], None, ALU.subtract))
        for itl in range(4):
            i = 4 * c + itl
            P.do("vector", lambda e, i=i, itl=itl: e.tensor_scalar(
                x3_s[:, 1024 * i: 1024 * i + 1024], x3_s[:, 1024 * i: 1024 * i + 1024],
                lst[:, 24 + itl: 25 + itl], None, ALU.mult),
                inc="dv" if itl == 3 else None)

    P.barrier()
    for i in range(NT):
        P.do("sync", lambda e, i=i: e.dma_start(
            out=x3_o.ap()[128 * i: 128 * i + 128, :],
            in_=x3_s[:, 1024 * i: 1024 * i + 1024]), inc="st", n=16)
    P.wcur("sync", "st")


def build_launch3a():
    P = Prog()
    nc = P.nc
    P.sem_names("dl", "dg", "pe", "dv", "ac", "po", "st", "bar",
                "dx0", "dx1", "dc0", "dc1", "dw0", "dw1", "dw2", "dwd")
    E = dict(
        x2=nc.dram_tensor("x2", [T, 1024], F32, kind="ExternalInput"),
        q1f=nc.dram_tensor("q1f", [512, T], BF16, kind="ExternalInput"),
        k1f=nc.dram_tensor("k1f", [512, T], BF16, kind="ExternalInput"),
        k1t=nc.dram_tensor("k1t", [T, 512], BF16, kind="ExternalInput"),
        v1t=nc.dram_tensor("v1t", [T, 512], BF16, kind="ExternalInput"),
        ds1p=nc.dram_tensor("ds1p", [3, 512, 512], F32, kind="ExternalInput"),
        wo1=nc.dram_tensor("wo1", [512, 1024], F32, kind="ExternalInput"),
        x3=nc.dram_tensor("x3", [T, 1024], F32, kind="ExternalOutput"))
    pf = [_ps(nc, "pf%d" % i, [128, 512], F32) for i in range(6)]
    pb = [_ps(nc, "pb%d" % i, [128, 1024], BF16) for i in range(2)]
    phase_l1attn(P, pf, pb, E)
    return P.build(None)


def phase_l1ffn(P, pf, pb, E):
    E1 = 8
    nc = P.nc
    x3_i, wr_i, br_i, w1_i, b1_i, w2_i, b2_i, y_o = (
        E["x3"], E["wr"], E["br"], E["w1"], E["b1"], E["w2"], E["b2"], E["y"])
    b1c_i = E["b1c"]

    ident = _sb(nc, "ident", [128, 128], BF16)
    identf = _sb(nc, "identf", [128, 128], F32)
    ones_col = _sb(nc, "ones_col", [1, 128], F32)
    onesb = _sb(nc, "onesb", [1, 512], BF16)
    rbrow = _sb(nc, "rbrow", [1, E1], F32)
    rbb = _sb(nc, "rbb", [128, E1], F32)
    rwf_s = _sb(nc, "rwf_s", [128, 8 * E1], F32)
    xtf = _sb(nc, "xtf", [128, 1024], F32)
    lgtA = _sb(nc, "lgtA", [128, NT * E1], F32)
    xh = _sb(nc, "xh", [128, 4 * 4096], BF16)
    xin = [_sb(nc, "xin%d" % i, [128, 1024], F32) for i in range(2)]
    xb = _sb(nc, "xb", [128, 1024], BF16)
    lgt = _sb(nc, "lgt", [128, NT * E1], F32)
    cmb = _sb(nc, "cmb", [128, NT * E1], F32)
    tks = _sb(nc, "tks", [128, 5 * NT], F32)
    mskd = _sb(nc, "mskd", [128, NT * E1], F32)
    bja = [_sb(nc, "bja%d" % i, [128, 4], F32) for i in range(3)]
    bjg = [_sb(nc, "bjg%d" % i, [128, 4], F32) for i in range(3)]
    b2e = [_sb(nc, "b2e%d" % i, [1, 1024], BF16) for i in range(2)]
    wa = [_sb(nc, "wa%d" % i, [128, 8 * 512], BF16) for i in range(3)]
    wg = [_sb(nc, "wg%d" % i, [128, 8 * 512], BF16) for i in range(3)]
    wd = [_sb(nc, "wd%d" % i, [128, 4 * 1024], BF16) for i in range(2)]
    act_s = _sb(nc, "act_s", [128, 8 * 512], BF16)
    sg = _sb(nc, "sg", [128, 512], F32)
    sg2 = _sb(nc, "sg2", [128, 512], F32)
    tmpe2 = _sb(nc, "tmpe2", [128, 1024], F32)
    acc = _sb(nc, "acc", [128, 8 * 1024], F32)
    dacc = _sb(nc, "dacc", [128, 8 * 1024], F32)
    tmpe = xtf
    lst = _sb(nc, "lst", [128, 64], F32)
    junk = _sb(nc, "junk", [128, 1024], BF16)

    NG = 2
    GT = 8

    rw_src = bass.AP(wr_i, 0, [[E1, 128], [128 * E1, 8], [1, E1]])
    rw_dst = bass.AP(rwf_s, 0, [[8 * E1, 128], [E1, 8], [1, E1]])
    P.do("sync", lambda e: e.dma_start(out=rw_dst, in_=rw_src), inc="dl", n=16)
    P.do("sync", lambda e: e.dma_start(out=rbrow[:, :], in_=br_i.ap()), inc="dl", n=16)
    dl_c = P.cnt["dl"]
    P.do("gpsimd", lambda e: e.memset(ident[:], 0.0))
    P.do("gpsimd", lambda e: e.affine_select(
        out=ident[:], in_=ident[:], compare_op=ALU.not_equal, fill=1.0,
        base=0, pattern=[[-1, 128]], channel_multiplier=1))
    P.do("gpsimd", lambda e: e.memset(identf[:], 0.0))
    P.do("gpsimd", lambda e: e.affine_select(
        out=identf[:], in_=identf[:], compare_op=ALU.not_equal, fill=1.0,
        base=0, pattern=[[-1, 128]], channel_multiplier=1))
    P.do("gpsimd", lambda e: e.memset(ones_col[:, :], 1.0))
    P.do("gpsimd", lambda e: e.memset(onesb[:, :], 1.0), inc="po")
    P.w("tensor", "po", P.cnt["po"])
    P.w("tensor", "dl", dl_c)
    P.do("tensor", lambda e: e.matmul(pf[5][:, 0:E1], ones_col[0:1, :],
                                      rbrow[0:1, :], start=True, stop=True), inc="pe")
    P.w("vector", "pe", P.cnt["pe"])
    P.do("vector", lambda e: e.tensor_copy(rbb[:, :], pf[5][:, 0:E1]), inc="dv")
    P.barrier()

    xin_cnt = {}
    P.do("sync", lambda e: e.dma_start(out=xin[0][:, :], in_=x3_i.ap()[0:128, :]),
         inc="dx0", n=16)
    xin_cnt[0] = ("dx0", P.cnt["dx0"])
    for i in range(NT):
        c, itl = i // 4, i % 4
        if i + 1 < NT:
            P.w("sync", "dv", P.cnt["dv"])
            sem = "dx%d" % ((i + 1) % 2)
            P.do("sync", lambda e, i=i: e.dma_start(
                out=xin[(i + 1) % 2][:, :],
                in_=x3_i.ap()[128 * (i + 1): 128 * (i + 1) + 128, :]), inc=sem, n=16)
            xin_cnt[i + 1] = (sem, P.cnt[sem])
        P.w("vector", *xin_cnt[i])
        P.do("vector", lambda e, i=i: e.tensor_copy(xb[:, :], xin[i % 2][:, :]),
             inc="dv")
        P.w("tensor", "dv", P.cnt["dv"])
        for grp in range(2):
            if i > 0 or grp > 0:
                P.w("tensor", "dv", P.cnt["dv"])
            for u in range(4):
                kd = 4 * grp + u
                P.do("tensor", lambda e, kd=kd, u=u, grp=grp: e.transpose(
                    pb[grp][:, 128 * u: 128 * u + 128],
                    xb[:, 128 * kd: 128 * kd + 128], ident[:, :]),
                    inc="pe" if u == 3 else None)
            P.w("vector", "pe", P.cnt["pe"])
            for u in range(4):
                kd = 4 * grp + u
                P.do("vector", lambda e, kd=kd, u=u, grp=grp, c=c, itl=itl:
                     e.tensor_copy(
                         xh[:, 4096 * c + 512 * kd + 128 * itl:
                            4096 * c + 512 * kd + 128 * itl + 128],
                         pb[grp][:, 128 * u: 128 * u + 128]),
                     inc="dv" if u == 3 else None)
        P.w("tensor", "dv", P.cnt["dv"])
        for grp in range(2):
            psx = pf[2 + grp]
            for u in range(4):
                kd = 4 * grp + u
                P.do("tensor", lambda e, kd=kd, u=u, psx=psx, i=i: e.transpose(
                    psx[:, 128 * u: 128 * u + 128],
                    xin[i % 2][:, 128 * kd: 128 * kd + 128], identf[:, :]),
                    inc="pe" if u == 3 else None)
        P.w("vector", "pe", P.cnt["pe"])
        for grp in range(2):
            psx = pf[2 + grp]
            for u in range(4):
                kd = 4 * grp + u
                P.do("vector", lambda e, kd=kd, u=u, psx=psx: e.tensor_copy(
                    xtf[:, 128 * kd: 128 * kd + 128], psx[:, 128 * u: 128 * u + 128]),
                    inc="dv" if (grp, u) == (1, 3) else None)
        P.w("tensor", "dv", P.cnt["dv"])
        if i > 0:
            P.w("tensor", "ac", P.cnt["ac"])
        for kd in range(8):
            P.do("tensor", lambda e, kd=kd: e.matmul(
                pf[4][:, 0:E1], xtf[:, 128 * kd: 128 * kd + 128],
                rwf_s[:, E1 * kd: E1 * kd + E1],
                start=(kd == 0), stop=(kd == 7)), inc="pe" if kd == 7 else None)
        P.w("scalar", "pe", P.cnt["pe"])
        P.do("scalar", lambda e, i=i: e.copy(lgtA[:, E1 * i: E1 * i + E1],
                                             pf[4][:, 0:E1]), inc="ac")

    prol_pe1 = P.cnt["pe"]
    prol_dv1 = P.cnt["dv"]
    prol_ac1 = P.cnt["ac"]

    emit_top2(P, E1, lgtA, rbb, lgt, cmb, tks, mskd)

    def load_slice(P, dstbuf, e_, col0, sem):
        src = bass.AP(w1_i, e_ * 1024 * 4096 + col0,
                      [[4096, 128], [128 * 4096, 8], [1, 512]])
        dst = bass.AP(dstbuf, 0, [[8 * 512, 128], [512, 8], [1, 512]])
        P.do("gpsimd", lambda e, src=src, dst=dst: e.dma_start(out=dst, in_=src),
             inc=sem, n=16)

    def load_w2_slice(P, dstbuf, e_, row0, sem):
        src = bass.AP(w2_i, e_ * 2048 * 1024 + row0 * 1024,
                      [[1024, 128], [128 * 1024, 4], [1, 1024]])
        dst = bass.AP(dstbuf, 0, [[4 * 1024, 128], [1024, 4], [1, 1024]])
        P.do("gpsimd", lambda e, src=src, dst=dst: e.dma_start(out=dst, in_=src),
             inc=sem, n=16)

    steps = [(g, e_, j) for g in range(NG) for e_ in range(E1) for j in range(4)]
    pe_guard = {0: 0, 1: 0}

    def load_wag(sidx):
        g, e_, j = steps[sidx]
        par = sidx % 3
        sem = "dw%d" % par
        P.w("gpsimd", "pe", pe_guard.get(sidx, 0))
        load_slice(P, wa[par], e_, 512 * j, sem)
        load_slice(P, wg[par], e_, 2048 + 512 * j, sem)
        asrc = bass.AP(b1c_i, 4096 * e_ + 1024 * j, [[4, 128], [1, 4]])
        gsrc = bass.AP(b1c_i, 4096 * e_ + 1024 * j + 512, [[4, 128], [1, 4]])
        P.do("gpsimd", lambda e, asrc=asrc, par=par: e.dma_start(
            out=bja[par][:, :], in_=asrc), inc=sem, n=16)
        P.do("gpsimd", lambda e, gsrc=gsrc, par=par: e.dma_start(
            out=bjg[par][:, :], in_=gsrc), inc=sem, n=16)
        return (sem, P.cnt[sem])

    def load_wdb(sidx, guard):
        g, e_, j = steps[sidx]
        par2 = sidx % 2
        P.w("gpsimd", "pe", guard)
        load_w2_slice(P, wd[par2], e_, 512 * j, "dwd")
        if j == 3:
            b2src = bass.AP(b2_i, 1024 * e_, [[1024, 1], [1, 1024]])
            P.do("gpsimd", lambda e, b2src=b2src, par2=par2: e.dma_start(
                out=b2e[par2][:, :], in_=b2src), inc="dwd", n=16)
        return ("dwd", P.cnt["dwd"])

    dg_done = {0: load_wag(0), 1: load_wag(1)}
    wdb_done = {0: load_wdb(0, 0)}
    evac_dv = {}
    silu_ac = {}
    dn_cnt = {}
    epi_state = {}
    for sidx, (g, e_, j) in enumerate(steps):
        par = sidx % 3
        par2 = sidx % 2
        cur_pe = P.cnt["pe"]
        if sidx + 1 < len(steps):
            wdb_done[sidx + 1] = load_wdb(sidx + 1, cur_pe)
        if sidx + 2 < len(steps):
            pe_guard[sidx + 2] = cur_pe
            dg_done[sidx + 2] = load_wag(sidx + 2)
        P.w("tensor", *dg_done[sidx])
        for hb in range(4):
            for cc in range(2):
                c = 2 * g + cc
                q = 2 * hb + cc
                pA, pB = (pf[0], pf[1]) if q % 2 == 0 else (pf[2], pf[3])
                sgq = sg if q % 2 == 0 else sg2
                key = (sidx, q)
                prev = (sidx, q - 2) if q >= 2 else (sidx - 1, q + 6)
                if prev in evac_dv:
                    P.w("tensor", "dv", evac_dv[prev])
                elif sidx > 0:
                    P.w("tensor", "dv", P.cnt["dv"])
                elif q == 1:
                    P.w("tensor", "dv", prol_dv1)
                if q == 1 and (sidx - 1, 7) in dn_cnt:
                    P.w("tensor", *dn_cnt[(sidx - 1, 7)])
                for kd in range(8):
                    P.do("tensor", lambda e, c=c, hb=hb, kd=kd, par=par, pA=pA:
                         e.matmul(
                             pA[:, :],
                             wa[par][:, 512 * kd + 128 * hb: 512 * kd + 128 * hb + 128],
                             xh[:, 4096 * c + 512 * kd: 4096 * c + 512 * kd + 512],
                             start=(kd == 0), stop=(kd == 7)))
                for kd in range(8):
                    P.do("tensor", lambda e, c=c, hb=hb, kd=kd, par=par, pB=pB:
                         e.matmul(
                             pB[:, :],
                             wg[par][:, 512 * kd + 128 * hb: 512 * kd + 128 * hb + 128],
                             xh[:, 4096 * c + 512 * kd: 4096 * c + 512 * kd + 512],
                             start=(kd == 0), stop=(kd == 7)),
                         inc="pe" if kd == 7 else None)
                pe_q = P.cnt["pe"]
                P.w("scalar", "pe", pe_q)
                if prev in evac_dv:
                    P.w("scalar", "dv", evac_dv[prev])
                P.do("scalar", lambda e, pA=pA, sgq=sgq, par=par, hb=hb:
                     e.activation(sgq[:, :], pA[:, :], AF.Silu,
                                  bias=bja[par][:, hb: hb + 1]), inc="ac")
                silu_ac[key] = P.cnt["ac"]
                P.w("vector", "ac", silu_ac[key])
                P.do("vector", lambda e, hb=hb, cc=cc, pB=pB, sgq=sgq, par=par:
                     e.scalar_tensor_tensor(
                         act_s[:, 512 * (2 * hb + cc): 512 * (2 * hb + cc) + 512],
                         pB[:, :], bjg[par][:, hb: hb + 1], sgq[:, :],
                         ALU.add, ALU.mult), inc="dv")
                evac_dv[key] = P.cnt["dv"]
        P.w("tensor", "dv", P.cnt["dv"])
        P.w("tensor", *wdb_done[sidx])
        if sidx == 0:
            P.w("tensor", "ac", prol_ac1)
        if g == 1 and e_ == 0 and j == 3:
            P.w("vector", "st", epi_state["st"])
        for t8 in range(GT):
            cc, itl = t8 // 4, t8 % 4
            i = 8 * g + t8
            pD = (pf[4], pf[5]) if t8 % 2 == 0 else (pf[2], pf[3])
            kprev = (sidx, t8 - 2) if t8 >= 2 else (sidx - 1, t8 + 6)
            if kprev in dn_cnt:
                P.w("tensor", *dn_cnt[kprev])
            last_stop = (j != 3)
            for hb in range(4):
                for nh in range(2):
                    P.do("tensor", lambda e, hb=hb, nh=nh, cc=cc, itl=itl,
                         par2=par2, last_stop=last_stop, pD=pD: e.matmul(
                             pD[nh][:, :],
                             act_s[:, 512 * (2 * hb + cc) + 128 * itl:
                                   512 * (2 * hb + cc) + 128 * itl + 128],
                             wd[par2][:, 1024 * hb + 512 * nh: 1024 * hb + 512 * nh + 512],
                             start=(hb == 0), stop=(last_stop and hb == 3)),
                         inc="pe" if (j != 3 and (hb, nh) == (3, 1)) else None)
            if j == 3:
                for nh in range(2):
                    P.do("tensor", lambda e, nh=nh, par2=par2, pD=pD: e.matmul(
                        pD[nh][:, :], onesb[0:1, 0:128],
                        b2e[par2][0:1, 512 * nh: 512 * nh + 512],
                        start=False, stop=True), inc="pe" if nh == 1 else None)
            pe_dn = P.cnt["pe"]
            if j == 0:
                P.w("scalar", "pe", pe_dn)
                P.w("scalar", "dv", P.cnt["dv"])
                for nh in range(2):
                    P.do("scalar", lambda e, t8=t8, nh=nh, pD=pD: e.copy(
                        dacc[:, 1024 * t8 + 512 * nh: 1024 * t8 + 512 * nh + 512],
                        pD[nh][:, :]), inc="ac" if nh == 1 else None)
                dn_cnt[(sidx, t8)] = ("ac", P.cnt["ac"])
            elif j < 3:
                P.w("vector", "pe", pe_dn)
                for nh in range(2):
                    P.do("vector", lambda e, t8=t8, nh=nh, pD=pD: e.tensor_tensor(
                        dacc[:, 1024 * t8 + 512 * nh: 1024 * t8 + 512 * nh + 512],
                        dacc[:, 1024 * t8 + 512 * nh: 1024 * t8 + 512 * nh + 512],
                        pD[nh][:, :], ALU.add), inc="dv" if nh == 1 else None)
                dn_cnt[(sidx, t8)] = ("dv", P.cnt["dv"])
            else:
                P.w("vector", "pe", pe_dn)
                for nh in range(2):
                    P.do("vector", lambda e, t8=t8, nh=nh, pD=pD: e.tensor_tensor(
                        dacc[:, 1024 * t8 + 512 * nh: 1024 * t8 + 512 * nh + 512],
                        dacc[:, 1024 * t8 + 512 * nh: 1024 * t8 + 512 * nh + 512],
                        pD[nh][:, :], ALU.add), inc="dv" if nh == 1 else None)
                dn_cnt[(sidx, t8)] = ("dv", P.cnt["dv"])
                P.w("scalar", "dv", P.cnt["dv"])
                tm = tmpe if t8 % 2 == 0 else tmpe2
                for nh in range(2):
                    P.do("scalar", lambda e, t8=t8, i=i, nh=nh, e_=e_, tm=tm:
                         e.activation(
                             tm[:, 512 * nh: 512 * nh + 512],
                             dacc[:, 1024 * t8 + 512 * nh: 1024 * t8 + 512 * nh + 512],
                             AF.Identity,
                             scale=cmb[:, E1 * i + e_: E1 * i + e_ + 1]),
                         inc="ac" if nh == 1 else None)
                P.w("vector", "ac", P.cnt["ac"])
                if e_ == 0:
                    for nh in range(2):
                        P.do("vector", lambda e, t8=t8, nh=nh, tm=tm: e.tensor_copy(
                            acc[:, 1024 * t8 + 512 * nh: 1024 * t8 + 512 * nh + 512],
                            tm[:, 512 * nh: 512 * nh + 512]), inc="dv" if nh == 1 else None)
                else:
                    for nh in range(2):
                        P.do("vector", lambda e, t8=t8, nh=nh, tm=tm: e.tensor_tensor(
                            acc[:, 1024 * t8 + 512 * nh: 1024 * t8 + 512 * nh + 512],
                            acc[:, 1024 * t8 + 512 * nh: 1024 * t8 + 512 * nh + 512],
                            tm[:, 512 * nh: 512 * nh + 512], ALU.add),
                            inc="dv" if nh == 1 else None)
        if e_ == E1 - 1 and j == 3:
            if g == 0:
                P.w("sync", "pe", prol_pe1)
                P.w("sync", "dv", prol_dv1)
            else:
                P.w("sync", "dv", epi_state["dv"])
            dl_g = {}
            i0 = 8 * g
            P.do("sync", lambda e, i0=i0: e.dma_start(
                out=xin[0][:, :], in_=x3_i.ap()[128 * i0: 128 * i0 + 128, :]),
                inc="dx0", n=16)
            dl_g[0] = ("dx0", P.cnt["dx0"])
            for t8 in range(GT):
                i = i0 + t8
                if t8 + 1 < GT:
                    P.w("sync", "dv", P.cnt["dv"])
                    sem = "dx%d" % ((t8 + 1) % 2)
                    P.do("sync", lambda e, i=i, t8=t8: e.dma_start(
                        out=xin[(t8 + 1) % 2][:, :],
                        in_=x3_i.ap()[128 * (i + 1): 128 * (i + 1) + 128, :]),
                        inc=sem, n=16)
                    dl_g[t8 + 1] = (sem, P.cnt[sem])
                P.w("vector", *dl_g[t8])
                P.do("vector", lambda e, t8=t8: e.tensor_tensor(
                    acc[:, 1024 * t8: 1024 * t8 + 1024],
                    acc[:, 1024 * t8: 1024 * t8 + 1024],
                    xin[t8 % 2][:, :], ALU.add), inc="dv")
            for half in range(2):
                for q4 in range(4):
                    t8 = 4 * half + q4
                    P.do("vector", lambda e, t8=t8, q4=q4: e.reduce_sum(
                        lst[:, q4: q4 + 1], acc[:, 1024 * t8: 1024 * t8 + 1024],
                        axis=AX.X))
                for q4 in range(4):
                    P.do("vector", lambda e, q4=q4: e.tensor_scalar(
                        lst[:, q4: q4 + 1], lst[:, q4: q4 + 1], 1.0 / 1024, None,
                        ALU.mult), inc="dv" if q4 == 3 else None)
                P.w("scalar", "dv", P.cnt["dv"])
                for q4 in range(4):
                    t8 = 4 * half + q4
                    P.do("scalar", lambda e, t8=t8, q4=q4: e.activation(
                        junk[:, :], acc[:, 1024 * t8: 1024 * t8 + 1024], AF.Square,
                        accum_out=lst[:, 4 + q4: 5 + q4]), inc="ac" if q4 == 3 else None)
                P.w("vector", "ac", P.cnt["ac"])
                for q4 in range(4):
                    P.do("vector", lambda e, q4=q4: e.tensor_tensor(
                        lst[:, 8 + q4: 9 + q4], lst[:, q4: q4 + 1],
                        lst[:, q4: q4 + 1], ALU.mult))
                for q4 in range(4):
                    P.do("vector", lambda e, q4=q4: e.tensor_scalar(
                        lst[:, 12 + q4: 13 + q4], lst[:, 4 + q4: 5 + q4],
                        1.0 / 1024, None, ALU.mult))
                for q4 in range(4):
                    P.do("vector", lambda e, q4=q4: e.tensor_tensor(
                        lst[:, 16 + q4: 17 + q4], lst[:, 12 + q4: 13 + q4],
                        lst[:, 8 + q4: 9 + q4], ALU.subtract))
                for q4 in range(4):
                    P.do("vector", lambda e, q4=q4: e.tensor_scalar(
                        lst[:, 16 + q4: 17 + q4], lst[:, 16 + q4: 17 + q4], EPS,
                        None, ALU.add), inc="dv" if q4 == 3 else None)
                P.w("scalar", "dv", P.cnt["dv"])
                for q4 in range(4):
                    P.do("scalar", lambda e, q4=q4: e.activation(
                        lst[:, 20 + q4: 21 + q4], lst[:, 16 + q4: 17 + q4],
                        AF.Sqrt), inc="ac" if q4 == 3 else None)
                P.w("vector", "ac", P.cnt["ac"])
                for q4 in range(4):
                    P.do("vector", lambda e, q4=q4: e.reciprocal(
                        lst[:, 24 + q4: 25 + q4], lst[:, 20 + q4: 21 + q4]))
                for q4 in range(4):
                    t8 = 4 * half + q4
                    P.do("vector", lambda e, t8=t8, q4=q4: e.tensor_scalar(
                        acc[:, 1024 * t8: 1024 * t8 + 1024],
                        acc[:, 1024 * t8: 1024 * t8 + 1024],
                        lst[:, q4: q4 + 1], None, ALU.subtract))
                for q4 in range(4):
                    t8 = 4 * half + q4
                    P.do("vector", lambda e, t8=t8, q4=q4: e.tensor_scalar(
                        acc[:, 1024 * t8: 1024 * t8 + 1024],
                        acc[:, 1024 * t8: 1024 * t8 + 1024],
                        lst[:, 24 + q4: 25 + q4], None, ALU.mult),
                        inc="dv" if q4 == 3 else None)
            P.wcur("sync", "dv")
            for t8 in range(GT):
                i = i0 + t8
                P.do("sync", lambda e, i=i, t8=t8: e.dma_start(
                    out=y_o.ap()[128 * i: 128 * i + 128, :],
                    in_=acc[:, 1024 * t8: 1024 * t8 + 1024]), inc="st", n=16)
            epi_state["dv"] = P.cnt["dv"]
            epi_state["st"] = P.cnt["st"]
    P.wcur("sync", "st")


def build_launch3b():
    P = Prog()
    nc = P.nc
    P.sem_names("dl", "dg", "pe", "dv", "ac", "po", "st", "bar",
                "dx0", "dx1", "dc0", "dc1", "dw0", "dw1", "dw2", "dwd")
    E = dict(
        x3=nc.dram_tensor("x3", [T, 1024], F32, kind="ExternalInput"),
        wr=nc.dram_tensor("wr", [1024, 8], F32, kind="ExternalInput"),
        br=nc.dram_tensor("br", [1, 8], F32, kind="ExternalInput"),
        w1=nc.dram_tensor("w1", [8, 1024, 4096], F32, kind="ExternalInput"),
        b1=nc.dram_tensor("b1", [8, 4096], F32, kind="ExternalInput"),
        b1c=nc.dram_tensor("b1c", [8, 4096], F32, kind="ExternalInput"),
        w2=nc.dram_tensor("w2", [8, 2048, 1024], F32, kind="ExternalInput"),
        b2=nc.dram_tensor("b2", [8, 1024], F32, kind="ExternalInput"),
        y=nc.dram_tensor("y", [T, 1024], F32, kind="ExternalOutput"))
    pf = [_ps(nc, "pf%d" % i, [128, 512], F32) for i in range(6)]
    pb = [_ps(nc, "pb%d" % i, [128, 1024], BF16) for i in range(2)]
    phase_l1ffn(P, pf, pb, E)
    return P.build(None)


def _phase_scope(P, tag, fn, pf, pb, E):
    global _CUR_STACK, _CUR_PREFIX
    import contextlib
    with contextlib.ExitStack() as stk:
        _CUR_STACK, _CUR_PREFIX = stk, tag + "_"
        fn(P, pf, pb, E)
        _CUR_STACK, _CUR_PREFIX = None, ""
    P.wcur("sync", "st")
    P.barrier()


def build_launchB():
    P = Prog()
    nc = P.nc
    P.sem_names("dl", "dg", "pe", "dv", "ac", "po", "st", "bar",
                "dx0", "dx1", "dc0", "dc1", "dw0", "dw1", "dw2", "dwd")
    E = dict(
        qt=nc.dram_tensor("qt", [T, 512], BF16, kind="ExternalInput"),
        kt=nc.dram_tensor("kt", [T, 512], BF16, kind="ExternalInput"),
        va=nc.dram_tensor("va", [T, 1040], BF16, kind="ExternalInput"),
        ds0p=nc.dram_tensor("ds0p", [3, 128, H * 2 * 65], F32, kind="ExternalInput"),
        x=nc.dram_tensor("x", [T, 1024], F32, kind="ExternalInput"),
        wo=nc.dram_tensor("wo", [1024, 1024], F32, kind="ExternalInput"),
        rw=nc.dram_tensor("rw", [1024, 16], F32, kind="ExternalInput"),
        rb=nc.dram_tensor("rb", [1, 16], F32, kind="ExternalInput"),
        up=nc.dram_tensor("up", [16, 1024, 512], F32, kind="ExternalInput"),
        ga=nc.dram_tensor("ga", [16, 1024, 512], F32, kind="ExternalInput"),
        dw=nc.dram_tensor("dw", [16, 512, 1024], F32, kind="ExternalInput"),
        shg=nc.dram_tensor("shg", [1024, 1024], F32, kind="ExternalInput"),
        shu=nc.dram_tensor("shu", [1024, 1024], F32, kind="ExternalInput"),
        shd=nc.dram_tensor("shd", [1024, 1024], F32, kind="ExternalInput"),
        wq1=nc.dram_tensor("wq1", [1024, 512], F32, kind="ExternalInput"),
        wk1=nc.dram_tensor("wk1", [1024, 512], F32, kind="ExternalInput"),
        wv1=nc.dram_tensor("wv1", [1024, 512], F32, kind="ExternalInput"),
        x1=nc.dram_tensor("x1", [T, 1024], F32),
        x2=nc.dram_tensor("x2", [T, 1024], F32, kind="ExternalOutput"),
        q1f=nc.dram_tensor("q1f", [512, T], BF16, kind="ExternalOutput"),
        k1f=nc.dram_tensor("k1f", [512, T], BF16, kind="ExternalOutput"),
        k1t=nc.dram_tensor("k1t", [T, 512], BF16, kind="ExternalOutput"),
        v1t=nc.dram_tensor("v1t", [T, 512], BF16, kind="ExternalOutput"),
        ds1=nc.dram_tensor("ds1", [512, 512], F32, kind="ExternalOutput"))
    pf = [_ps(nc, "pf%d" % i, [128, 512], F32) for i in range(6)]
    pb = [_ps(nc, "pb%d" % i, [128, 1024], BF16) for i in range(2)]
    _phase_scope(P, "a", phase_l0attn, pf, pb, E)
    _phase_scope(P, "b", phase_l0ffn, pf, pb, E)
    _phase_scope(P, "c", phase_l1qkv, pf, pb, E)
    return P.build(None)


def build_launchC():
    P = Prog()
    nc = P.nc
    P.sem_names("dl", "dg", "pe", "dv", "ac", "po", "st", "bar",
                "dx0", "dx1", "dc0", "dc1", "dw0", "dw1", "dw2", "dwd")
    E = dict(
        x2=nc.dram_tensor("x2", [T, 1024], F32, kind="ExternalInput"),
        q1f=nc.dram_tensor("q1f", [512, T], BF16, kind="ExternalInput"),
        k1f=nc.dram_tensor("k1f", [512, T], BF16, kind="ExternalInput"),
        k1t=nc.dram_tensor("k1t", [T, 512], BF16, kind="ExternalInput"),
        v1t=nc.dram_tensor("v1t", [T, 512], BF16, kind="ExternalInput"),
        ds1p=nc.dram_tensor("ds1p", [3, 512, 512], F32, kind="ExternalInput"),
        wo1=nc.dram_tensor("wo1", [512, 1024], F32, kind="ExternalInput"),
        wr=nc.dram_tensor("wr", [1024, 8], F32, kind="ExternalInput"),
        br=nc.dram_tensor("br", [1, 8], F32, kind="ExternalInput"),
        w1=nc.dram_tensor("w1", [8, 1024, 4096], F32, kind="ExternalInput"),
        b1=nc.dram_tensor("b1", [8, 4096], F32, kind="ExternalInput"),
        b1c=nc.dram_tensor("b1c", [8, 4096], F32, kind="ExternalInput"),
        w2=nc.dram_tensor("w2", [8, 2048, 1024], F32, kind="ExternalInput"),
        b2=nc.dram_tensor("b2", [8, 1024], F32, kind="ExternalInput"),
        x3=nc.dram_tensor("x3", [T, 1024], F32),
        y=nc.dram_tensor("y", [T, 1024], F32, kind="ExternalOutput"))
    pf = [_ps(nc, "pf%d" % i, [128, 512], F32) for i in range(6)]
    pb = [_ps(nc, "pb%d" % i, [128, 1024], BF16) for i in range(2)]
    _phase_scope(P, "a", phase_l1attn, pf, pb, E)
    _phase_scope(P, "b", phase_l1ffn, pf, pb, E)
    return P.build(None)


def _pad_qk_weight(w):
    out = np.zeros((1024, H * HP), dtype=np.float32)
    for h in range(H):
        out[:, HP * h: HP * h + FD] = w[FD * h: FD * h + FD, :].T
    return np.ascontiguousarray(out)


def kernel(**inp):
    x = np.asarray(inp["L0_attn_x"])
    n_cores = 8
    cores = list(range(n_cores))

    def A(v):
        return np.ascontiguousarray(np.asarray(v))

    wq = _pad_qk_weight(np.asarray(inp["L0_attn_Wq"]))
    wk = _pad_qk_weight(np.asarray(inp["L0_attn_Wk"]))
    wv = A(np.asarray(inp["L0_attn_Wv"]).T)
    gam = A(np.asarray(inp["L0_attn_gamma"]).reshape(1, FD))
    bet = A(np.asarray(inp["L0_attn_beta"]).reshape(1, FD))
    in1 = []
    for c in cores:
        b, s = c // 4, c % 4
        in1.append(dict(xT=A(x[b, T * s: T * s + T, :].T),
                        wq=wq, wk=wk, wv=wv, gam=gam, bet=bet))
    r1 = run_bass_kernel_spmd(build_launch1(), in1, core_ids=cores).results

    wo = A(np.asarray(inp["L0_attn_Wo"]).T)
    ds0 = [np.asarray(r1[c]["ds0"]) for c in cores]
    rw = A(np.asarray(inp["L0_ffn_router_w"]).T)
    rb = A(np.asarray(inp["L0_ffn_router_bias"]).reshape(1, 16))
    up = A(inp["L0_ffn_up_proj"])
    ga = A(inp["L0_ffn_gate_proj"])
    dw = A(inp["L0_ffn_down_proj"])
    shg = A(np.asarray(inp["L0_ffn_sh_gate"]).T)
    shu = A(np.asarray(inp["L0_ffn_sh_up"]).T)
    shd = A(np.asarray(inp["L0_ffn_sh_down"]).T)
    wq1 = A(np.asarray(inp["L1_attn_Wq"]).T)
    wk1 = A(np.asarray(inp["L1_attn_Wk"]).T)
    wv1 = A(np.asarray(inp["L1_attn_Wv"]).T)
    inB = []
    for c in cores:
        b, s = c // 4, c % 4
        inB.append(dict(qt=np.asarray(r1[c]["qt"]), kt=np.asarray(r1[c]["kt"]),
                        va=np.asarray(r1[c]["va"]), ds0p=_prefix_stack(ds0, c),
                        x=A(x[b, T * s: T * s + T, :]), wo=wo,
                        rw=rw, rb=rb, up=up, ga=ga, dw=dw, shg=shg, shu=shu,
                        shd=shd, wq1=wq1, wk1=wk1, wv1=wv1))
    rB = run_bass_kernel_spmd(build_launchB(), inB, core_ids=cores).results

    wo1 = A(np.asarray(inp["L1_attn_Wo"]).T)
    ds1 = [np.asarray(rB[c]["ds1"]) for c in cores]
    wr = A(inp["L1_ffn_Wr"])
    br = A(np.asarray(inp["L1_ffn_br"]).reshape(1, 8))
    w1 = A(inp["L1_ffn_W1"])
    b1 = A(inp["L1_ffn_b1"])
    b1c = A(np.asarray(inp["L1_ffn_b1"], np.float32).reshape(8, 2, 4, 4, 128)
            .transpose(0, 2, 1, 4, 3).reshape(8, 4096))
    w2 = A(inp["L1_ffn_W2"])
    b2 = A(inp["L1_ffn_b2"])
    inC = []
    for c in cores:
        inC.append(dict(
            x2=np.asarray(rB[c]["x2"]),
            q1f=np.asarray(rB[c]["q1f"]), k1f=np.asarray(rB[c]["k1f"]),
            k1t=np.asarray(rB[c]["k1t"]), v1t=np.asarray(rB[c]["v1t"]),
            ds1p=_prefix_stack(ds1, c), wo1=wo1, wr=wr, br=br, w1=w1, b1=b1,
            b1c=b1c, w2=w2, b2=b2))
    rC = run_bass_kernel_spmd(build_launchC(), inC, core_ids=cores).results

    out = np.zeros((2, 8192, 1024), dtype=np.float32)
    for c in cores:
        b, s = c // 4, c % 4
        out[b, T * s: T * s + T, :] = np.asarray(rC[c]["y"], np.float32)
    return out


def _dev_launch2c(inp, x2_all):
    nc = build_launch2c()
    wq1 = np.ascontiguousarray(np.asarray(inp["L1_attn_Wq"]).T)
    wk1 = np.ascontiguousarray(np.asarray(inp["L1_attn_Wk"]).T)
    wv1 = np.ascontiguousarray(np.asarray(inp["L1_attn_Wv"]).T)
    in_maps = [dict(x2=np.ascontiguousarray(x2_all[c]), wq1=wq1, wk1=wk1, wv1=wv1)
               for c in range(8)]
    return run_bass_kernel_spmd(nc, in_maps, core_ids=list(range(8))).results


def _dev_launch3a(inp, x2_all, r2c):
    nc = build_launch3a()
    wo1 = np.ascontiguousarray(np.asarray(inp["L1_attn_Wo"]).T)
    ds1 = [np.asarray(r2c[c]["ds1"]) for c in range(8)]
    in_maps = []
    for c in range(8):
        in_maps.append(dict(
            x2=np.ascontiguousarray(x2_all[c]),
            q1f=np.asarray(r2c[c]["q1f"]), k1f=np.asarray(r2c[c]["k1f"]),
            k1t=np.asarray(r2c[c]["k1t"]), v1t=np.asarray(r2c[c]["v1t"]),
            ds1p=_prefix_stack(ds1, c), wo1=wo1))
    return run_bass_kernel_spmd(nc, in_maps, core_ids=list(range(8))).results


def _dev_launch3b(inp, x3_all):
    nc = build_launch3b()
    wr = np.ascontiguousarray(np.asarray(inp["L1_ffn_Wr"]))
    br = np.ascontiguousarray(np.asarray(inp["L1_ffn_br"]).reshape(1, 8))
    w1 = np.ascontiguousarray(np.asarray(inp["L1_ffn_W1"]))
    b1 = np.ascontiguousarray(np.asarray(inp["L1_ffn_b1"]))
    w2 = np.ascontiguousarray(np.asarray(inp["L1_ffn_W2"]))
    b2 = np.ascontiguousarray(np.asarray(inp["L1_ffn_b2"]))
    in_maps = [dict(x3=np.ascontiguousarray(x3_all[c]), wr=wr, br=br,
                    w1=w1, b1=b1, b1c=np.ascontiguousarray(np.asarray(b1, np.float32).reshape(8,2,4,4,128).transpose(0,2,1,4,3).reshape(8,4096)), w2=w2, b2=b2) for c in range(8)]
    return run_bass_kernel_spmd(nc, in_maps, core_ids=list(range(8))).results


def _prefix_stack(arrs, c, nslots=3):
    b, s = c // 4, c % 4
    preds = [arrs[4 * b + j] for j in range(s)]
    shape = arrs[0].shape
    out = np.zeros((nslots,) + shape, dtype=arrs[0].dtype)
    for j, a in enumerate(preds):
        out[j] = a
    return np.ascontiguousarray(out)


def _dev_launch2a(inp, res1):
    x = np.asarray(inp["L0_attn_x"])
    wo = np.ascontiguousarray(np.asarray(inp["L0_attn_Wo"]).T)
    ds0 = [np.asarray(res1[c]["ds0"]) for c in range(8)]
    nc2 = build_launch2a()
    in_maps = []
    for c in range(8):
        b, s = c // 4, c % 4
        in_maps.append(dict(
            qt=np.asarray(res1[c]["qt"]), kt=np.asarray(res1[c]["kt"]),
            va=np.asarray(res1[c]["va"]), ds0p=_prefix_stack(ds0, c),
            x=np.ascontiguousarray(x[b, T * s: T * s + T, :]), wo=wo))
    res = run_bass_kernel_spmd(nc2, in_maps, core_ids=list(range(8)))
    return res.results


def _dev_launch1(inp):
    x = np.asarray(inp["L0_attn_x"])
    wq = _pad_qk_weight(np.asarray(inp["L0_attn_Wq"]))
    wk = _pad_qk_weight(np.asarray(inp["L0_attn_Wk"]))
    wv = np.ascontiguousarray(np.asarray(inp["L0_attn_Wv"]).T)
    gam = np.ascontiguousarray(np.asarray(inp["L0_attn_gamma"]).reshape(1, FD))
    bet = np.ascontiguousarray(np.asarray(inp["L0_attn_beta"]).reshape(1, FD))
    nc1 = build_launch1()
    in_maps = []
    for c in range(8):
        b, s = c // 4, c % 4
        xT = np.ascontiguousarray(x[b, T * s: T * s + T, :].T)
        in_maps.append(dict(xT=xT, wq=wq, wk=wk, wv=wv, gam=gam, bet=bet))
    res = run_bass_kernel_spmd(nc1, in_maps, core_ids=list(range(8)))
    return res.results
